# version 05
# baseline (speedup 1.0000x reference)
import os as _os
NO_INTRA = bool(_os.environ.get("MK_NO_INTRA"))
NO_SKIP = bool(_os.environ.get("MK_NO_SKIP"))
NO_INTER = bool(_os.environ.get("MK_NO_INTER"))
PH = int(_os.environ.get("MK_PH", "99"))
CUT = int(_os.environ.get("MK_CUT", "0"))
"""Fused megakernel for mk_adaln_d1: adaLN block = mamba2-SSD mixer + MoE FFN.

One raw-Bass program per NeuronCore, 8 cores, ONE run_bass_kernel_spmd launch.

Sharding: 8 cores x 2048 contiguous tokens (2 sequences x 4 segments each).
The scan state entering a segment is recomputed on-core by a state-only pass
over the zero-padded prefix of its sequence; a 0/1 validity mask (per-core
data) zeroes the state injections of pad tokens, so the program is
SPMD-uniform.

Layouts: residual feature-major [128 feat x tokens] fp32 (host stages x
transposed; output transposed back on host - layout only). Matmuls bf16
operands / fp32 PSUM accumulation; elementwise fp32 except matmul-operand
level factors (rope cos/sin, decay/q factors) which are bf16.
Scan: chunked SSD (Q=128); RoPE-on-state handled exactly by derotating B/C
with cumulative angles; decay mask exp(min(rowrep(-a sr_i) + col(a sr_j +
ln dt_j), caus)) with caus = 4.0 for i>=j else -1e30 (causality + clamp).
"""
import math
import numpy as np
import ml_dtypes
import concourse.bass as bass
import concourse.mybir as mybir
from concourse.bass_utils import run_bass_kernel_spmd
from contextlib import ExitStack

F32 = mybir.dt.float32
BF16 = mybir.dt.bfloat16
I32 = mybir.dt.int32
AF = mybir.ActivationFunctionType
ALU = mybir.AluOpType

DM, DI, NH = 1024, 2048, 32
LSEG, LPRE, Q = 2048, 6144, 128
NCH, NPC = LSEG // Q, LPRE // Q
NE = 18
GM, NGM = 256, 8
GF, NGF = 512, 4
TWO_PI = 2.0 * math.pi
PI_CL = 3.1415925
NIX = 2208
ENGS = ("sync", "tensor", "vector", "scalar", "gpsimd")
BF = ml_dtypes.bfloat16


class Sem:
    def __init__(self, h, owner=None):
        self.h = h
        self.n = 0
        self.owner = owner
        self.rr = 0

    def inc(self, k=1):
        self.n += k
        return self.n


CHAINED = ("vector", "scalar", "gpsimd")


class Bld:
    def __init__(self, esem=None):
        self.ops = {k: [] for k in ENGS}
        self.waited = {k: {} for k in ENGS}
        self.last_unmarked = {k: None for k in ENGS}
        self.last_mark = {k: None for k in ENGS}
        self.esem = esem or {}

    def wait(self, eng, dep):
        if dep is None:
            return
        sem, val = dep
        if val <= 0:
            return
        if self.waited[eng].get(id(sem), -1) >= val:
            return
        self.waited[eng][id(sem)] = val
        self.ops[eng].append(lambda e, s=sem, v=val: e.wait_ge(s.h, v))

    def emit(self, eng, fn, deps=()):
        if eng in CHAINED:
            sem = self.esem[eng]
            self.wait(eng, (sem, sem.n))
            for d in deps:
                self.wait(eng, d)
            v = sem.inc(1)

            def wrapped(e, p=fn, s=sem):
                r = p(e)
                inst = r[-1] if isinstance(r, tuple) else r
                inst.then_inc(s.h, 1)
                return r
            self.ops[eng].append(wrapped)
            self.last_mark[eng] = (sem, v)
        else:
            for d in deps:
                self.wait(eng, d)
            self.ops[eng].append(fn)
            self.last_unmarked[eng] = len(self.ops[eng]) - 1

    def mark(self, eng, sem):
        if eng in CHAINED:
            return self.last_mark[eng]
        idx = self.last_unmarked[eng]
        if idx is None:
            return self.last_mark[eng]
        v = sem.inc(1)
        prev = self.ops[eng][idx]

        def wrapped(e, p=prev, s=sem):
            r = p(e)
            inst = r[-1] if isinstance(r, tuple) else r
            inst.then_inc(s.h, 1)
            return r
        self.ops[eng][idx] = wrapped
        self.last_unmarked[eng] = None
        self.last_mark[eng] = (sem, v)
        return (sem, v)

    def dma(self, eng, out, in_, sem, deps=(), slow=False):
        if isinstance(sem, list):
            pool = sem
            slot = pool[0].rr % len(pool)
            pool[0].rr += 1
            ssem = pool[slot]
        else:
            ssem = sem
        for d in deps:
            self.wait(eng, d)
        self.wait(eng, (ssem, ssem.n))
        v = ssem.inc(16)
        self.ops[eng].append(
            lambda e, o=out, i=in_, s=ssem, sl=slow: e.dma_start(
                out=o, in_=i, allow_slow_non_contiguous=sl).then_inc(s.h, 16))
        return (ssem, v)


def build_core_program(debug=False):
    nc = bass.Bass("TRN2", target_bir_lowering=False, debug=False)

    def din(name, shape, dtype=F32):
        return nc.dram_tensor(name, shape, dtype, kind="ExternalInput").ap()

    xseg = din("xseg", [DM, LSEG])
    xpre = din("xpre", [DM, LPRE])
    validd = din("validd", [128, NPC])
    win_x = din("win_x", [DM, NIX])
    win_z = din("win_z", [DM, DI])
    wout_d = din("wout", [DI, DM])
    dtb = din("dtb", [NH, 1])
    alog = din("alog", [NH, 1])
    dcols = din("dcols", [128, NH])
    normw_d = din("normw", [128, 16])
    wg_d = din("wg", [DM, 16])
    w13_d = din("w13", [NE, DM, 1024])
    w2e_d = din("w2e", [NE, 512, DM])
    cvec = din("cvec", [128, 8])
    wmod_d = din("wmod", [DM, 6 * DM])
    identb = din("identb", [128, 128], BF16)
    identf = din("identf", [128, 128], F32)
    onesb = din("onesb", [128, 1], BF16)
    ifreqt = din("ifreqt", [128, 512])
    causn = din("causn", [128, 128])

    out_T = nc.dram_tensor("out_T", [DM, LSEG], F32, kind="ExternalOutput").ap()
    taps = {}
    if debug:
        taps["h_dbg"] = nc.dram_tensor("h_dbg", [DM, LSEG], BF16, kind="ExternalOutput").ap()
        taps["yg_dbg"] = nc.dram_tensor("yg_dbg", [DI, LSEG], BF16, kind="ExternalOutput").ap()
        taps["xmid_dbg"] = nc.dram_tensor("xmid_dbg", [DM, LSEG], F32, kind="ExternalOutput").ap()
        taps["l_dbg"] = nc.dram_tensor("l_dbg", [128, 4096], F32, kind="ExternalOutput").ap()
        taps["wt_dbg"] = nc.dram_tensor("wt_dbg", [128, 4096], BF16, kind="ExternalOutput").ap()
        taps["g_dbg"] = nc.dram_tensor("g_dbg", [128, 4096], F32, kind="ExternalOutput").ap()
        taps["bfm_dbg"] = nc.dram_tensor("bfm_dbg", [128, 2048], BF16, kind="ExternalOutput").ap()
        taps["cfm_dbg"] = nc.dram_tensor("cfm_dbg", [128, 2048], BF16, kind="ExternalOutput").ap()
        taps["btm_dbg"] = nc.dram_tensor("btm_dbg", [128, 2048], BF16, kind="ExternalOutput").ap()
        taps["ctm_dbg"] = nc.dram_tensor("ctm_dbg", [128, 2048], BF16, kind="ExternalOutput").ap()
        taps["sin_dbg"] = nc.dram_tensor("sin_dbg", [128, 512], BF16, kind="ExternalOutput").ap()
        taps["cos_dbg"] = nc.dram_tensor("cos_dbg", [128, 512], BF16, kind="ExternalOutput").ap()
        taps["stm_dbg"] = nc.dram_tensor("stm_dbg", [128, 32], F32, kind="ExternalOutput").ap()
        taps["spr_dbg"] = nc.dram_tensor("spr_dbg", [128, 128], BF16, kind="ExternalOutput").ap()
        taps["h2_dbg"] = nc.dram_tensor("h2_dbg", [128, 4096], BF16, kind="ExternalOutput").ap()
        taps["cmb2_dbg"] = nc.dram_tensor("cmb2_dbg", [16, 512], F32, kind="ExternalOutput").ap()
        taps["v_dbg"] = nc.dram_tensor("v_dbg", [128, 2048], BF16, kind="ExternalOutput").ap()
        taps["m1_dbg"] = nc.dram_tensor("m1_dbg", [128, 2048], BF16, kind="ExternalOutput").ap()

    scr_row = nc.dram_tensor("scr_row", [2, NH * Q], F32, kind="Internal").ap()
    scr_dec = nc.dram_tensor("scr_dec", [2, NH], F32, kind="Internal").ap()
    scr_ln = nc.dram_tensor("scr_ln", [2, 1024], F32, kind="Internal").ap()
    scr_r2 = nc.dram_tensor("scr_r2", [2, GM], F32, kind="Internal").ap()
    scr_cmb = nc.dram_tensor("scr_cmb", [1, 16 * GF], F32, kind="Internal").ap()
    scr_mod = nc.dram_tensor("scr_mod", [1, 6 * DM], F32, kind="Internal").ap()

    b = Bld()

    with ExitStack() as es:
        EN = es.enter_context

        def sbt(name, shape, dtype=F32):
            return EN(nc.sbuf_tensor(name, shape, dtype))

        X = sbt("X", [128, 8 * LSEG])
        WAR = sbt("WAR", [128, 34048], BF16)
        WINX = WAR[:, 0:8 * NIX]
        WINZ = WAR[:, 8 * NIX:8 * NIX + 8 * DI]
        EWB = [WAR[:, i * 12288:(i + 1) * 12288] for i in range(2)]

        BFA = sbt("BFA", [128, 23552], BF16)
        BT_TM = BFA[:, 0:2048]
        CB_TM = BFA[:, 2048:4096]
        BT_FM = BFA[:, 4096:6144]
        CB_FM = BFA[:, 6144:8192]
        XSb = BFA[:, 8192:10240]
        XQA = BFA[:, 10240:10752]
        XQB = BFA[:, 10752:11264]
        WTs = BFA[:, 11264:11776]
        SINB = BFA[:, 11776:12288]
        COSB = BFA[:, 12288:12800]
        HG = BFA[:, 12800:14848]
        SILUZ = BFA[:, 14848:18944]
        YGb = BFA[:, 18944:23040]
        SPARE = BFA[:, 23040:23552]
        ZIP = SPARE[:, 384:512]
        WMBUF = [BFA[:, 0:4096], BFA[:, 4096:8192]]
        XPB = SILUZ[:, 0:1024]
        M1T = BFA[:, 0:2048]
        TVT = BFA[:, 2048:4096]
        CBTB = BFA[:, 4096:5120]
        H2G = BFA[:, 5120:9216]
        XBF = BFA[:, 14848:18944]
        WGB = BFA[:, 23040:23168]

        FA = sbt("FA", [128, 4608])
        ROWREP = FA[:, 0:1024]
        LBUF = FA[:, 1024:2048]
        ANG_A = FA[:, 2048:2304]
        ANG_B = FA[:, 2304:2560]
        YSUM = FA[:, 2048:2560]
        HST = FA[:, 2560:3584]
        MRB = FA[:, 1024:2048]
        R2B = FA[:, 3584:3840]
        R1 = FA[0:1, 3840:4352]
        R2 = FA[0:1, 2048:2560]
        R3 = FA[0:1, 4352:4608]
        RMSA = FA[0:1, 4352:4608]
        OUT67 = FA[:, 0:1024]
        OUT3 = FA[:, 2048:2560]
        CBTF = FA[:, 1024:1536]
        FFT = FA[:, 1536:2048]
        GTF = FA[:, 2560:3584]

        KI = sbt("KI", [128, 512], I32)
        IDB = sbt("IDB", [128, 128], BF16)
        IDF = sbt("IDF", [128, 128], F32)
        ONEB = sbt("ONEB", [128, 1], BF16)
        IFRQ = sbt("IFRQ", [128, 512])
        CAUS = sbt("CAUS", [128, 128])
        ADLN = sbt("ADLN", [128, 48])
        ADL1 = sbt("ADL1", [128, 16])
        NRMW = sbt("NRMW", [128, 16])
        DCOL = sbt("DCOL", [128, NH])
        VALID = sbt("VALID", [128, NPC])
        CV = sbt("CV", [128, 8])
        CVB = sbt("CVB", [128, 8], BF16)
        T32 = sbt("T32", [NH, 5 * Q])
        TDEC = sbt("TDEC", [NH, 1])
        TCOL = sbt("TCOL", [NH, 2])
        BASE = sbt("BASE", [NH, 1])
        DTBN = sbt("DTBN", [NH, 1])
        ACOL = sbt("ACOL", [NH, 1])
        EPS = sbt("EPS", [128, 1])
        STM = sbt("STM", [128, 32])
        JLTM = sbt("JLTM", [128, 32])
        QTM = sbt("QTM", [128, 32])
        CSCTM = sbt("CSCTM", [128, 32])
        DECB = sbt("DECB", [128, 16])
        SQW = sbt("SQW", [128, 2048], BF16)

        def t32(i):
            return T32[:, i * Q:(i + 1) * Q]

        PA = EN(nc.psum_tensor("PA", [128, 512], F32))
        PB = EN(nc.psum_tensor("PB", [128, 512], F32))
        PC = EN(nc.psum_tensor("PC", [128, 512], F32))
        PD = EN(nc.psum_tensor("PD", [128, 512], F32))
        PE2 = EN(nc.psum_tensor("PE2", [128, 512], F32))
        PY = EN(nc.psum_tensor("PY", [128, 512], F32))
        PS = EN(nc.psum_tensor("PS", [128, 512], F32))
        PT = EN(nc.psum_tensor("PT", [128, 1024], BF16))

        sQ = [Sem(EN(nc.semaphore(f"sQ{i}"))) for i in range(8)]
        sG = [Sem(EN(nc.semaphore(f"sG{i}"))) for i in range(8)]
        sB = [Sem(EN(nc.semaphore(f"sB{i}"))) for i in range(8)]
        sT = Sem(EN(nc.semaphore("sT")), "tensor")
        sV = Sem(EN(nc.semaphore("sV")), "vector")
        sS = Sem(EN(nc.semaphore("sS")), "scalar")
        sO = Sem(EN(nc.semaphore("sO")))
        sP = Sem(EN(nc.semaphore("sP")), "gpsimd")
        ESEM = {"tensor": sT, "vector": sV, "scalar": sS, "gpsimd": sP}
        b.esem = {"vector": sV, "scalar": sS, "gpsimd": sP}

        def vop(fn, deps=()):
            b.emit("vector", fn, deps)

        def sop(fn, deps=()):
            b.emit("scalar", fn, deps)

        def top(fn, deps=()):
            b.emit("tensor", fn, deps)

        def barrier():
            ms = [b.mark(e, ESEM[e]) for e in ("tensor", "vector", "scalar")]
            dsems = sQ + sG + sB + [sO]
            for e in ("tensor", "vector", "scalar"):
                for m in ms:
                    b.wait(e, m)
                for ds in dsems:
                    b.wait(e, (ds, ds.n))
            return ms

        pf = {k: None for k in ("PA", "PB", "PC", "PD", "PY", "PS", "PT")}
        pe2_free = [None, None]
        hst_ok = [None] * 16
        st = {"rowrep": None, "siluz": None, "hpre": None,
              "ycons": None, "laststt": None, "cbc": None,
              "exp_prev": [None, None], "lnw": None, "r2w": None,
              "rowrep_dma": None, "rowbounce": None, "decw": None,
              "cmbw": None, "tap_h": None, "tap_yg": None, "modw": None,
              "wo": None, "tap_wt": None, "tap_fm": None}

        m_cons = None
        for t, d in ((IDB, identb), (IDF, identf), (ONEB, onesb), (IFRQ, ifreqt),
                     (CAUS, causn), (VALID, validd), (CV, cvec), (DTBN, dtb),
                     (ACOL, alog), (NRMW, normw_d), (DCOL, dcols)):
            m_cons = b.dma("sync", t[:, :], d, sQ[0])
        d_x = None
        for kc in range(8):
            d_x = b.dma("sync", X[:, kc * LSEG:(kc + 1) * LSEG],
                        xseg[kc * 128:(kc + 1) * 128, :], sQ[1])
        m_winx = None
        for kc in range(8):
            m_winx = b.dma("gpsimd", WINX[:, kc * NIX:(kc + 1) * NIX],
                           win_x[kc * 128:(kc + 1) * 128, :], sG[0])
        m_winz = None
        for kc in range(8):
            m_winz = b.dma("gpsimd", WINZ[:, kc * DI:(kc + 1) * DI],
                           win_z[kc * 128:(kc + 1) * 128, :], sG[1])

        vop(lambda e: e.memset(EPS[:, :], 1e-5))
        vop(lambda e: e.memset(BASE[:, :], 0.0))
        vop(lambda e: e.memset(HST[:, :], 0.0), deps=[m_cons])
        vop(lambda e: e.memset(ZIP[:, :], 0.0))
        vop(lambda e: e.tensor_scalar(out=DTBN[:, :], in0=DTBN[:, :], scalar1=-1.0,
                                      scalar2=None, op0=ALU.mult), deps=[m_cons])
        sop(lambda e: e.activation(out=ACOL[:, :], in_=ACOL[:, :], func=AF.Exp,
                                   bias=0.0, scale=1.0), deps=[m_cons])
        sop(lambda e: e.activation(out=CVB[:, :], in_=CV[:, :], func=AF.Silu,
                                   bias=0.0, scale=1.0), deps=[m_cons])
        m_hst0 = b.mark("vector", sV)
        m_silu = b.mark("scalar", sS)
        for i in range(16):
            hst_ok[i] = m_hst0

        m_mm_hist = []
        m_modw = None
        for n in range(12):
            buf = WMBUF[n % 2]
            deps = [m_silu] if n < 2 else [m_mm_hist[n - 2]]
            d_wm = b.dma("gpsimd", buf.rearrange("p (k m) -> p k m", k=8),
                         wmod_d[:, n * 512:(n + 1) * 512]
                         .rearrange("(k p) m -> p k m", p=128), sG, deps=deps)
            for kc in range(8):
                top(lambda e, buf=buf, kc=kc: e.matmul(
                    PS[0:1, 0:512], CVB[:, kc:kc + 1],
                    buf[:, kc * 512:(kc + 1) * 512],
                    start=(kc == 0), stop=(kc == 7)),
                    deps=[d_wm, m_silu, pf["PS"]])
            m_mm = b.mark("tensor", sT)
            m_mm_hist.append(m_mm)
            vop(lambda e: e.tensor_copy(out=R1[:, :], in_=PS[0:1, 0:512]),
                deps=[m_mm, st["modw"]])
            if n in (4, 5, 10, 11):
                for hcol in (slice(0, 256), slice(256, 512)):
                    vop(lambda e, hc=hcol: e.tensor_single_scalar(
                        out=R3[:, 0:256], in_=R1[:, hc], scalar=0.0, op=ALU.is_ge))
                    vop(lambda e: e.tensor_scalar(out=R3[:, 0:256], in0=R3[:, 0:256],
                                                  scalar1=2.0, scalar2=-1.0,
                                                  op0=ALU.mult, op1=ALU.add))
                    vop(lambda e, hc=hcol: e.tensor_mul(out=R1[:, hc], in0=R1[:, hc],
                                                        in1=R3[:, 0:256]))
                    vop(lambda e, hc=hcol: e.tensor_scalar(
                        out=R1[:, hc], in0=R1[:, hc], scalar1=0.25, scalar2=None,
                        op0=ALU.max))
                    vop(lambda e, hc=hcol: e.tensor_mul(out=R1[:, hc], in0=R1[:, hc],
                                                        in1=R3[:, 0:256]))
            m_row = b.mark("vector", sV)
            pf["PS"] = m_row
            m_modw = b.dma("sync", scr_mod[0:1, n * 512:(n + 1) * 512], R1[:, :],
                           sQ[2], deps=[m_row])
            st["modw"] = m_modw
        d_adln = b.dma("sync", ADLN.rearrange("p (v k) -> p v k", v=6),
                       scr_mod.rearrange("o (v k p) -> (o p) v k", v=6, k=8, p=128),
                       sQ, deps=[m_modw], slow=True)
        vop(lambda e: e.tensor_scalar(out=ADL1[:, 0:8], in0=ADLN[:, 8:16], scalar1=1.0,
                                      scalar2=None, op0=ALU.add), deps=[d_adln])
        vop(lambda e: e.tensor_scalar(out=ADL1[:, 8:16], in0=ADLN[:, 32:40], scalar1=1.0,
                                      scalar2=None, op0=ALU.add))
        m_adln = b.mark("vector", sV)

        def sa_col(kc):
            return ADLN[:, 0 + kc:1 + kc]

        def ga_col(kc):
            return ADLN[:, 16 + kc:17 + kc]

        def sb_col(kc):
            return ADLN[:, 24 + kc:25 + kc]

        def gb_col(kc):
            return ADLN[:, 40 + kc:41 + kc]

        def ca1_col(kc):
            return ADL1[:, kc:kc + 1]

        def cb1_col(kc):
            return ADL1[:, 8 + kc:9 + kc]

        barrier()

        def emit_ln(x_tile, ncols, dst, c1col, s0col, parity, castbuf, deps):
            for kc in range(8):
                vop(lambda e, kc=kc: e.tensor_copy(
                    out=castbuf[:, kc * ncols:(kc + 1) * ncols], in_=x_tile(kc)),
                    deps=list(deps))
            m_cast = b.mark("vector", sV)
            for kc in range(8):
                if CUT & 256:
                    break
                top(lambda e, kc=kc: e.matmul(PS[0:1, 0:ncols], ONEB[:, :],
                                              castbuf[:, kc * ncols:(kc + 1) * ncols],
                                              start=(kc == 0), stop=(kc == 7)),
                    deps=([m_cast, pf["PS"], pf["PY"]] if kc == 0 else ()))
            m_s1 = b.mark("tensor", sT)
            sop(lambda e: e.activation(out=castbuf[:, 0:8 * ncols],
                                       in_=castbuf[:, 0:8 * ncols],
                                       func=AF.Square, bias=0.0, scale=1.0), deps=[m_s1])
            m_sq = b.mark("scalar", sS)
            for kc in range(8):
                if CUT & 256:
                    break
                top(lambda e, kc=kc: e.matmul(PY[0:1, 0:ncols], ONEB[:, :],
                                              castbuf[:, kc * ncols:(kc + 1) * ncols],
                                              start=(kc == 0), stop=(kc == 7)),
                    deps=([m_sq] if kc == 0 else ()))
            m_s2 = b.mark("tensor", sT)
            sop(lambda e: e.activation(out=R1[:, 0:ncols], in_=PS[0:1, 0:ncols],
                                       func=AF.Copy, bias=0.0, scale=1.0 / DM),
                deps=[m_s1, st["lnw"], st["modw"]])
            sop(lambda e: e.activation(out=R2[:, 0:ncols], in_=PY[0:1, 0:ncols],
                                       func=AF.Copy, bias=0.0, scale=1.0 / DM),
                deps=[m_s2])
            if ncols <= 256:
                sop(lambda e: e.activation(out=R3[:, 0:ncols], in_=R1[:, 0:ncols],
                                           func=AF.Square, bias=0.0, scale=1.0),
                    deps=[st["r2w"]])
            m_sc = b.mark("scalar", sS)
            pf["PS"] = m_sc
            pf["PY"] = m_sc
            if ncols <= 256:
                vop(lambda e: e.tensor_sub(out=R2[:, 0:ncols], in0=R2[:, 0:ncols],
                                           in1=R3[:, 0:ncols]), deps=[m_sc])
            else:
                vop(lambda e: e.tensor_mul(out=R3[:, 0:256], in0=R1[:, 0:256],
                                           in1=R1[:, 0:256]), deps=[m_sc])
                vop(lambda e: e.tensor_sub(out=R2[:, 0:256], in0=R2[:, 0:256],
                                           in1=R3[:, 0:256]))
                vop(lambda e: e.tensor_mul(out=R3[:, 0:256], in0=R1[:, 256:512],
                                           in1=R1[:, 256:512]))
                vop(lambda e: e.tensor_sub(out=R2[:, 256:512], in0=R2[:, 256:512],
                                           in1=R3[:, 0:256]))
            m_var = b.mark("vector", sV)
            sop(lambda e: e.activation(out=R2[:, 0:ncols], in_=R2[:, 0:ncols],
                                       func=AF.Ln, bias=EPS[0:1, :], scale=1.0),
                deps=[m_var])
            sop(lambda e: e.activation(out=R2[:, 0:ncols], in_=R2[:, 0:ncols],
                                       func=AF.Exp, bias=0.0, scale=-0.5))
            m_r = b.mark("scalar", sS)
            b.dma("sync", scr_ln[parity:parity + 1, 0:ncols], R1[:, 0:ncols], sQ[3],
                  deps=[m_r])
            m_w = b.dma("sync", scr_ln[parity:parity + 1, ncols:2 * ncols],
                        R2[:, 0:ncols], sQ[3])
            st["lnw"] = m_w
            m_rd = b.dma("sync", MRB[:, 0:2 * ncols],
                         scr_ln[parity:parity + 1, 0:2 * ncols]
                         .broadcast_to([128, 2 * ncols]), sQ,
                         deps=[m_w, pf["PA"], pf["PB"], st.get("ffn_free")])
            for kc in range(8):
                vop(lambda e, kc=kc: e.tensor_sub(out=ROWREP[:, 0:ncols],
                                                  in0=x_tile(kc), in1=MRB[:, 0:ncols]),
                    deps=[m_rd, m_adln, st["rowrep_dma"]])
                vop(lambda e, kc=kc: e.tensor_mul(out=ROWREP[:, 0:ncols],
                                                  in0=ROWREP[:, 0:ncols],
                                                  in1=MRB[:, ncols:2 * ncols]))
                vop(lambda e, kc=kc: e.tensor_scalar(out=dst(kc), in0=ROWREP[:, 0:ncols],
                                                     scalar1=c1col(kc), scalar2=s0col(kc),
                                                     op0=ALU.mult, op1=ALU.add))
            mh = b.mark("vector", sV)
            st["rowrep"] = mh
            return mh

        def emit_chunk(h_src, h_dep, cglob, own, gci=0):
            par = cglob % 2
            nix_t = [(PA, "PA", 0, 512), (PB, "PB", 512, 512), (PC, "PC", 1024, 512),
                     (PD, "PD", 1536, 512), (PE2, None, 2048, 160)]
            ip_deps = [h_dep, m_winx, pf["PA"], pf["PB"], pf["PC"], pf["PD"],
                       pe2_free[0], pe2_free[1]]
            for kc in range(8 if not CUT & 1024 else 0):
                for ti, (pt, _nm, c0, w) in enumerate(nix_t):
                    top(lambda e, kc=kc, pt=pt, c0=c0, w=w: e.matmul(
                        pt[:, 0:w], h_src(kc),
                        WINX[:, kc * NIX + c0:kc * NIX + c0 + w],
                        start=(kc == 0), stop=(kc == 7)),
                        deps=(ip_deps if (kc == 0 and ti == 0) else ()))
            m_ip = b.mark("tensor", sT)
            for i, pt in enumerate((PA, PB, PC, PD)):
                vop(lambda e, i=i, pt=pt: e.tensor_copy(
                    out=XSb[:, i * 512:(i + 1) * 512], in_=pt[:, :]), deps=[m_ip])
            vop(lambda e: e.tensor_copy(out=SPARE[:, 0:128], in_=PE2[:, 0:128]))
            vop(lambda e: e.tensor_copy(out=ROWREP[:, 512:544], in_=PE2[:, 128:160]),
                deps=[st["rowrep_dma"]])
            m_ev = b.mark("vector", sV)
            pf["PA"] = pf["PB"] = pf["PC"] = pf["PD"] = m_ev
            pe2_free[0] = pe2_free[1] = m_ev

            top(lambda e: e.transpose(PS[0:32, 0:128], ROWREP[:, 512:544], IDF[:, :]),
                deps=[m_ev, pf["PS"]])
            m_t1 = b.mark("tensor", sT)
            vop(lambda e: e.tensor_copy(out=t32(0), in_=PS[0:32, 0:128]),
                deps=[m_t1, st["rowbounce"]])
            m_v1 = b.mark("vector", sV)
            pf["PS"] = m_v1
            sop(lambda e: e.activation(out=t32(1), in_=t32(0), func=AF.Sigmoid,
                                       bias=DTBN[:, :], scale=-1.0), deps=[m_v1])
            sop(lambda e: e.activation(out=t32(1), in_=t32(1), func=AF.Ln,
                                       bias=0.0, scale=1.0))
            m_s1 = b.mark("scalar", sS)
            vop(lambda e: e.tensor_scalar(out=t32(0), in0=t32(1), scalar1=-1.0,
                                          scalar2=None, op0=ALU.mult), deps=[m_s1])
            m_dt = b.mark("vector", sV)
            sop(lambda e: e.activation(out=t32(2), in_=t32(0), func=AF.Ln,
                                       bias=0.0, scale=1.0), deps=[m_dt])
            m_lndt = b.mark("scalar", sS)
            vop(lambda e: e.tensor_tensor_scan(out=t32(1), data0=t32(0), data1=t32(0),
                                               initial=0.0, op0=ALU.add, op1=ALU.bypass))
            vop(lambda e: e.tensor_copy(out=TCOL[:, 0:1], in_=t32(1)[:, 127:128]))
            vop(lambda e: e.tensor_scalar(out=t32(3), in0=t32(1), scalar1=BASE[:, :],
                                          scalar2=None, op0=ALU.add))
            vop(lambda e: e.tensor_add(out=BASE[:, :], in0=BASE[:, :], in1=TCOL[:, 0:1]))
            vop(lambda e: e.tensor_scalar(out=t32(1), in0=t32(1), scalar1=ACOL[:, :],
                                          scalar2=None, op0=ALU.mult))
            vop(lambda e: e.tensor_scalar(out=TCOL[:, 1:2], in0=TCOL[:, 0:1],
                                          scalar1=ACOL[:, :], scalar2=None,
                                          op0=ALU.mult))
            vop(lambda e: e.scalar_tensor_tensor(out=t32(4), in0=t32(1),
                                                 scalar=TCOL[:, 1:2], in1=t32(2),
                                                 op0=ALU.subtract, op1=ALU.add),
                deps=[m_lndt])
            if own:
                vop(lambda e: e.tensor_add(out=t32(2), in0=t32(1), in1=t32(2)))
            m_v3 = b.mark("vector", sV)
            sop(lambda e: e.activation(out=t32(4), in_=t32(4), func=AF.Exp,
                                       bias=0.0, scale=1.0), deps=[m_v3])
            sop(lambda e: e.activation(out=TDEC[:, :], in_=TCOL[:, 1:2], func=AF.Exp,
                                       bias=0.0, scale=-1.0), deps=[m_v3, st["decw"]])
            m_q = b.mark("scalar", sS)
            vop(lambda e: e.tensor_scalar(out=t32(0), in0=t32(1), scalar1=-1.0,
                                          scalar2=None, op0=ALU.mult))
            m_nasr = b.mark("vector", sV)
            d_r2m = None
            if own:
                b.dma("scalar",
                      scr_row[par:par + 1, 0:2048].rearrange("o (h t) -> (o h) t", h=16),
                      t32(0)[0:16, :], sB[0], deps=[m_nasr])
                d_r2m = b.dma("scalar",
                              scr_row[par:par + 1, 2048:4096].rearrange("o (h t) -> (o h) t", h=16),
                              t32(0)[16:32, :], sB[0])
                st["rowbounce"] = d_r2m
            d_dc = b.dma("scalar", scr_dec[par:par + 1, :].rearrange("o n -> n o"),
                         TDEC[:, :], sB[1], deps=[m_q])
            st["decw"] = d_dc
            dview = scr_dec[par:par + 1, :].rearrange("o (bp two) -> (o two) bp", two=2)
            b.dma("scalar", DECB[0:64, :], dview[0:1, :].broadcast_to([64, 16]),
                  sB[2], deps=[d_dc], slow=True)
            m_decb = b.dma("scalar", DECB[64:128, :],
                           dview[1:2, :].broadcast_to([64, 16]), sB[2], slow=True)
            if own:
                sop(lambda e: e.activation(out=t32(0), in_=t32(1), func=AF.Exp,
                                           bias=0.0, scale=-1.0),
                    deps=[d_r2m, m_nasr])
            m_csc = b.mark("scalar", sS)
            top(lambda e: e.transpose(PS[0:128, 0:32], t32(3), IDF[0:32, 0:32]),
                deps=[m_v3, pf["PS"]])
            top(lambda e: e.transpose(PS[0:128, 64:96], t32(4), IDF[0:32, 0:32]),
                deps=[m_q])
            if own:
                top(lambda e: e.transpose(PS[0:128, 32:64], t32(2), IDF[0:32, 0:32]))
                top(lambda e: e.transpose(PS[0:128, 96:128], t32(0), IDF[0:32, 0:32]),
                    deps=[m_csc])
            m_t2 = b.mark("tensor", sT)
            vop(lambda e: e.tensor_copy(out=STM[:, :], in_=PS[:, 0:32]), deps=[m_t2])
            vop(lambda e: e.tensor_copy(out=QTM[:, :], in_=PS[:, 64:96]))
            if own:
                vop(lambda e: e.tensor_copy(out=JLTM[:, :], in_=PS[:, 32:64]))
                vop(lambda e: e.tensor_copy(out=CSCTM[:, :], in_=PS[:, 96:128]))
            else:
                vop(lambda e, c=cglob: e.tensor_scalar(
                    out=QTM[:, :], in0=QTM[:, :], scalar1=VALID[:, c:c + 1],
                    scalar2=None, op0=ALU.mult))
            m_tm = b.mark("vector", sV)
            pf["PS"] = m_tm

            ANG_A5 = ROWREP[:, 0:512]
            ANG_B5 = ROWREP[:, 512:1024]
            vop(lambda e: e.tensor_mul(
                out=ANG_A5.rearrange("p (h k) -> p h k", h=32),
                in0=STM[:, 0:32].rearrange("p (h o) -> p h o", o=1)
                .broadcast_to([128, 32, 16]),
                in1=IFRQ[:, 0:512].rearrange("p (h k) -> p h k", h=32)),
                deps=[m_tm, st.get("trig")])
            m_trig = None
            for trig, dstT in ((0, SINB), (1, COSB)):
                if trig == 1:
                    vop(lambda e: e.tensor_scalar(out=ANG_A5, in0=ANG_A5,
                                                  scalar1=0.25, scalar2=None,
                                                  op0=ALU.add), deps=[m_trig])
                vop(lambda e: e.tensor_copy(out=KI[:, 0:512], in_=ANG_A5))
                vop(lambda e: e.tensor_copy(out=ANG_B5, in_=KI[:, 0:512]))
                vop(lambda e: e.tensor_sub(out=ANG_B5, in0=ANG_A5, in1=ANG_B5))
                vop(lambda e: e.tensor_scalar(out=ANG_B5, in0=ANG_B5,
                                              scalar1=TWO_PI, scalar2=None,
                                              op0=ALU.mult))
                vop(lambda e: e.tensor_single_scalar(out=MRB[:, 0:512], in_=ANG_B5,
                                                     scalar=PI_CL, op=ALU.is_gt))
                vop(lambda e: e.tensor_single_scalar(out=MRB[:, 512:1024], in_=ANG_B5,
                                                     scalar=-PI_CL, op=ALU.is_lt))
                vop(lambda e: e.tensor_sub(out=MRB[:, 0:512], in0=MRB[:, 0:512],
                                           in1=MRB[:, 512:1024]))
                vop(lambda e: e.scalar_tensor_tensor(out=ANG_B5, in0=MRB[:, 0:512],
                                                     scalar=-TWO_PI, in1=ANG_B5,
                                                     op0=ALU.mult, op1=ALU.add))
                vop(lambda e: e.tensor_scalar(out=ANG_B5, in0=ANG_B5,
                                              scalar1=PI_CL, scalar2=-PI_CL,
                                              op0=ALU.min, op1=ALU.max))
                m_red = b.mark("vector", sV)
                sop(lambda e, dstT=dstT, sc=(-1.0 if trig == 0 else 1.0):
                    e.activation(out=dstT[:, 0:512], in_=ANG_B5,
                                 func=AF.Sin, bias=0.0, scale=sc), deps=[m_red])
                m_trig = b.mark("scalar", sS)
            st["trig"] = m_trig

            Bt = SPARE[:, 0:64]
            Ct = SPARE[:, 64:128]
            bview = BT_TM.rearrange("p (h n) -> p h n", h=32)
            cview = CB_TM.rearrange("p (h n) -> p h n", h=32)
            sview = SINB.rearrange("p (h k) -> p h k", h=32)
            cosv = COSB.rearrange("p (h k) -> p h k", h=32)
            wview = WTs.rearrange("p (h k) -> p h k", h=32)

            def b_bc(c0, c1):
                return (Bt[:, c0:c1].rearrange("p (o n) -> p o n", o=1)
                        .broadcast_to([128, 32, c1 - c0]))

            def c_bc(c0, c1):
                return (Ct[:, c0:c1].rearrange("p (o n) -> p o n", o=1)
                        .broadcast_to([128, 32, c1 - c0]))

            vop(lambda e: e.tensor_mul(out=wview, in0=cosv, in1=b_bc(0, 16)),
                deps=[m_trig, m_ev, st["ycons"], st.get("tap_wt")])
            vop(lambda e: e.tensor_mul(out=bview[:, :, 0:16], in0=sview,
                                       in1=b_bc(16, 32)))
            vop(lambda e: e.tensor_sub(out=bview[:, :, 0:16], in0=wview,
                                       in1=bview[:, :, 0:16]))
            vop(lambda e: e.tensor_mul(out=wview, in0=sview, in1=b_bc(0, 16)))
            vop(lambda e: e.tensor_mul(out=bview[:, :, 16:32], in0=cosv,
                                       in1=b_bc(16, 32)))
            vop(lambda e: e.tensor_add(out=bview[:, :, 16:32], in0=wview,
                                       in1=bview[:, :, 16:32]))
            vop(lambda e: e.tensor_copy(out=bview[:, :, 32:64], in_=b_bc(32, 64)))
            m_brope = b.mark("vector", sV)
            if own:
                vop(lambda e: e.tensor_mul(out=wview, in0=cosv, in1=c_bc(0, 16)))
                vop(lambda e: e.tensor_mul(out=cview[:, :, 0:16], in0=sview,
                                           in1=c_bc(16, 32)))
                vop(lambda e: e.tensor_sub(out=cview[:, :, 0:16], in0=wview,
                                           in1=cview[:, :, 0:16]))
                vop(lambda e: e.tensor_mul(out=wview, in0=sview, in1=c_bc(0, 16)))
                vop(lambda e: e.tensor_mul(out=cview[:, :, 16:32], in0=cosv,
                                           in1=c_bc(16, 32)))
                vop(lambda e: e.tensor_add(out=cview[:, :, 16:32], in0=wview,
                                           in1=cview[:, :, 16:32]))
                vop(lambda e: e.tensor_copy(out=cview[:, :, 32:64], in_=c_bc(32, 64)))
            m_rope = b.mark("vector", sV)

            srcs = ([(BT_TM, BT_FM), (CB_TM, CB_FM)]
                    if (own and not CUT & 512) else ([(BT_TM, BT_FM)] if own else []))
            for (srcT, dstF) in srcs:
                for half in range(2):
                    for j in range(8):
                        jj = half * 8 + j
                        top(lambda e, jj=jj, j=j, srcT=srcT: e.transpose(
                            PT[:, j * 128:(j + 1) * 128],
                            srcT[:, jj * 128:(jj + 1) * 128], IDB[:, :]),
                            deps=[m_rope, pf["PT"]])
                    m_pt = b.mark("tensor", sT)
                    vop(lambda e, half=half, dstF=dstF: e.tensor_copy(
                        out=dstF[:, half * 1024:(half + 1) * 1024],
                        in_=PT[:, 0:1024]), deps=[m_pt])
                    pf["PT"] = b.mark("vector", sV)
            m_fm = b.mark("vector", sV)
            if debug and own and cglob == NPC:
                b.dma("sync", taps["sin_dbg"][:, :], SINB[:, :], sO, deps=[m_trig])
                b.dma("sync", taps["cos_dbg"][:, :], COSB[:, :], sO, deps=[m_trig])
                b.dma("sync", taps["stm_dbg"][:, :], STM[:, :], sO, deps=[m_tm])
                b.dma("sync", taps["spr_dbg"][:, :], SPARE[:, 0:128], sO, deps=[m_ev])
                b.dma("sync", taps["btm_dbg"][:, :], BT_TM[:, :], sO, deps=[m_rope])
                b.dma("sync", taps["ctm_dbg"][:, :], CB_TM[:, :], sO, deps=[m_rope])
                b.dma("sync", taps["bfm_dbg"][:, :], BT_FM[:, :], sO, deps=[m_fm])
                st["tap_fm"] = b.dma("sync", taps["cfm_dbg"][:, :], CB_FM[:, :], sO,
                                     deps=[m_fm])
                for _e in ("vector", "scalar", "tensor"):
                    b.wait(_e, (sO, sO.n))

            for qh in range(4):
                hset = list(range(qh * 8, qh * 8 + 8))
                if own:
                    d_rr = b.dma("scalar", ROWREP[:, 0:1024],
                                 scr_row[par:par + 1, qh * 1024:(qh + 1) * 1024]
                                 .broadcast_to([128, 1024]), sB,
                                 deps=[d_r2m, st["rowrep"], st["trig"]])
                    st["rowrep_dma"] = d_rr
                    for i, h in enumerate(hset):
                        vop(lambda e, i=i, h=h: e.scalar_tensor_tensor(
                            out=LBUF[:, i * 128:(i + 1) * 128],
                            in0=ROWREP[:, i * 128:(i + 1) * 128],
                            scalar=JLTM[:, h:h + 1], in1=CAUS[:, :],
                            op0=ALU.add, op1=ALU.min),
                            deps=[d_rr, m_tm, m_cons])
                    st["rowrep"] = b.mark("vector", sV)
                    sop(lambda e: e.activation(out=LBUF[:, 0:1024], in_=LBUF[:, 0:1024],
                                               func=AF.Exp, bias=0.0, scale=1.0),
                        deps=[st["rowrep"]])
                    m_msk = b.mark("scalar", sS)
                    if debug and cglob == NPC:
                        b.dma("sync", taps["l_dbg"][:, qh * 1024:(qh + 1) * 1024],
                              LBUF[:, 0:1024], sO, deps=[m_msk])
                    for i in (0, 2, 4, 6, None, 1, 3, 5, 7):
                        if CUT & 1:
                            break
                        if i is None:
                            top(lambda e: e.transpose(PT[:, 0:128], IDB[:, :],
                                                      IDB[:, :]), deps=[pf["PT"]])
                            continue
                        h = hset[i]
                        pg = PA if i < 4 else PB
                        ql = h % 2
                        top(lambda e, h=h, pg=pg, i=i, ql=ql: e.matmul(
                            pg[:, (i % 4) * 128:(i % 4) * 128 + 128],
                            BT_FM[ql * 64:ql * 64 + 64,
                                  (h // 2) * 128:(h // 2) * 128 + 128],
                            CB_FM[ql * 64:ql * 64 + 64,
                                  (h // 2) * 128:(h // 2) * 128 + 128],
                            start=True, stop=True),
                            deps=[m_fm, pf["PA" if i < 4 else "PB"]])
                    m_g = b.mark("tensor", sT)
                    vop(lambda e, qh=qh: e.tensor_mul(
                        out=CB_TM[:, qh * 512:(qh + 1) * 512]
                        .rearrange("p (h n) -> p h n", h=8),
                        in0=CB_TM[:, qh * 512:(qh + 1) * 512]
                        .rearrange("p (h n) -> p h n", h=8),
                        in1=CSCTM[:, qh * 8:(qh + 1) * 8]
                        .rearrange("p (h o) -> p h o", o=1)
                        .broadcast_to([128, 8, 64])),
                        deps=[m_g, m_tm])
                    m_cbs = b.mark("vector", sV)
                    for lb in range(4):
                        if CUT & 2:
                            break
                        bp = qh * 4 + lb
                        top(lambda e, bp=bp, lb=lb: e.transpose(
                            PT[:, lb * 128:(lb + 1) * 128],
                            CB_TM[:, bp * 128:bp * 128 + 128], IDB[:, :]),
                            deps=[m_cbs, pf["PT"]])
                    m_cbt = b.mark("tensor", sT)
                    vop(lambda e, qh=qh: e.tensor_copy(
                        out=CB_FM[:, qh * 512:(qh + 1) * 512], in_=PT[:, 0:512]),
                        deps=[m_cbt, st.get("tap_fm")])
                    pf["PT"] = b.mark("vector", sV)
                    m_cbf = pf["PT"]
                    for lb in range(4):
                        bp = qh * 4 + lb
                        vop(lambda e, lb=lb, bp=bp: e.tensor_copy(
                            out=SPARE[:, 128 + lb * 64:128 + lb * 64 + 64],
                            in_=HST[:, bp * 64:bp * 64 + 64]),
                            deps=[hst_ok[bp]])
                    m_hstb = b.mark("vector", sV)
                    vop(lambda e, qh=qh: e.tensor_mul(
                        out=XQA[:, 0:512].rearrange("p (h n) -> p h n", h=8),
                        in0=XSb[:, qh * 512:(qh + 1) * 512]
                        .rearrange("p (h n) -> p h n", h=8),
                        in1=DCOL[:, qh * 8:(qh + 1) * 8]
                        .rearrange("p (h o) -> p h o", o=1)
                        .broadcast_to([128, 8, 64])),
                        deps=[m_cons, m_ev, m_rope])
                    vop(lambda e, qh=qh: e.tensor_mul(
                        out=XQB[:, 0:512].rearrange("p (h n) -> p h n", h=8),
                        in0=XSb[:, qh * 512:(qh + 1) * 512]
                        .rearrange("p (h n) -> p h n", h=8),
                        in1=QTM[:, qh * 8:(qh + 1) * 8]
                        .rearrange("p (h o) -> p h o", o=1)
                        .broadcast_to([128, 8, 64])),
                        deps=[m_tm])
                    m_xq = b.mark("vector", sV)
                    m_ycons = st["ycons"]
                    for h4 in range(2):
                        pg = PA if h4 == 0 else PB
                        vop(lambda e, pg=pg, h4=h4: e.tensor_mul(
                            out=WTs[:, 0:512], in0=pg[:, :],
                            in1=LBUF[:, h4 * 512:h4 * 512 + 512]),
                            deps=[m_g, m_msk, m_ycons, st.get("tap_wt")])
                        m_w = b.mark("vector", sV)
                        if debug and cglob == NPC:
                            st["tap_wt"] = b.dma(
                                "sync", taps["wt_dbg"]
                                [:, qh * 1024 + h4 * 512:qh * 1024 + h4 * 512 + 512],
                                WTs[:, 0:512], sO, deps=[m_w])
                        pf["PA" if h4 == 0 else "PB"] = m_w
                        for i4 in range(4):
                            i = h4 * 4 + i4
                            h = qh * 8 + i
                            bp = h // 2
                            lb = i // 2
                            ql = h % 2
                            yc = lb * 128
                            if not CUT & 4:
                                top(lambda e, h=h, i4=i4, ql=ql, yc=yc: e.matmul(
                                    PY[ql * 64:ql * 64 + 64, yc:yc + 128],
                                    XSb[:, h * 64:(h + 1) * 64],
                                    ZIP[:, 0:128] if NO_INTRA else
                                    WTs[:, i4 * 128:(i4 + 1) * 128],
                                    start=True, stop=False),
                                    deps=[m_w, pf["PY"]])
                                top(lambda e, i=i, ql=ql, yc=yc: e.matmul(
                                    PY[ql * 64:ql * 64 + 64, yc:yc + 128],
                                    XQA[:, i * 64:(i + 1) * 64],
                                    ZIP[:, 0:128] if NO_SKIP else IDB[:, :],
                                    start=False, stop=False),
                                    deps=[m_xq])
                            lbq = (bp % 4)
                            if not CUT & 8:
                                top(lambda e, ql=ql, yc=yc, bp=bp, lbq=lbq: e.matmul(
                                    PY[ql * 64:ql * 64 + 64, yc:yc + 128],
                                    SPARE[ql * 64:ql * 64 + 64,
                                          128 + lbq * 64:128 + lbq * 64 + 64],
                                    ZIP[ql * 64:ql * 64 + 64, 0:128] if NO_INTER
                                    else CB_FM[ql * 64:ql * 64 + 64,
                                               bp * 128:bp * 128 + 128],
                                    start=False, stop=True),
                                    deps=[m_fm, m_hstb, m_cbf])
                        m_ycons = b.mark("tensor", sT)
                        st["ycons"] = m_ycons
                    m_y = m_ycons
                    for i, h in enumerate(hset):
                        if CUT & 16:
                            break
                        lb = i // 2
                        ql = h % 2
                        cc = (qh % 2) * 256 + lb * 64
                        top(lambda e, h=h, i=i, ql=ql, cc=cc: e.matmul(
                            PE2[ql * 64:ql * 64 + 64, cc:cc + 64],
                            BT_TM[:, h * 64:(h + 1) * 64],
                            XQB[:, i * 64:(i + 1) * 64],
                            start=True, stop=True),
                            deps=[m_xq, m_brope, pe2_free[qh % 2], st["laststt"]])
                    m_st = b.mark("tensor", sT)
                    p0 = qh * 4
                    vop(lambda e, p0=p0, gci=gci: e.tensor_mul(
                        out=YGb.rearrange("p (m t) -> p m t", m=16)
                        [:, p0:p0 + 4, gci * 128:gci * 128 + 128],
                        in0=PY[:, 0:512].rearrange("p (m t) -> p m t", m=4),
                        in1=SILUZ.rearrange("p (m t) -> p m t", m=16)
                        [:, p0:p0 + 4, gci * 128:gci * 128 + 128]),
                        deps=[m_y, st["siluz"], st["tap_yg"]])
                    pf["PY"] = b.mark("vector", sV)
                    for lb in range(4):
                        bp = qh * 4 + lb
                        cc = (qh % 2) * 256 + lb * 64
                        vop(lambda e, bp=bp, cc=cc: e.scalar_tensor_tensor(
                            out=HST[:, bp * 64:bp * 64 + 64],
                            in0=HST[:, bp * 64:bp * 64 + 64],
                            scalar=DECB[:, bp:bp + 1],
                            in1=PE2[:, cc:cc + 64],
                            op0=ALU.mult, op1=ALU.add),
                            deps=[m_st, m_decb])
                        hst_ok[bp] = b.mark("vector", sV)
                    pe2_free[qh % 2] = hst_ok[qh * 4 + 3]
                    st["laststt"] = hst_ok[qh * 4 + 3]
                else:
                    vop(lambda e, qh=qh: e.tensor_mul(
                        out=XQB[:, 0:512].rearrange("p (h n) -> p h n", h=8),
                        in0=XSb[:, qh * 512:(qh + 1) * 512]
                        .rearrange("p (h n) -> p h n", h=8),
                        in1=QTM[:, qh * 8:(qh + 1) * 8]
                        .rearrange("p (h o) -> p h o", o=1)
                        .broadcast_to([128, 8, 64])),
                        deps=[m_tm, m_ev, m_brope])
                    m_xq = b.mark("vector", sV)
                    for i, h in enumerate(hset):
                        lb = i // 2
                        ql = h % 2
                        cc = (qh % 2) * 256 + lb * 64
                        top(lambda e, h=h, i=i, ql=ql, cc=cc: e.matmul(
                            PE2[ql * 64:ql * 64 + 64, cc:cc + 64],
                            BT_TM[:, h * 64:(h + 1) * 64],
                            XQB[:, i * 64:(i + 1) * 64],
                            start=True, stop=True),
                            deps=[m_xq, m_brope, pe2_free[qh % 2], st["laststt"]])
                    m_st = b.mark("tensor", sT)
                    for lb in range(4):
                        bp = qh * 4 + lb
                        cc = (qh % 2) * 256 + lb * 64
                        vop(lambda e, bp=bp, cc=cc: e.scalar_tensor_tensor(
                            out=HST[:, bp * 64:bp * 64 + 64],
                            in0=HST[:, bp * 64:bp * 64 + 64],
                            scalar=DECB[:, bp:bp + 1],
                            in1=PE2[:, cc:cc + 64],
                            op0=ALU.mult, op1=ALU.add),
                            deps=[m_st, m_decb])
                        hst_ok[bp] = b.mark("vector", sV)
                    pe2_free[qh % 2] = hst_ok[qh * 4 + 3]
                    st["laststt"] = hst_ok[qh * 4 + 3]

            if own:
                vop(lambda e, gci=gci: e.tensor_copy(
                    out=SQW[:, 0:2048],
                    in_=YGb.rearrange("p (m t) -> p m t", m=16)
                    [:, :, gci * 128:gci * 128 + 128]), deps=[pf["PY"], st["wo"]])
                m_c = b.mark("vector", sV)
                sop(lambda e: e.activation(out=SQW[:, 0:2048], in_=SQW[:, 0:2048],
                                           func=AF.Square, bias=0.0, scale=1.0),
                    deps=[m_c])
                m_sq2 = b.mark("scalar", sS)
                for mc in range(16):
                    if CUT & 32:
                        break
                    top(lambda e, mc=mc: e.matmul(
                        PS[0:1, 0:128], ONEB[:, :],
                        SQW[:, mc * 128:(mc + 1) * 128],
                        start=(mc == 0), stop=(mc == 15)),
                        deps=([m_sq2, pf["PS"]] if mc == 0 else ()))
                m_rmsmm = b.mark("tensor", sT)
                vop(lambda e, gci=gci: e.tensor_copy(
                    out=RMSA[0:1, gci * 128:gci * 128 + 128], in_=PS[0:1, 0:128]),
                    deps=[m_rmsmm, st["r2w"]])
                pf["PS"] = b.mark("vector", sV)
                return pf["PS"]
            return b.mark("vector", sV)

        for c in range(NPC if PH >= 1 else 0):
            d_xp = b.dma("gpsimd", XPB.rearrange("p (k t) -> p k t", k=8),
                         xpre[:, c * Q:(c + 1) * Q]
                         .rearrange("(k p) t -> p k t", p=128), sG,
                         deps=[st["hpre"], m_winz])
            m_h = emit_ln(lambda kc: XPB[:, kc * 128:(kc + 1) * 128], 128,
                          lambda kc: HG[:, kc * 128:(kc + 1) * 128],
                          ca1_col, sa_col, c % 2, SQW, deps=[d_xp])
            st["hpre"] = m_h
            emit_chunk(lambda kc: HG[:, kc * 128:(kc + 1) * 128], m_h, c, own=False)


        for g in range(NGM if PH >= 3 else (1 if PH >= 2 else 0)):
            t0 = g * GM
            m_h = emit_ln(lambda kc, t0=t0: X[:, kc * LSEG + t0:kc * LSEG + t0 + GM], GM,
                          lambda kc: HG[:, kc * GM:(kc + 1) * GM],
                          ca1_col, sa_col, g % 2, SQW,
                          deps=[d_x, st["tap_h"], st["wo"]])
            if debug:
                for kc in range(8):
                    st["tap_h"] = b.dma(
                        "sync", taps["h_dbg"][kc * 128:(kc + 1) * 128, t0:t0 + GM],
                        HG[:, kc * GM:(kc + 1) * GM], sO, deps=[m_h])
            for mz in range(16):
                pz = PE2 if mz % 2 == 0 else PY
                pzf = pe2_free[0] if mz % 2 == 0 else pf["PY"]
                for kc in range(8):
                    if CUT & 64:
                        break
                    top(lambda e, mz=mz, kc=kc, pz=pz: e.matmul(
                        pz[:, 0:GM],
                        WINZ[:, kc * DI + mz * 128:kc * DI + (mz + 1) * 128],
                        HG[:, kc * GM:(kc + 1) * GM],
                        start=(kc == 0), stop=(kc == 7)),
                        deps=[m_h, m_winz, pzf, pe2_free[1] if mz % 2 == 0 else None])
                m_z = b.mark("tensor", sT)
                sop(lambda e, mz=mz, pz=pz: e.activation(
                    out=SILUZ[:, mz * GM:(mz + 1) * GM], in_=pz[:, 0:GM],
                    func=AF.Silu, bias=0.0, scale=1.0), deps=[m_z])
                m_sz = b.mark("scalar", sS)
                if mz % 2 == 0:
                    pe2_free[0] = m_sz
                    pe2_free[1] = m_sz
                else:
                    pf["PY"] = m_sz
            st["siluz"] = b.mark("scalar", sS)
            m_rms = None
            for gci in range(2):
                cglob = NPC + g * 2 + gci
                m_rms = emit_chunk(
                    lambda kc, gci=gci: HG[:, kc * GM + gci * 128:kc * GM + gci * 128 + 128],
                    m_h, cglob, own=True, gci=gci)
            sop(lambda e: e.activation(out=R3[:, 0:GM], in_=RMSA[0:1, 0:GM],
                                       func=AF.Ln, bias=EPS[0:1, :], scale=1.0 / DI),
                deps=[m_rms, st["r2w"]])
            sop(lambda e: e.activation(out=R3[:, 0:GM], in_=R3[:, 0:GM],
                                       func=AF.Exp, bias=0.0, scale=-0.5))
            m_r2 = b.mark("scalar", sS)
            m_wr2 = b.dma("sync", scr_r2[g % 2:g % 2 + 1, 0:GM], R3[:, 0:GM], sQ,
                          deps=[m_r2])
            st["r2w"] = m_wr2
            m_r2b = b.dma("sync", R2B[:, 0:GM],
                          scr_r2[g % 2:g % 2 + 1, 0:GM].broadcast_to([128, GM]), sQ,
                          deps=[m_wr2])
            if debug:
                for mz in range(16):
                    st["tap_yg"] = b.dma(
                        "sync", taps["yg_dbg"][mz * 128:(mz + 1) * 128, t0:t0 + GM],
                        YGb.rearrange("p (m t) -> p m t", m=16)[:, mz, 0:GM],
                        sO, deps=[m_rms])
            wo_marks = []
            for kc in range(16):
                buf = SQW[:, (kc % 2) * 1024:(kc % 2) * 1024 + 1024]
                d_wo = b.dma("gpsimd", buf, wout_d[kc * 128:(kc + 1) * 128, :], sG,
                             deps=[wo_marks[kc - 2] if kc >= 2 else m_rms,
                                   pf["PA"], pf["PB"], pf["PC"], pf["PD"]])
                vop(lambda e, buf=buf, kc=kc: e.tensor_scalar(
                    out=buf, in0=buf, scalar1=NRMW[:, kc:kc + 1], scalar2=None,
                    op0=ALU.mult), deps=[d_wo, m_cons])
                m_nw = b.mark("vector", sV)
                for m8 in range(8):
                    if CUT & 128:
                        break
                    pt = (PA, PB, PC, PD)[m8 % 4]
                    cc = (m8 // 4) * 256
                    stt_ = (kc == 0 and m8 < 4)
                    stp_ = (kc == 15 and m8 >= 4)
                    top(lambda e, buf=buf, m8=m8, pt=pt, cc=cc, kc=kc, stt_=stt_,
                        stp_=stp_: e.matmul(
                        pt[:, cc:cc + GM],
                        buf[:, m8 * 128:(m8 + 1) * 128],
                        YGb[:, kc * GM:(kc + 1) * GM],
                        start=stt_, stop=stp_),
                        deps=[m_nw, pf["PA"], pf["PB"], pf["PC"], pf["PD"]])
                wo_marks.append(b.mark("tensor", sT))
            m_wo = wo_marks[-1]
            st["wo"] = m_wo
            for m8 in range(8):
                pt = (PA, PB, PC, PD)[m8 % 4]
                cc = (m8 // 4) * 256
                vop(lambda e, pt=pt, cc=cc, m8=m8: e.scalar_tensor_tensor(
                    out=FFT[:, 0:256], in0=pt[:, cc:cc + GM],
                    scalar=ga_col(m8), in1=R2B[:, 0:GM],
                    op0=ALU.mult, op1=ALU.mult), deps=[m_wo, m_r2b, m_adln])
                vop(lambda e, m8=m8, t0=t0: e.tensor_add(
                    out=X[:, m8 * LSEG + t0:m8 * LSEG + t0 + GM],
                    in0=X[:, m8 * LSEG + t0:m8 * LSEG + t0 + GM],
                    in1=FFT[:, 0:256]))
            m_xu = b.mark("vector", sV)
            pf["PA"] = pf["PB"] = pf["PC"] = pf["PD"] = m_xu

        m_mixer = barrier()
        if debug:
            for kc in range(8):
                b.dma("sync", taps["xmid_dbg"][kc * 128:(kc + 1) * 128, :],
                      X[:, kc * LSEG:(kc + 1) * LSEG], sO, deps=[m_mixer[1]])
            m_mixer = barrier()

        NGF_RUN = NGF if PH >= 5 else (1 if PH >= 4 else 0)
        d_wg = b.dma("gpsimd", WGB.rearrange("p (k g) -> p k g", k=8),
                     wg_d.rearrange("(k p) g -> p k g", p=128), sG,
                     deps=[m_mixer[0]])
        for g in range(NGF_RUN):
            t0 = g * GF
            m_h2 = emit_ln(lambda kc, t0=t0: X[:, kc * LSEG + t0:kc * LSEG + t0 + GF], GF,
                           lambda kc: H2G[:, kc * GF:(kc + 1) * GF],
                           cb1_col, sb_col, g % 2, XBF, deps=[m_mixer[1]])
            for kc in range(8):
                top(lambda e, kc=kc: e.matmul(
                    PS[0:16, 0:512], WGB[:, kc * 16:(kc + 1) * 16],
                    H2G[:, kc * GF:(kc + 1) * GF],
                    start=(kc == 0), stop=(kc == 7)),
                    deps=[d_wg, m_h2, pf["PS"]])
            m_gs = b.mark("tensor", sT)
            sop(lambda e: e.activation(out=GTF[0:16, 0:512], in_=PS[0:16, 0:512],
                                       func=AF.Sigmoid, bias=0.0, scale=1.0),
                deps=[m_gs, st["cmbw"]])
            m_sig = b.mark("scalar", sS)
            pf["PS"] = m_sig
            for sc in range(4):
                top(lambda e, sc=sc: e.transpose(PS[0:128, sc * 16:(sc + 1) * 16],
                                                 GTF[0:16, sc * 128:(sc + 1) * 128],
                                                 IDF[0:16, 0:16]),
                    deps=[m_sig, pf["PS"]])
            m_sct = b.mark("tensor", sT)
            vop(lambda e: e.tensor_copy(out=GTF[:, 512:576], in_=PS[:, 0:64]),
                deps=[m_sct])
            m_sctm = b.mark("vector", sV)
            pf["PS"] = m_sctm
            for sc in range(4):
                S0 = GTF[:, 512 + sc * 16:512 + (sc + 1) * 16]
                W1 = GTF[:, 640:656]
                W2 = GTF[:, 656:672]
                G2 = GTF[:, 672:676]
                M2c = GTF[:, 676:677]
                T1c = GTF[:, 677:678]
                CMB = GTF[:, 576 + sc * 16:576 + (sc + 1) * 16]
                AM = GTF[:, 678:682]
                pairs = [(i, j) for i in range(4) for j in range(i + 1, 4)]
                for pi, (i, j) in enumerate(pairs):
                    vop(lambda e, i=i, j=j, pi=pi: e.tensor_add(
                        out=GTF[:, 682 + pi * 4:686 + pi * 4],
                        in0=S0.rearrange("p (g j) -> p g j", g=4)[:, :, i],
                        in1=S0.rearrange("p (g j) -> p g j", g=4)[:, :, j]),
                        deps=[m_sctm])
                vop(lambda e: e.tensor_copy(out=G2, in_=GTF[:, 682:686]))
                for pi in range(1, 6):
                    vop(lambda e, pi=pi: e.tensor_tensor(
                        out=G2, in0=G2, in1=GTF[:, 682 + pi * 4:686 + pi * 4],
                        op=ALU.max))
                gp = [(i, j) for i in range(4) for j in range(i + 1, 4)]
                for pi, (i, j) in enumerate(gp):
                    vop(lambda e, i=i, j=j, pi=pi: e.tensor_tensor(
                        out=GTF[:, 682 + pi:683 + pi], in0=G2[:, i:i + 1],
                        in1=G2[:, j:j + 1], op=ALU.min))
                vop(lambda e: e.tensor_copy(out=M2c, in_=GTF[:, 682:683]))
                for pi in range(1, 6):
                    vop(lambda e, pi=pi: e.tensor_tensor(
                        out=M2c, in0=M2c, in1=GTF[:, 682 + pi:683 + pi], op=ALU.max))
                vop(lambda e: e.tensor_scalar(out=AM, in0=G2, scalar1=M2c,
                                              scalar2=None, op0=ALU.is_ge))
                amx = AM.rearrange("p (g o) -> p g o", o=1).broadcast_to([128, 4, 4])
                vop(lambda e: e.tensor_mul(out=W1.rearrange("p (g j) -> p g j", g=4),
                                           in0=S0.rearrange("p (g j) -> p g j", g=4),
                                           in1=amx))
                vop(lambda e: e.tensor_scalar(
                    out=W2.rearrange("p (g j) -> p g j", g=4), in0=amx,
                    scalar1=1e30, scalar2=-1e30, op0=ALU.mult, op1=ALU.add))
                vop(lambda e: e.tensor_add(out=W1, in0=W1, in1=W2))
                for it in range(3):
                    vop(lambda e: e.reduce_max(out=T1c, in_=W1, axis=mybir.AxisListType.X))
                    vop(lambda e: e.tensor_scalar(out=W2, in0=W1, scalar1=T1c,
                                                  scalar2=None, op0=ALU.is_ge))
                    vop(lambda e: e.tensor_scalar(out=W2, in0=W2, scalar1=-1e30,
                                                  scalar2=None, op0=ALU.mult))
                    vop(lambda e: e.tensor_add(out=W1, in0=W1, in1=W2))
                vop(lambda e: e.reduce_max(out=T1c, in_=W1, axis=mybir.AxisListType.X))
                vop(lambda e: e.tensor_mul(out=W1.rearrange("p (g j) -> p g j", g=4),
                                           in0=S0.rearrange("p (g j) -> p g j", g=4),
                                           in1=amx))
                vop(lambda e: e.tensor_scalar(
                    out=W2.rearrange("p (g j) -> p g j", g=4), in0=amx,
                    scalar1=1e30, scalar2=-1e30, op0=ALU.mult, op1=ALU.add))
                vop(lambda e: e.tensor_add(out=W1, in0=W1, in1=W2))
                vop(lambda e: e.tensor_scalar(out=W2, in0=W1, scalar1=T1c,
                                              scalar2=None, op0=ALU.is_ge))
                vop(lambda e: e.tensor_mul(out=W1, in0=S0, in1=W2))
                vop(lambda e: e.reduce_sum(out=T1c, in_=W1, axis=mybir.AxisListType.X))
                vop(lambda e: e.tensor_scalar(out=T1c, in0=T1c, scalar1=1e-6,
                                              scalar2=None, op0=ALU.add))
                vop(lambda e: e.reciprocal(out=T1c, in_=T1c))
                vop(lambda e: e.tensor_scalar(out=CMB, in0=W1, scalar1=T1c,
                                              scalar2=None, op0=ALU.mult))
            m_cmb = b.mark("vector", sV)
            for sc in range(4):
                top(lambda e, sc=sc: e.transpose(PS[0:16, sc * 128:(sc + 1) * 128],
                                                 GTF[:, 576 + sc * 16:576 + (sc + 1) * 16],
                                                 IDF[0:128, 0:128]),
                    deps=[m_cmb, pf["PS"]])
            m_ct = b.mark("tensor", sT)
            vop(lambda e: e.tensor_copy(out=GTF[0:16, 0:512], in_=PS[0:16, 0:512]),
                deps=[m_ct])
            m_cfm = b.mark("vector", sV)
            pf["PS"] = m_cfm
            if debug and g == 0:
                b.dma("sync", taps["cmb2_dbg"][:, :], GTF[0:16, 0:512], sO,
                      deps=[m_cfm])
                for kkc in range(8):
                    b.dma("sync", taps["h2_dbg"][:, kkc * 512:(kkc + 1) * 512],
                          H2G[:, kkc * 512:(kkc + 1) * 512], sO, deps=[m_h2])
                for _e in ("vector", "scalar", "tensor"):
                    b.wait(_e, (sO, sO.n))
            m_cw = b.dma("sync", scr_cmb[0:1, :].rearrange("o (e t) -> (o e) t", e=16),
                         GTF[0:16, 0:512], sQ, deps=[m_cfm])
            st["cmbw"] = m_cw
            exp_marks = []
            out67_mark = st.get("ffn_free")
            for ei in range(NE):
                buf = EWB[ei % 2]
                dd = [exp_marks[ei - 2] if ei >= 2 else
                      (m_mixer[0] if g == 0 else st["exp_prev"][ei % 2])]
                d13 = b.dma("gpsimd", buf[:, 0:8192].rearrange("p (k f) -> p k f", k=8),
                            w13_d[ei].rearrange("(k p) f -> p k f", p=128), sG,
                            deps=dd)
                d2 = b.dma("gpsimd",
                           buf[:, 8192:12288].rearrange("p (k m) -> p k m", k=4),
                           w2e_d[ei].rearrange("(k p) m -> p k m", p=128), sG)
                m_cbc = None
                if ei < 16:
                    d_cb = b.dma("sync", CBTF[:, 0:512],
                                 scr_cmb[0:1, ei * 512:(ei + 1) * 512]
                                 .broadcast_to([128, 512]), sQ,
                                 deps=[m_cw, st["cbc"]])
                    vop(lambda e, ei=ei: e.tensor_copy(
                        out=CBTB[:, (ei % 2) * 512:(ei % 2) * 512 + 512],
                        in_=CBTF[:, 0:512]), deps=[d_cb])
                    m_cbc = b.mark("vector", sV)
                    st["cbc"] = m_cbc
                m_vfc = [None] * 4
                mul_marks = [None] * 4
                silu_marks = [None] * 4
                w2b = ("PA", "PB", "PC", "PD", "PE2")

                def emit_p1(fc):
                    for kc in range(8):
                        top(lambda e, fc=fc, kc=kc, buf=buf: e.matmul(
                            PS[:, 0:512],
                            buf[:, kc * 1024 + fc * 128:
                                kc * 1024 + fc * 128 + 128],
                            H2G[:, kc * GF:(kc + 1) * GF],
                            start=(kc == 0), stop=(kc == 7)),
                            deps=([d13, m_h2, pf["PS"]] if fc == 0 else
                                  ([silu_marks[fc - 1]] if kc == 0 else ())))
                    m_p1u = b.mark("tensor", sT)
                    sop(lambda e, fc=fc: e.activation(
                        out=M1T[:, fc * 512:(fc + 1) * 512], in_=PS[:, 0:512],
                        func=AF.Silu, bias=0.0, scale=1.0), deps=[m_p1u])
                    silu_marks[fc] = b.mark("scalar", sS)

                def emit_p3(fc):
                    for kc in range(8):
                        top(lambda e, fc=fc, kc=kc, buf=buf: e.matmul(
                            PY[:, 0:512],
                            buf[:, kc * 1024 + 512 + fc * 128:
                                kc * 1024 + 512 + fc * 128 + 128],
                            H2G[:, kc * GF:(kc + 1) * GF],
                            start=(kc == 0), stop=(kc == 7)),
                            deps=(([silu_marks[fc], pf["PY"]] if fc == 0 else
                                   [silu_marks[fc], mul_marks[fc - 1]])
                                  if kc == 0 else ()))
                    m_p3u = b.mark("tensor", sT)
                    vop(lambda e, fc=fc: e.tensor_mul(
                        out=TVT[:, fc * 512:(fc + 1) * 512], in0=PY[:, 0:512],
                        in1=M1T[:, fc * 512:(fc + 1) * 512]),
                        deps=[m_p3u, silu_marks[fc]])
                    if ei < 16:
                        vop(lambda e, fc=fc, ei=ei: e.tensor_mul(
                            out=TVT[:, fc * 512:(fc + 1) * 512],
                            in0=TVT[:, fc * 512:(fc + 1) * 512],
                            in1=CBTB[:, (ei % 2) * 512:(ei % 2) * 512 + 512]),
                            deps=[m_cbc])
                    mul_marks[fc] = b.mark("vector", sV)
                    m_vfc[fc] = mul_marks[fc]

                def emit_w2(fc):
                    for m8 in range(5):
                        tgt = (PA, PB, PC, PD, PE2)[m8]
                        stt_ = (ei == 0 and fc == 0)
                        stp_ = (ei == NE - 1 and fc == 3)
                        top(lambda e, m8=m8, fc=fc, buf=buf, tgt=tgt, stt_=stt_,
                            stp_=stp_: e.matmul(
                            tgt[:, 0:512],
                            buf[:, 8192 + fc * 1024 + m8 * 128:
                                8192 + fc * 1024 + m8 * 128 + 128],
                            TVT[:, fc * 512:(fc + 1) * 512],
                            start=stt_, stop=stp_),
                            deps=[d2, m_vfc[fc],
                                  (pf.get(w2b[m8])
                                   if (ei == 0 and fc == 0 and m8 != 4) else None),
                                  (pe2_free[0] if (m8 == 4 and ei == 0 and fc == 0)
                                   else None),
                                  (pe2_free[1] if (m8 == 4 and ei == 0 and fc == 0)
                                   else None)])

                emit_p1(0)
                emit_p3(0)
                emit_p1(1)
                emit_w2(0)
                emit_p3(1)
                emit_p1(2)
                emit_w2(1)
                emit_p3(2)
                emit_p1(3)
                emit_w2(2)
                emit_p3(3)
                emit_w2(3)
                if debug and g == 0 and ei == 0:
                    b.dma("sync", taps["v_dbg"][:, :], TVT[:, :], sO, deps=[m_vfc[3]])
                    b.dma("sync", taps["m1_dbg"][:, :], M1T[:, :], sO, deps=[m_vfc[3]])
                    for _e in ("vector", "scalar", "tensor"):
                        b.wait(_e, (sO, sO.n))
                evac5 = None
                for (m8, tgt, acc, a0) in ((5, PS, OUT3, 0), (6, PY, OUT67, 0),
                                           (7, PS, OUT67, 512)):
                    pdep = (silu_marks[3] if m8 == 5 else
                            (mul_marks[3] if m8 == 6 else evac5))
                    for fc in range(4):
                        top(lambda e, m8=m8, fc=fc, buf=buf, tgt=tgt: e.matmul(
                            tgt[:, 0:512],
                            buf[:, 8192 + fc * 1024 + m8 * 128:
                                8192 + fc * 1024 + m8 * 128 + 128],
                            TVT[:, fc * 512:(fc + 1) * 512],
                            start=(fc == 0), stop=(fc == 3)),
                            deps=([d2, m_vfc[fc]] + ([pdep] if fc == 0 else [])))
                    m_w2 = b.mark("tensor", sT)
                    if ei == 0:
                        vop(lambda e, acc=acc, a0=a0, tgt=tgt: e.tensor_copy(
                            out=acc[:, a0:a0 + 512], in_=tgt[:, 0:512]),
                            deps=[m_w2, out67_mark])
                    else:
                        vop(lambda e, acc=acc, a0=a0, tgt=tgt: e.tensor_add(
                            out=acc[:, a0:a0 + 512], in0=acc[:, a0:a0 + 512],
                            in1=tgt[:, 0:512]), deps=[m_w2])
                    mm = b.mark("vector", sV)
                    if m8 == 5:
                        evac5 = mm
                    elif m8 == 6:
                        pf["PY"] = mm
                    else:
                        pf["PS"] = mm
                exp_marks.append(b.mark("tensor", sT))
            st["exp_prev"] = [exp_marks[16], exp_marks[17]]
            m_exp = exp_marks[-1]
            m_o67 = b.mark("vector", sV)
            for m8 in range(8):
                if m8 < 5:
                    src = (PA, PB, PC, PD, PE2)[m8]
                    vop(lambda e, src=src, m8=m8: e.tensor_scalar(
                        out=FFT[:, 0:512], in0=src[:, 0:512], scalar1=gb_col(m8),
                        scalar2=None, op0=ALU.mult), deps=[m_exp, m_adln])
                else:
                    srcb = OUT3 if m8 == 5 else OUT67
                    a0 = 0 if m8 <= 6 else 512
                    vop(lambda e, srcb=srcb, a0=a0, m8=m8: e.tensor_scalar(
                        out=FFT[:, 0:512],
                        in0=srcb[:, a0:a0 + 512],
                        scalar1=gb_col(m8), scalar2=None, op0=ALU.mult),
                        deps=[m_o67, m_adln])
                vop(lambda e, m8=m8, t0=t0: e.tensor_add(
                    out=X[:, m8 * LSEG + t0:m8 * LSEG + t0 + GF],
                    in0=X[:, m8 * LSEG + t0:m8 * LSEG + t0 + GF],
                    in1=FFT[:, 0:512]))
            ffn_out_free = b.mark("vector", sV)
            st["ffn_free"] = ffn_out_free
            pf["PA"] = pf["PB"] = pf["PC"] = pf["PD"] = pf["PY"] = ffn_out_free
            pe2_free[0] = pe2_free[1] = ffn_out_free

        m_done = barrier()
        for kc in range(8):
            b.dma("sync", out_T[kc * 128:(kc + 1) * 128, :],
                  X[:, kc * LSEG:(kc + 1) * LSEG], sO, deps=[m_done[1]])
        b.emit("sync", lambda e, v=sO.n: e.wait_ge(sO.h, v))

        with nc.Block() as blk:
            blk.sync(lambda e: [f(e) for f in b.ops["sync"]])
            blk.tensor(lambda e: [f(e) for f in b.ops["tensor"]])
            blk.vector(lambda e: [f(e) for f in b.ops["vector"]])
            blk.scalar(lambda e: [f(e) for f in b.ops["scalar"]])
            blk.gpsimd(lambda e: [f(e) for f in b.ops["gpsimd"]])
    return nc


_PROG_CACHE = {}


def _get_prog(debug=False):
    if debug not in _PROG_CACHE:
        _PROG_CACHE[debug] = build_core_program(debug)
    return _PROG_CACHE[debug]


def _stage_inputs(inp):
    f = np.float32
    x = np.asarray(inp["L0_attn_x"], f)
    win = np.asarray(inp["L0_attn_W_in"], f)
    wout = np.ascontiguousarray(np.asarray(inp["L0_attn_W_out"], f))
    dtbias = np.asarray(inp["L0_attn_dt_bias"], f)
    alogv = np.asarray(inp["L0_attn_A_log"], f)
    dskip = np.asarray(inp["L0_attn_D_skip"], f)
    nrm = np.asarray(inp["L0_attn_norm_weight"], f)
    wgate = np.asarray(inp["L0_ffn_w_gate"], f)
    w1e = np.asarray(inp["L0_ffn_w1_e"], f)
    w3e = np.asarray(inp["L0_ffn_w3_e"], f)
    w2e = np.asarray(inp["L0_ffn_w2_e"], f)
    w1s = np.asarray(inp["L0_ffn_w1_s"], f)
    w3s = np.asarray(inp["L0_ffn_w3_s"], f)
    w2s = np.asarray(inp["L0_ffn_w2_s"], f)
    cv = np.asarray(inp["L0_c"], f)
    wmod = np.ascontiguousarray(np.asarray(inp["L0_wmod"], f))

    perm = np.concatenate([np.arange(0, 32, 2), np.arange(1, 32, 2),
                           np.arange(32, 64)])
    win_x = np.ascontiguousarray(np.concatenate(
        [win[:, 2048:4096], win[:, 4096:4160][:, perm],
         win[:, 4160:4224][:, perm], win[:, 4224:4256]], axis=1))
    win_z = np.ascontiguousarray(win[:, 0:2048])

    w13 = np.empty((NE, DM, 1024), f)
    w2a = np.empty((NE, 512, DM), f)
    for e in range(16):
        w13[e, :, 0:512] = w1e[e].T
        w13[e, :, 512:1024] = w3e[e].T
        w2a[e] = w2e[e].T
    for half in range(2):
        sl = slice(half * 512, (half + 1) * 512)
        w13[16 + half, :, 0:512] = w1s[sl, :].T
        w13[16 + half, :, 512:1024] = w3s[sl, :].T
        w2a[16 + half] = w2s[:, sl].T

    invfreq = ((1.0 / 10000.0 ** (np.arange(16, dtype=np.float64) / 16.0))
               / TWO_PI).astype(f)
    ifr = np.broadcast_to(np.tile(invfreq, 32)[None, :], (128, 512)).astype(f).copy()
    caus = np.where(np.arange(128)[None, :] >= np.arange(128)[:, None],
                    f(4.0), f(-1e30)).astype(f)
    common = dict(
        win_x=win_x, win_z=win_z, wout=wout,
        dtb=np.ascontiguousarray(dtbias[:, None]),
        alog=np.ascontiguousarray(alogv[:, None]),
        dcols=np.broadcast_to(dskip[None, :], (128, 32)).copy(),
        normw=np.ascontiguousarray(nrm.reshape(16, 128).T),
        wg=np.ascontiguousarray(wgate.T), w13=w13, w2e=w2a,
        wmod=wmod,
        identb=np.eye(128, dtype=BF), identf=np.eye(128, dtype=f),
        onesb=np.ones((128, 1), BF), ifreqt=ifr, causn=caus,
    )
    in_maps = []
    for core in range(8):
        bi = core // 4
        seg = core % 4
        s0 = seg * LSEG
        xs_T = np.ascontiguousarray(x[bi, s0:s0 + LSEG, :].T)
        npad = LPRE - s0
        xpre_T = np.zeros((DM, LPRE), f)
        if s0 > 0:
            xpre_T[:, npad:] = x[bi, 0:s0, :].T
        val = np.zeros((128, NPC), f)
        val[:, npad // Q:] = 1.0
        m = dict(common)
        m.update(xseg=xs_T, xpre=xpre_T, validd=val,
                 cvec=np.ascontiguousarray(cv[bi].reshape(8, 128).T))
        in_maps.append(m)
    return in_maps


def kernel(**inputs):
    import os
    debug = bool(os.environ.get("MK_DBG"))
    nc = _get_prog(debug)
    in_maps = _stage_inputs(inputs)
    res = run_bass_kernel_spmd(nc, in_maps, core_ids=list(range(8)))
    out = np.empty((2, 8192, DM), np.float32)
    for core in range(8):
        bi, seg = core // 4, core % 4
        out[bi, seg * LSEG:(seg + 1) * LSEG, :] = res.results[core]["out_T"].T
    if debug:
        kernel._dbg = res.results
    return out


# version 06
# speedup vs baseline: 1.0849x; 1.0849x over previous
import os as _os
NO_INTRA = bool(_os.environ.get("MK_NO_INTRA"))
NO_SKIP = bool(_os.environ.get("MK_NO_SKIP"))
NO_INTER = bool(_os.environ.get("MK_NO_INTER"))
PH = int(_os.environ.get("MK_PH", "99"))
CUT = int(_os.environ.get("MK_CUT", "0"))
"""Fused megakernel for mk_adaln_d1: adaLN block = mamba2-SSD mixer + MoE FFN.

One raw-Bass program per NeuronCore, 8 cores, ONE run_bass_kernel_spmd launch.

Sharding: 8 cores x 2048 contiguous tokens (2 sequences x 4 segments each).
The scan state entering a segment is recomputed on-core by a state-only pass
over the zero-padded prefix of its sequence; a 0/1 validity mask (per-core
data) zeroes the state injections of pad tokens, so the program is
SPMD-uniform.

Layouts: residual feature-major [128 feat x tokens] fp32 (host stages x
transposed; output transposed back on host - layout only). Matmuls bf16
operands / fp32 PSUM accumulation; elementwise fp32 except matmul-operand
level factors (rope cos/sin, decay/q factors) which are bf16.
Scan: chunked SSD (Q=128); RoPE-on-state handled exactly by derotating B/C
with cumulative angles; decay mask exp(min(rowrep(-a sr_i) + col(a sr_j +
ln dt_j), caus)) with caus = 4.0 for i>=j else -1e30 (causality + clamp).
"""
import math
import numpy as np
import ml_dtypes
import concourse.bass as bass
import concourse.mybir as mybir
from concourse.bass_utils import run_bass_kernel_spmd
from contextlib import ExitStack

F32 = mybir.dt.float32
BF16 = mybir.dt.bfloat16
I32 = mybir.dt.int32
AF = mybir.ActivationFunctionType
ALU = mybir.AluOpType

DM, DI, NH = 1024, 2048, 32
LSEG, LPRE, Q = 2048, 6144, 128
NCH, NPC = LSEG // Q, LPRE // Q
NE = 18
GM, NGM = 256, 8
GF, NGF = 512, 4
TWO_PI = 2.0 * math.pi
PI_CL = 3.1415925
NIX = 2208
ENGS = ("sync", "tensor", "vector", "scalar", "gpsimd")
BF = ml_dtypes.bfloat16


class Sem:
    def __init__(self, h, owner=None):
        self.h = h
        self.n = 0
        self.owner = owner
        self.rr = 0

    def inc(self, k=1):
        self.n += k
        return self.n


CHAINED = ("vector", "scalar", "gpsimd")


class Bld:
    def __init__(self, esem=None):
        self.ops = {k: [] for k in ENGS}
        self.waited = {k: {} for k in ENGS}
        self.last_unmarked = {k: None for k in ENGS}
        self.last_mark = {k: None for k in ENGS}
        self.esem = esem or {}

    def wait(self, eng, dep):
        if dep is None:
            return
        sem, val = dep
        if val <= 0:
            return
        if self.waited[eng].get(id(sem), -1) >= val:
            return
        self.waited[eng][id(sem)] = val
        self.ops[eng].append(lambda e, s=sem, v=val: e.wait_ge(s.h, v))

    def emit(self, eng, fn, deps=()):
        if eng in CHAINED:
            sem = self.esem[eng]
            self.wait(eng, (sem, sem.n))
            for d in deps:
                self.wait(eng, d)
            v = sem.inc(1)

            def wrapped(e, p=fn, s=sem):
                r = p(e)
                inst = r[-1] if isinstance(r, tuple) else r
                inst.then_inc(s.h, 1)
                return r
            self.ops[eng].append(wrapped)
            self.last_mark[eng] = (sem, v)
        else:
            for d in deps:
                self.wait(eng, d)
            self.ops[eng].append(fn)
            self.last_unmarked[eng] = len(self.ops[eng]) - 1

    def mark(self, eng, sem):
        if eng in CHAINED:
            return self.last_mark[eng]
        idx = self.last_unmarked[eng]
        if idx is None:
            return self.last_mark[eng]
        v = sem.inc(1)
        prev = self.ops[eng][idx]

        def wrapped(e, p=prev, s=sem):
            r = p(e)
            inst = r[-1] if isinstance(r, tuple) else r
            inst.then_inc(s.h, 1)
            return r
        self.ops[eng][idx] = wrapped
        self.last_unmarked[eng] = None
        self.last_mark[eng] = (sem, v)
        return (sem, v)

    def dma(self, eng, out, in_, sem, deps=(), slow=False):
        if isinstance(sem, list):
            pool = sem
            slot = pool[0].rr % len(pool)
            pool[0].rr += 1
            ssem = pool[slot]
        else:
            ssem = sem
        for d in deps:
            self.wait(eng, d)
        self.wait(eng, (ssem, ssem.n))
        v = ssem.inc(16)
        self.ops[eng].append(
            lambda e, o=out, i=in_, s=ssem, sl=slow: e.dma_start(
                out=o, in_=i, allow_slow_non_contiguous=sl).then_inc(s.h, 16))
        return (ssem, v)


def build_core_program(debug=False):
    nc = bass.Bass("TRN2", target_bir_lowering=False, debug=False)

    def din(name, shape, dtype=F32):
        return nc.dram_tensor(name, shape, dtype, kind="ExternalInput").ap()

    xseg = din("xseg", [DM, LSEG])
    xpre = din("xpre", [DM, LPRE])
    validd = din("validd", [128, NPC])
    win_x = din("win_x", [DM, NIX])
    win_z = din("win_z", [DM, DI])
    wout_d = din("wout", [DI, DM])
    dtb = din("dtb", [NH, 1])
    alog = din("alog", [NH, 1])
    dcols = din("dcols", [128, NH])
    normw_d = din("normw", [128, 16])
    wg_d = din("wg", [DM, 16])
    w13_d = din("w13", [NE, DM, 1024])
    w2e_d = din("w2e", [NE, 512, DM])
    cvec = din("cvec", [128, 8])
    wmod_d = din("wmod", [DM, 6 * DM])
    identb = din("identb", [128, 128], BF16)
    identf = din("identf", [128, 128], F32)
    onesb = din("onesb", [128, 1], BF16)
    ifreqt = din("ifreqt", [128, 512])
    causn = din("causn", [128, 128])

    out_T = nc.dram_tensor("out_T", [DM, LSEG], F32, kind="ExternalOutput").ap()
    taps = {}
    if debug:
        taps["h_dbg"] = nc.dram_tensor("h_dbg", [DM, LSEG], BF16, kind="ExternalOutput").ap()
        taps["yg_dbg"] = nc.dram_tensor("yg_dbg", [DI, LSEG], BF16, kind="ExternalOutput").ap()
        taps["xmid_dbg"] = nc.dram_tensor("xmid_dbg", [DM, LSEG], F32, kind="ExternalOutput").ap()
        taps["l_dbg"] = nc.dram_tensor("l_dbg", [128, 4096], F32, kind="ExternalOutput").ap()
        taps["wt_dbg"] = nc.dram_tensor("wt_dbg", [128, 4096], BF16, kind="ExternalOutput").ap()
        taps["g_dbg"] = nc.dram_tensor("g_dbg", [128, 4096], F32, kind="ExternalOutput").ap()
        taps["bfm_dbg"] = nc.dram_tensor("bfm_dbg", [128, 2048], BF16, kind="ExternalOutput").ap()
        taps["cfm_dbg"] = nc.dram_tensor("cfm_dbg", [128, 2048], BF16, kind="ExternalOutput").ap()
        taps["btm_dbg"] = nc.dram_tensor("btm_dbg", [128, 2048], BF16, kind="ExternalOutput").ap()
        taps["ctm_dbg"] = nc.dram_tensor("ctm_dbg", [128, 2048], BF16, kind="ExternalOutput").ap()
        taps["sin_dbg"] = nc.dram_tensor("sin_dbg", [128, 512], BF16, kind="ExternalOutput").ap()
        taps["cos_dbg"] = nc.dram_tensor("cos_dbg", [128, 512], BF16, kind="ExternalOutput").ap()
        taps["stm_dbg"] = nc.dram_tensor("stm_dbg", [128, 32], F32, kind="ExternalOutput").ap()
        taps["spr_dbg"] = nc.dram_tensor("spr_dbg", [128, 128], BF16, kind="ExternalOutput").ap()
        taps["h2_dbg"] = nc.dram_tensor("h2_dbg", [128, 4096], BF16, kind="ExternalOutput").ap()
        taps["cmb2_dbg"] = nc.dram_tensor("cmb2_dbg", [16, 512], F32, kind="ExternalOutput").ap()
        taps["v_dbg"] = nc.dram_tensor("v_dbg", [128, 2048], BF16, kind="ExternalOutput").ap()
        taps["m1_dbg"] = nc.dram_tensor("m1_dbg", [128, 2048], BF16, kind="ExternalOutput").ap()

    scr_row = nc.dram_tensor("scr_row", [2, NH * Q], F32, kind="Internal").ap()
    scr_dec = nc.dram_tensor("scr_dec", [2, NH], F32, kind="Internal").ap()
    scr_ln = nc.dram_tensor("scr_ln", [2, 1024], F32, kind="Internal").ap()
    scr_r2 = nc.dram_tensor("scr_r2", [2, GM], F32, kind="Internal").ap()
    scr_cmb = nc.dram_tensor("scr_cmb", [1, 16 * GF], F32, kind="Internal").ap()
    scr_mod = nc.dram_tensor("scr_mod", [1, 6 * DM], F32, kind="Internal").ap()
    w13b_d = nc.dram_tensor("w13b", [NE, DM, 1024], BF16, kind="Internal").ap()
    w2b_d = nc.dram_tensor("w2b", [NE, 512, DM], BF16, kind="Internal").ap()
    woutb_d = nc.dram_tensor("woutb", [DI, DM], BF16, kind="Internal").ap()

    b = Bld()

    with ExitStack() as es:
        EN = es.enter_context

        def sbt(name, shape, dtype=F32):
            return EN(nc.sbuf_tensor(name, shape, dtype))

        X = sbt("X", [128, 8 * LSEG])
        WAR = sbt("WAR", [128, 34048], BF16)
        WINX = WAR[:, 0:8 * NIX]
        WINZ = WAR[:, 8 * NIX:8 * NIX + 8 * DI]
        EWB = [WAR[:, i * 12288:(i + 1) * 12288] for i in range(2)]

        BFA = sbt("BFA", [128, 23552], BF16)
        BT_TM = BFA[:, 0:2048]
        CB_TM = BFA[:, 2048:4096]
        BT_FM = BFA[:, 4096:6144]
        CB_FM = BFA[:, 6144:8192]
        XSb = BFA[:, 8192:10240]
        XQA = BFA[:, 10240:10752]
        XQB = BFA[:, 10752:11264]
        WTs = BFA[:, 11264:11776]
        SINB = BFA[:, 11776:12288]
        COSB = BFA[:, 12288:12800]
        HG = BFA[:, 12800:14848]
        SILUZ = BFA[:, 14848:18944]
        YGb = BFA[:, 18944:23040]
        SPARE = BFA[:, 23040:23552]
        ZIP = SPARE[:, 384:512]
        WMBUF = [BFA[:, 0:4096], BFA[:, 4096:8192]]
        XPB = SILUZ[:, 0:1024]
        M1T = BFA[:, 0:2048]
        TVT = BFA[:, 2048:4096]
        CBTB = BFA[:, 4096:5120]
        H2G = BFA[:, 5120:9216]
        XBF = BFA[:, 14848:18944]
        WGB = BFA[:, 23040:23168]

        FA = sbt("FA", [128, 4096])
        ROWREP = FA[:, 0:1024]
        LBUF = FA[:, 1024:2048]
        ANG_A = FA[:, 2048:2304]
        ANG_B = FA[:, 2304:2560]
        YSUM = FA[:, 2048:2560]
        HST = FA[:, 2560:3584]
        MRB = FA[:, 1024:2048]
        R2B = FA[:, 3584:3840]
        R1 = FA[0:1, 1024:1536]
        R2 = FA[0:1, 2048:2560]
        R3 = FA[0:1, 3840:4096]
        RMSA = FA[0:1, 3840:4096]
        OUT67 = FA[:, 0:1024]
        OUT3 = FA[:, 2048:2560]
        CBTF = FA[:, 1024:1536]
        FFT = FA[:, 1536:2048]
        GTF = FA[:, 2560:3584]

        KI = sbt("KI", [128, 512], I32)
        IDB = sbt("IDB", [128, 128], BF16)
        IDF = sbt("IDF", [128, 128], F32)
        ONEB = sbt("ONEB", [128, 1], BF16)
        ONESR = sbt("ONESR", [1, 128], F32)
        IFRQ = sbt("IFRQ", [128, 512])
        CAUS = sbt("CAUS", [128, 128])
        ADLN = sbt("ADLN", [128, 48])
        ADL1 = sbt("ADL1", [128, 16])
        NRMW = sbt("NRMW", [128, 16])
        DCOL = sbt("DCOL", [128, NH])
        VALID = sbt("VALID", [128, NPC])
        CV = sbt("CV", [128, 8])
        CVB = sbt("CVB", [128, 8], BF16)
        T32 = sbt("T32", [NH, 5 * Q])
        TDEC = sbt("TDEC", [NH, 1])
        TCOL = sbt("TCOL", [NH, 2])
        BASE = sbt("BASE", [NH, 1])
        DTBN = sbt("DTBN", [NH, 1])
        ACOL = sbt("ACOL", [NH, 1])
        EPS = sbt("EPS", [128, 1])
        STM = sbt("STM", [128, 32])
        JLTM = sbt("JLTM", [128, 32])
        QTM = sbt("QTM", [128, 32])
        CSCTM = sbt("CSCTM", [128, 32])
        DECB = sbt("DECB", [128, 16])
        SQW = sbt("SQW", [128, 2048], BF16)

        def t32(i):
            return T32[:, i * Q:(i + 1) * Q]

        PA = EN(nc.psum_tensor("PA", [128, 512], F32))
        PB = EN(nc.psum_tensor("PB", [128, 512], F32))
        PC = EN(nc.psum_tensor("PC", [128, 512], F32))
        PD = EN(nc.psum_tensor("PD", [128, 512], F32))
        PE2 = EN(nc.psum_tensor("PE2", [128, 512], F32))
        PY = EN(nc.psum_tensor("PY", [128, 512], F32))
        PS = EN(nc.psum_tensor("PS", [128, 512], F32))
        PT = EN(nc.psum_tensor("PT", [128, 1024], BF16))

        sQ = [Sem(EN(nc.semaphore(f"sQ{i}"))) for i in range(8)]
        sG = [Sem(EN(nc.semaphore(f"sG{i}"))) for i in range(8)]
        sB = [Sem(EN(nc.semaphore(f"sB{i}"))) for i in range(8)]
        sT = Sem(EN(nc.semaphore("sT")), "tensor")
        sV = Sem(EN(nc.semaphore("sV")), "vector")
        sS = Sem(EN(nc.semaphore("sS")), "scalar")
        sO = Sem(EN(nc.semaphore("sO")))
        sP = Sem(EN(nc.semaphore("sP")), "gpsimd")
        sC = Sem(EN(nc.semaphore("sC")))
        sCW = Sem(EN(nc.semaphore("sCW")))
        ESEM = {"tensor": sT, "vector": sV, "scalar": sS, "gpsimd": sP}
        b.esem = {"vector": sV, "scalar": sS, "gpsimd": sP}

        def vop(fn, deps=()):
            b.emit("vector", fn, deps)

        def sop(fn, deps=()):
            b.emit("scalar", fn, deps)

        def top(fn, deps=()):
            b.emit("tensor", fn, deps)

        def barrier():
            ms = [b.mark(e, ESEM[e]) for e in ("tensor", "vector", "scalar")]
            dsems = sQ + sG + sB + [sO]
            for e in ("tensor", "vector", "scalar"):
                for m in ms:
                    b.wait(e, m)
                for ds in dsems:
                    b.wait(e, (ds, ds.n))
            return ms

        pf = {k: None for k in ("PA", "PB", "PC", "PD", "PY", "PS", "PT")}
        pe2_free = [None, None]
        hst_ok = [None] * 16
        st = {"rowrep": None, "siluz": None, "hpre": None,
              "ycons": None, "laststt": None, "cbc": None,
              "exp_prev": [None, None], "lnw": None, "r2w": None,
              "rowrep_dma": None, "rowbounce": None, "decw": None,
              "cmbw": None, "tap_h": None, "tap_yg": None, "modw": None,
              "wo": None, "tap_wt": None, "tap_fm": None}

        m_cons = None
        for t, d in ((IDB, identb), (IDF, identf), (ONEB, onesb), (IFRQ, ifreqt),
                     (CAUS, causn), (VALID, validd), (CV, cvec), (DTBN, dtb),
                     (ACOL, alog), (NRMW, normw_d), (DCOL, dcols)):
            m_cons = b.dma("sync", t[:, :], d, sQ[0])
        d_x = None
        for kc in range(8):
            d_x = b.dma("sync", X[:, kc * LSEG:(kc + 1) * LSEG],
                        xseg[kc * 128:(kc + 1) * 128, :], sQ[1])
        m_winx = None
        for kc in range(8):
            m_winx = b.dma("gpsimd", WINX[:, kc * NIX:(kc + 1) * NIX],
                           win_x[kc * 128:(kc + 1) * 128, :], sG[0])
        m_winz = None
        for kc in range(8):
            m_winz = b.dma("gpsimd", WINZ[:, kc * DI:(kc + 1) * DI],
                           win_z[kc * 128:(kc + 1) * 128, :], sG[1])

        vop(lambda e: e.memset(EPS[:, :], 1e-5))
        vop(lambda e: e.memset(ONESR[:, :], 1.0))
        vop(lambda e: e.memset(BASE[:, :], 0.0))
        vop(lambda e: e.memset(HST[:, :], 0.0), deps=[m_cons])
        vop(lambda e: e.memset(ZIP[:, :], 0.0))
        vop(lambda e: e.tensor_scalar(out=DTBN[:, :], in0=DTBN[:, :], scalar1=-1.0,
                                      scalar2=None, op0=ALU.mult), deps=[m_cons])
        sop(lambda e: e.activation(out=ACOL[:, :], in_=ACOL[:, :], func=AF.Exp,
                                   bias=0.0, scale=1.0), deps=[m_cons])
        sop(lambda e: e.activation(out=CVB[:, :], in_=CV[:, :], func=AF.Silu,
                                   bias=0.0, scale=1.0), deps=[m_cons])
        m_hst0 = b.mark("vector", sV)
        m_silu = b.mark("scalar", sS)
        for i in range(16):
            hst_ok[i] = m_hst0

        m_mm_hist = []
        m_modw = None
        for n in range(12):
            buf = WMBUF[n % 2]
            deps = [m_silu] if n < 2 else [m_mm_hist[n - 2]]
            d_wm = b.dma("gpsimd", buf.rearrange("p (k m) -> p k m", k=8),
                         wmod_d[:, n * 512:(n + 1) * 512]
                         .rearrange("(k p) m -> p k m", p=128), sG, deps=deps)
            for kc in range(8):
                top(lambda e, buf=buf, kc=kc: e.matmul(
                    PS[0:1, 0:512], CVB[:, kc:kc + 1],
                    buf[:, kc * 512:(kc + 1) * 512],
                    start=(kc == 0), stop=(kc == 7)),
                    deps=[d_wm, m_silu, pf["PS"]])
            m_mm = b.mark("tensor", sT)
            m_mm_hist.append(m_mm)
            vop(lambda e: e.tensor_copy(out=R1[:, :], in_=PS[0:1, 0:512]),
                deps=[m_mm, st["modw"]])
            if n in (4, 5, 10, 11):
                for hcol in (slice(0, 256), slice(256, 512)):
                    vop(lambda e, hc=hcol: e.tensor_single_scalar(
                        out=R3[:, 0:256], in_=R1[:, hc], scalar=0.0, op=ALU.is_ge))
                    vop(lambda e: e.tensor_scalar(out=R3[:, 0:256], in0=R3[:, 0:256],
                                                  scalar1=2.0, scalar2=-1.0,
                                                  op0=ALU.mult, op1=ALU.add))
                    vop(lambda e, hc=hcol: e.tensor_mul(out=R1[:, hc], in0=R1[:, hc],
                                                        in1=R3[:, 0:256]))
                    vop(lambda e, hc=hcol: e.tensor_scalar(
                        out=R1[:, hc], in0=R1[:, hc], scalar1=0.25, scalar2=None,
                        op0=ALU.max))
                    vop(lambda e, hc=hcol: e.tensor_mul(out=R1[:, hc], in0=R1[:, hc],
                                                        in1=R3[:, 0:256]))
            m_row = b.mark("vector", sV)
            pf["PS"] = m_row
            m_modw = b.dma("sync", scr_mod[0:1, n * 512:(n + 1) * 512], R1[:, :],
                           sQ[2], deps=[m_row])
            st["modw"] = m_modw
        d_adln = b.dma("sync", ADLN.rearrange("p (v k) -> p v k", v=6),
                       scr_mod.rearrange("o (v k p) -> (o p) v k", v=6, k=8, p=128),
                       sQ, deps=[m_modw], slow=True)
        vop(lambda e: e.tensor_scalar(out=ADL1[:, 0:8], in0=ADLN[:, 8:16], scalar1=1.0,
                                      scalar2=None, op0=ALU.add), deps=[d_adln])
        vop(lambda e: e.tensor_scalar(out=ADL1[:, 8:16], in0=ADLN[:, 32:40], scalar1=1.0,
                                      scalar2=None, op0=ALU.add))
        m_adln = b.mark("vector", sV)

        def sa_col(kc):
            return ADLN[:, 0 + kc:1 + kc]

        def ga_col(kc):
            return ADLN[:, 16 + kc:17 + kc]

        def sb_col(kc):
            return ADLN[:, 24 + kc:25 + kc]

        def gb_col(kc):
            return ADLN[:, 40 + kc:41 + kc]

        def ca1_col(kc):
            return ADL1[:, kc:kc + 1]

        def cb1_col(kc):
            return ADL1[:, 8 + kc:9 + kc]

        barrier()

        def emit_ln(x_tile, ncols, dst, c1col, s0col, parity, castbuf, deps,
                    x_wide=None):
            if x_wide is not None:
                vop(lambda e: e.tensor_copy(
                    out=castbuf[:, 0:8 * ncols].rearrange("p (k n) -> p k n", k=8),
                    in_=x_wide), deps=list(deps))
            else:
                for kc in range(8):
                    vop(lambda e, kc=kc: e.tensor_copy(
                        out=castbuf[:, kc * ncols:(kc + 1) * ncols], in_=x_tile(kc)),
                        deps=list(deps))
            m_cast = b.mark("vector", sV)
            for kc in range(8):
                if CUT & 256:
                    break
                top(lambda e, kc=kc: e.matmul(PS[0:1, 0:ncols], ONEB[:, :],
                                              castbuf[:, kc * ncols:(kc + 1) * ncols],
                                              start=(kc == 0), stop=(kc == 7)),
                    deps=([m_cast, pf["PS"], pf["PY"]] if kc == 0 else ()))
            m_s1 = b.mark("tensor", sT)
            sop(lambda e: e.activation(out=castbuf[:, 0:8 * ncols],
                                       in_=castbuf[:, 0:8 * ncols],
                                       func=AF.Square, bias=0.0, scale=1.0), deps=[m_s1])
            m_sq = b.mark("scalar", sS)
            for kc in range(8):
                if CUT & 256:
                    break
                top(lambda e, kc=kc: e.matmul(PY[0:1, 0:ncols], ONEB[:, :],
                                              castbuf[:, kc * ncols:(kc + 1) * ncols],
                                              start=(kc == 0), stop=(kc == 7)),
                    deps=([m_sq] if kc == 0 else ()))
            m_s2 = b.mark("tensor", sT)
            sop(lambda e: e.activation(out=R1[:, 0:ncols], in_=PS[0:1, 0:ncols],
                                       func=AF.Copy, bias=0.0, scale=1.0 / DM),
                deps=[m_s1, st["lnw"], st["modw"]])
            sop(lambda e: e.activation(out=R2[:, 0:ncols], in_=PY[0:1, 0:ncols],
                                       func=AF.Copy, bias=0.0, scale=1.0 / DM),
                deps=[m_s2])
            if ncols <= 256:
                sop(lambda e: e.activation(out=R3[:, 0:ncols], in_=R1[:, 0:ncols],
                                           func=AF.Square, bias=0.0, scale=1.0),
                    deps=[st["r2w"]])
            m_sc = b.mark("scalar", sS)
            pf["PS"] = m_sc
            pf["PY"] = m_sc
            if ncols <= 256:
                vop(lambda e: e.tensor_sub(out=R2[:, 0:ncols], in0=R2[:, 0:ncols],
                                           in1=R3[:, 0:ncols]), deps=[m_sc])
            else:
                vop(lambda e: e.tensor_mul(out=R3[:, 0:256], in0=R1[:, 0:256],
                                           in1=R1[:, 0:256]), deps=[m_sc])
                vop(lambda e: e.tensor_sub(out=R2[:, 0:256], in0=R2[:, 0:256],
                                           in1=R3[:, 0:256]))
                vop(lambda e: e.tensor_mul(out=R3[:, 0:256], in0=R1[:, 256:512],
                                           in1=R1[:, 256:512]))
                vop(lambda e: e.tensor_sub(out=R2[:, 256:512], in0=R2[:, 256:512],
                                           in1=R3[:, 0:256]))
            m_var = b.mark("vector", sV)
            sop(lambda e: e.activation(out=R2[:, 0:ncols], in_=R2[:, 0:ncols],
                                       func=AF.Ln, bias=EPS[0:1, :], scale=1.0),
                deps=[m_var])
            sop(lambda e: e.activation(out=R2[:, 0:ncols], in_=R2[:, 0:ncols],
                                       func=AF.Exp, bias=0.0, scale=-0.5))
            m_r = b.mark("scalar", sS)
            top(lambda e, ncols=ncols: e.matmul(PS[:, 0:ncols], ONESR[0:1, :],
                                                R1[0:1, 0:ncols],
                                                start=True, stop=True), deps=[m_r])
            top(lambda e, ncols=ncols: e.matmul(PY[:, 0:ncols], ONESR[0:1, :],
                                                R2[0:1, 0:ncols],
                                                start=True, stop=True))
            m_bc = b.mark("tensor", sT)
            st["lnw"] = m_bc
            for kc in range(8):
                vop(lambda e, kc=kc: e.tensor_sub(out=ROWREP[:, 0:ncols],
                                                  in0=x_tile(kc), in1=PS[:, 0:ncols]),
                    deps=[m_bc, m_adln, st["rowrep_dma"]])
                vop(lambda e, kc=kc: e.tensor_mul(out=ROWREP[:, 0:ncols],
                                                  in0=ROWREP[:, 0:ncols],
                                                  in1=PY[:, 0:ncols]))
                vop(lambda e, kc=kc: e.tensor_scalar(out=dst(kc), in0=ROWREP[:, 0:ncols],
                                                     scalar1=c1col(kc), scalar2=s0col(kc),
                                                     op0=ALU.mult, op1=ALU.add))
            mh = b.mark("vector", sV)
            st["rowrep"] = mh
            pf["PS"] = mh
            pf["PY"] = mh
            return mh

        def emit_chunk(h_src, h_dep, cglob, own, gci=0):
            par = cglob % 2
            nix_t = [(PA, "PA", 0, 512), (PB, "PB", 512, 512), (PC, "PC", 1024, 512),
                     (PD, "PD", 1536, 512), (PE2, None, 2048, 160)]
            ip_deps = [h_dep, m_winx, pf["PA"], pf["PB"], pf["PC"], pf["PD"],
                       pe2_free[0], pe2_free[1]]
            for kc in range(8 if not CUT & 1024 else 0):
                for ti, (pt, _nm, c0, w) in enumerate(nix_t):
                    top(lambda e, kc=kc, pt=pt, c0=c0, w=w: e.matmul(
                        pt[:, 0:w], h_src(kc),
                        WINX[:, kc * NIX + c0:kc * NIX + c0 + w],
                        start=(kc == 0), stop=(kc == 7)),
                        deps=(ip_deps if (kc == 0 and ti == 0) else ()))
            m_ip = b.mark("tensor", sT)
            for i, pt in enumerate((PA, PB, PC, PD)):
                vop(lambda e, i=i, pt=pt: e.tensor_copy(
                    out=XSb[:, i * 512:(i + 1) * 512], in_=pt[:, :]), deps=[m_ip])
            vop(lambda e: e.tensor_copy(out=SPARE[:, 0:128], in_=PE2[:, 0:128]))
            vop(lambda e: e.tensor_copy(out=ROWREP[:, 512:544], in_=PE2[:, 128:160]),
                deps=[st["rowrep_dma"]])
            m_ev = b.mark("vector", sV)
            pf["PA"] = pf["PB"] = pf["PC"] = pf["PD"] = m_ev
            pe2_free[0] = pe2_free[1] = m_ev

            top(lambda e: e.transpose(PS[0:32, 0:128], ROWREP[:, 512:544], IDF[:, :]),
                deps=[m_ev, pf["PS"]])
            m_t1 = b.mark("tensor", sT)
            vop(lambda e: e.tensor_copy(out=t32(0), in_=PS[0:32, 0:128]),
                deps=[m_t1, st["rowbounce"]])
            m_v1 = b.mark("vector", sV)
            pf["PS"] = m_v1
            sop(lambda e: e.activation(out=t32(1), in_=t32(0), func=AF.Sigmoid,
                                       bias=DTBN[:, :], scale=-1.0), deps=[m_v1])
            sop(lambda e: e.activation(out=t32(1), in_=t32(1), func=AF.Ln,
                                       bias=0.0, scale=1.0))
            m_s1 = b.mark("scalar", sS)
            vop(lambda e: e.tensor_scalar(out=t32(0), in0=t32(1), scalar1=-1.0,
                                          scalar2=None, op0=ALU.mult), deps=[m_s1])
            m_dt = b.mark("vector", sV)
            sop(lambda e: e.activation(out=t32(2), in_=t32(0), func=AF.Ln,
                                       bias=0.0, scale=1.0), deps=[m_dt])
            m_lndt = b.mark("scalar", sS)
            vop(lambda e: e.tensor_tensor_scan(out=t32(1), data0=t32(0), data1=t32(0),
                                               initial=0.0, op0=ALU.add, op1=ALU.bypass))
            vop(lambda e: e.tensor_copy(out=TCOL[:, 0:1], in_=t32(1)[:, 127:128]))
            vop(lambda e: e.tensor_scalar(out=t32(3), in0=t32(1), scalar1=BASE[:, :],
                                          scalar2=None, op0=ALU.add))
            vop(lambda e: e.tensor_add(out=BASE[:, :], in0=BASE[:, :], in1=TCOL[:, 0:1]))
            vop(lambda e: e.tensor_scalar(out=t32(1), in0=t32(1), scalar1=ACOL[:, :],
                                          scalar2=None, op0=ALU.mult))
            vop(lambda e: e.tensor_scalar(out=TCOL[:, 1:2], in0=TCOL[:, 0:1],
                                          scalar1=ACOL[:, :], scalar2=None,
                                          op0=ALU.mult))
            vop(lambda e: e.scalar_tensor_tensor(out=t32(4), in0=t32(1),
                                                 scalar=TCOL[:, 1:2], in1=t32(2),
                                                 op0=ALU.subtract, op1=ALU.add),
                deps=[m_lndt])
            if own:
                vop(lambda e: e.tensor_add(out=t32(2), in0=t32(1), in1=t32(2)))
            m_v3 = b.mark("vector", sV)
            sop(lambda e: e.activation(out=t32(4), in_=t32(4), func=AF.Exp,
                                       bias=0.0, scale=1.0), deps=[m_v3])
            sop(lambda e: e.activation(out=TDEC[:, :], in_=TCOL[:, 1:2], func=AF.Exp,
                                       bias=0.0, scale=-1.0), deps=[m_v3, st["decw"]])
            m_q = b.mark("scalar", sS)
            vop(lambda e: e.tensor_scalar(out=t32(0), in0=t32(1), scalar1=-1.0,
                                          scalar2=None, op0=ALU.mult))
            m_nasr = b.mark("vector", sV)
            d_r2m = None
            if own:
                b.dma("scalar",
                      scr_row[par:par + 1, 0:2048].rearrange("o (h t) -> (o h) t", h=16),
                      t32(0)[0:16, :], sB[0], deps=[m_nasr])
                d_r2m = b.dma("scalar",
                              scr_row[par:par + 1, 2048:4096].rearrange("o (h t) -> (o h) t", h=16),
                              t32(0)[16:32, :], sB[0])
                st["rowbounce"] = d_r2m
            d_dc = b.dma("scalar", scr_dec[par:par + 1, :].rearrange("o n -> n o"),
                         TDEC[:, :], sB[1], deps=[m_q])
            st["decw"] = d_dc
            dview = scr_dec[par:par + 1, :].rearrange("o (bp two) -> (o two) bp", two=2)
            b.dma("scalar", DECB[0:64, :], dview[0:1, :].broadcast_to([64, 16]),
                  sB[2], deps=[d_dc], slow=True)
            m_decb = b.dma("scalar", DECB[64:128, :],
                           dview[1:2, :].broadcast_to([64, 16]), sB[2], slow=True)
            if own:
                sop(lambda e: e.activation(out=t32(0), in_=t32(1), func=AF.Exp,
                                           bias=0.0, scale=-1.0),
                    deps=[d_r2m, m_nasr])
            m_csc = b.mark("scalar", sS)
            top(lambda e: e.transpose(PS[0:128, 0:32], t32(3), IDF[0:32, 0:32]),
                deps=[m_v3, pf["PS"]])
            top(lambda e: e.transpose(PS[0:128, 64:96], t32(4), IDF[0:32, 0:32]),
                deps=[m_q])
            if own:
                top(lambda e: e.transpose(PS[0:128, 32:64], t32(2), IDF[0:32, 0:32]))
                top(lambda e: e.transpose(PS[0:128, 96:128], t32(0), IDF[0:32, 0:32]),
                    deps=[m_csc])
            m_t2 = b.mark("tensor", sT)
            vop(lambda e: e.tensor_copy(out=STM[:, :], in_=PS[:, 0:32]), deps=[m_t2])
            vop(lambda e: e.tensor_copy(out=QTM[:, :], in_=PS[:, 64:96]))
            if own:
                vop(lambda e: e.tensor_copy(out=JLTM[:, :], in_=PS[:, 32:64]))
                vop(lambda e: e.tensor_copy(out=CSCTM[:, :], in_=PS[:, 96:128]))
            else:
                vop(lambda e, c=cglob: e.tensor_scalar(
                    out=QTM[:, :], in0=QTM[:, :], scalar1=VALID[:, c:c + 1],
                    scalar2=None, op0=ALU.mult))
            m_tm = b.mark("vector", sV)
            pf["PS"] = m_tm

            ANG_A5 = ROWREP[:, 0:512]
            ANG_B5 = ROWREP[:, 512:1024]
            vop(lambda e: e.tensor_mul(
                out=ANG_A5.rearrange("p (h k) -> p h k", h=32),
                in0=STM[:, 0:32].rearrange("p (h o) -> p h o", o=1)
                .broadcast_to([128, 32, 16]),
                in1=IFRQ[:, 0:512].rearrange("p (h k) -> p h k", h=32)),
                deps=[m_tm, st.get("trig")])
            m_trig = None
            for trig, dstT in ((0, SINB), (1, COSB)):
                if trig == 1:
                    vop(lambda e: e.tensor_scalar(out=ANG_A5, in0=ANG_A5,
                                                  scalar1=0.25, scalar2=None,
                                                  op0=ALU.add), deps=[m_trig])
                vop(lambda e: e.tensor_copy(out=KI[:, 0:512], in_=ANG_A5))
                vop(lambda e: e.tensor_copy(out=ANG_B5, in_=KI[:, 0:512]))
                vop(lambda e: e.tensor_sub(out=ANG_B5, in0=ANG_A5, in1=ANG_B5))
                vop(lambda e: e.tensor_scalar(out=ANG_B5, in0=ANG_B5,
                                              scalar1=TWO_PI, scalar2=None,
                                              op0=ALU.mult))
                vop(lambda e: e.tensor_single_scalar(out=MRB[:, 0:512], in_=ANG_B5,
                                                     scalar=PI_CL, op=ALU.is_gt))
                vop(lambda e: e.tensor_single_scalar(out=MRB[:, 512:1024], in_=ANG_B5,
                                                     scalar=-PI_CL, op=ALU.is_lt))
                vop(lambda e: e.tensor_sub(out=MRB[:, 0:512], in0=MRB[:, 0:512],
                                           in1=MRB[:, 512:1024]))
                vop(lambda e: e.scalar_tensor_tensor(out=ANG_B5, in0=MRB[:, 0:512],
                                                     scalar=-TWO_PI, in1=ANG_B5,
                                                     op0=ALU.mult, op1=ALU.add))
                vop(lambda e: e.tensor_scalar(out=ANG_B5, in0=ANG_B5,
                                              scalar1=PI_CL, scalar2=-PI_CL,
                                              op0=ALU.min, op1=ALU.max))
                m_red = b.mark("vector", sV)
                sop(lambda e, dstT=dstT, sc=(-1.0 if trig == 0 else 1.0):
                    e.activation(out=dstT[:, 0:512], in_=ANG_B5,
                                 func=AF.Sin, bias=0.0, scale=sc), deps=[m_red])
                m_trig = b.mark("scalar", sS)
            st["trig"] = m_trig

            Bt = SPARE[:, 0:64]
            Ct = SPARE[:, 64:128]
            bview = BT_TM.rearrange("p (h n) -> p h n", h=32)
            cview = CB_TM.rearrange("p (h n) -> p h n", h=32)
            sview = SINB.rearrange("p (h k) -> p h k", h=32)
            cosv = COSB.rearrange("p (h k) -> p h k", h=32)
            wview = WTs.rearrange("p (h k) -> p h k", h=32)

            def b_bc(c0, c1):
                return (Bt[:, c0:c1].rearrange("p (o n) -> p o n", o=1)
                        .broadcast_to([128, 32, c1 - c0]))

            def c_bc(c0, c1):
                return (Ct[:, c0:c1].rearrange("p (o n) -> p o n", o=1)
                        .broadcast_to([128, 32, c1 - c0]))

            vop(lambda e: e.tensor_mul(out=wview, in0=cosv, in1=b_bc(0, 16)),
                deps=[m_trig, m_ev, st["ycons"], st.get("tap_wt")])
            vop(lambda e: e.tensor_mul(out=bview[:, :, 0:16], in0=sview,
                                       in1=b_bc(16, 32)))
            vop(lambda e: e.tensor_sub(out=bview[:, :, 0:16], in0=wview,
                                       in1=bview[:, :, 0:16]))
            vop(lambda e: e.tensor_mul(out=wview, in0=sview, in1=b_bc(0, 16)))
            vop(lambda e: e.tensor_mul(out=bview[:, :, 16:32], in0=cosv,
                                       in1=b_bc(16, 32)))
            vop(lambda e: e.tensor_add(out=bview[:, :, 16:32], in0=wview,
                                       in1=bview[:, :, 16:32]))
            vop(lambda e: e.tensor_copy(out=bview[:, :, 32:64], in_=b_bc(32, 64)))
            m_brope = b.mark("vector", sV)
            if own:
                vop(lambda e: e.tensor_mul(out=wview, in0=cosv, in1=c_bc(0, 16)))
                vop(lambda e: e.tensor_mul(out=cview[:, :, 0:16], in0=sview,
                                           in1=c_bc(16, 32)))
                vop(lambda e: e.tensor_sub(out=cview[:, :, 0:16], in0=wview,
                                           in1=cview[:, :, 0:16]))
                vop(lambda e: e.tensor_mul(out=wview, in0=sview, in1=c_bc(0, 16)))
                vop(lambda e: e.tensor_mul(out=cview[:, :, 16:32], in0=cosv,
                                           in1=c_bc(16, 32)))
                vop(lambda e: e.tensor_add(out=cview[:, :, 16:32], in0=wview,
                                           in1=cview[:, :, 16:32]))
                vop(lambda e: e.tensor_copy(out=cview[:, :, 32:64], in_=c_bc(32, 64)))
            m_rope = b.mark("vector", sV)

            srcs = ([(BT_TM, BT_FM), (CB_TM, CB_FM)]
                    if (own and not CUT & 512) else ([(BT_TM, BT_FM)] if own else []))
            for (srcT, dstF) in srcs:
                for half in range(2):
                    for j in range(8):
                        jj = half * 8 + j
                        top(lambda e, jj=jj, j=j, srcT=srcT: e.transpose(
                            PT[:, j * 128:(j + 1) * 128],
                            srcT[:, jj * 128:(jj + 1) * 128], IDB[:, :]),
                            deps=[m_rope, pf["PT"]])
                    m_pt = b.mark("tensor", sT)
                    vop(lambda e, half=half, dstF=dstF: e.tensor_copy(
                        out=dstF[:, half * 1024:(half + 1) * 1024],
                        in_=PT[:, 0:1024]), deps=[m_pt])
                    pf["PT"] = b.mark("vector", sV)
            m_fm = b.mark("vector", sV)
            if debug and own and cglob == NPC:
                b.dma("sync", taps["sin_dbg"][:, :], SINB[:, :], sO, deps=[m_trig])
                b.dma("sync", taps["cos_dbg"][:, :], COSB[:, :], sO, deps=[m_trig])
                b.dma("sync", taps["stm_dbg"][:, :], STM[:, :], sO, deps=[m_tm])
                b.dma("sync", taps["spr_dbg"][:, :], SPARE[:, 0:128], sO, deps=[m_ev])
                b.dma("sync", taps["btm_dbg"][:, :], BT_TM[:, :], sO, deps=[m_rope])
                b.dma("sync", taps["ctm_dbg"][:, :], CB_TM[:, :], sO, deps=[m_rope])
                b.dma("sync", taps["bfm_dbg"][:, :], BT_FM[:, :], sO, deps=[m_fm])
                st["tap_fm"] = b.dma("sync", taps["cfm_dbg"][:, :], CB_FM[:, :], sO,
                                     deps=[m_fm])
                for _e in ("vector", "scalar", "tensor"):
                    b.wait(_e, (sO, sO.n))

            for qh in range(4):
                hset = list(range(qh * 8, qh * 8 + 8))
                if own:
                    d_rr = b.dma("scalar", ROWREP[:, 0:1024],
                                 scr_row[par:par + 1, qh * 1024:(qh + 1) * 1024]
                                 .broadcast_to([128, 1024]), sB,
                                 deps=[d_r2m, st["rowrep"], st["trig"]])
                    st["rowrep_dma"] = d_rr
                    for i, h in enumerate(hset):
                        vop(lambda e, i=i, h=h: e.scalar_tensor_tensor(
                            out=LBUF[:, i * 128:(i + 1) * 128],
                            in0=ROWREP[:, i * 128:(i + 1) * 128],
                            scalar=JLTM[:, h:h + 1], in1=CAUS[:, :],
                            op0=ALU.add, op1=ALU.min),
                            deps=[d_rr, m_tm, m_cons])
                    st["rowrep"] = b.mark("vector", sV)
                    sop(lambda e: e.activation(out=LBUF[:, 0:1024], in_=LBUF[:, 0:1024],
                                               func=AF.Exp, bias=0.0, scale=1.0),
                        deps=[st["rowrep"]])
                    m_msk = b.mark("scalar", sS)
                    if debug and cglob == NPC:
                        b.dma("sync", taps["l_dbg"][:, qh * 1024:(qh + 1) * 1024],
                              LBUF[:, 0:1024], sO, deps=[m_msk])
                    for i in (0, 2, 4, 6, None, 1, 3, 5, 7):
                        if CUT & 1:
                            break
                        if i is None:
                            top(lambda e: e.transpose(PT[:, 0:128], IDB[:, :],
                                                      IDB[:, :]), deps=[pf["PT"]])
                            continue
                        h = hset[i]
                        pg = PA if i < 4 else PB
                        ql = h % 2
                        top(lambda e, h=h, pg=pg, i=i, ql=ql: e.matmul(
                            pg[:, (i % 4) * 128:(i % 4) * 128 + 128],
                            BT_FM[ql * 64:ql * 64 + 64,
                                  (h // 2) * 128:(h // 2) * 128 + 128],
                            CB_FM[ql * 64:ql * 64 + 64,
                                  (h // 2) * 128:(h // 2) * 128 + 128],
                            start=True, stop=True),
                            deps=[m_fm, pf["PA" if i < 4 else "PB"]])
                    m_g = b.mark("tensor", sT)
                    vop(lambda e, qh=qh: e.tensor_mul(
                        out=CB_TM[:, qh * 512:(qh + 1) * 512]
                        .rearrange("p (h n) -> p h n", h=8),
                        in0=CB_TM[:, qh * 512:(qh + 1) * 512]
                        .rearrange("p (h n) -> p h n", h=8),
                        in1=CSCTM[:, qh * 8:(qh + 1) * 8]
                        .rearrange("p (h o) -> p h o", o=1)
                        .broadcast_to([128, 8, 64])),
                        deps=[m_g, m_tm])
                    m_cbs = b.mark("vector", sV)
                    for lb in range(4):
                        if CUT & 2:
                            break
                        bp = qh * 4 + lb
                        top(lambda e, bp=bp, lb=lb: e.transpose(
                            PT[:, lb * 128:(lb + 1) * 128],
                            CB_TM[:, bp * 128:bp * 128 + 128], IDB[:, :]),
                            deps=[m_cbs, pf["PT"]])
                    m_cbt = b.mark("tensor", sT)
                    vop(lambda e, qh=qh: e.tensor_copy(
                        out=CB_FM[:, qh * 512:(qh + 1) * 512], in_=PT[:, 0:512]),
                        deps=[m_cbt, st.get("tap_fm")])
                    pf["PT"] = b.mark("vector", sV)
                    m_cbf = pf["PT"]
                    for lb in range(4):
                        bp = qh * 4 + lb
                        vop(lambda e, lb=lb, bp=bp: e.tensor_copy(
                            out=SPARE[:, 128 + lb * 64:128 + lb * 64 + 64],
                            in_=HST[:, bp * 64:bp * 64 + 64]),
                            deps=[hst_ok[bp]])
                    m_hstb = b.mark("vector", sV)
                    vop(lambda e, qh=qh: e.tensor_mul(
                        out=XQA[:, 0:512].rearrange("p (h n) -> p h n", h=8),
                        in0=XSb[:, qh * 512:(qh + 1) * 512]
                        .rearrange("p (h n) -> p h n", h=8),
                        in1=DCOL[:, qh * 8:(qh + 1) * 8]
                        .rearrange("p (h o) -> p h o", o=1)
                        .broadcast_to([128, 8, 64])),
                        deps=[m_cons, m_ev, m_rope])
                    vop(lambda e, qh=qh: e.tensor_mul(
                        out=XQB[:, 0:512].rearrange("p (h n) -> p h n", h=8),
                        in0=XSb[:, qh * 512:(qh + 1) * 512]
                        .rearrange("p (h n) -> p h n", h=8),
                        in1=QTM[:, qh * 8:(qh + 1) * 8]
                        .rearrange("p (h o) -> p h o", o=1)
                        .broadcast_to([128, 8, 64])),
                        deps=[m_tm])
                    m_xq = b.mark("vector", sV)
                    m_ycons = st["ycons"]
                    for h4 in range(2):
                        pg = PA if h4 == 0 else PB
                        vop(lambda e, pg=pg, h4=h4: e.tensor_mul(
                            out=WTs[:, 0:512], in0=pg[:, :],
                            in1=LBUF[:, h4 * 512:h4 * 512 + 512]),
                            deps=[m_g, m_msk, m_ycons, st.get("tap_wt")])
                        m_w = b.mark("vector", sV)
                        if debug and cglob == NPC:
                            st["tap_wt"] = b.dma(
                                "sync", taps["wt_dbg"]
                                [:, qh * 1024 + h4 * 512:qh * 1024 + h4 * 512 + 512],
                                WTs[:, 0:512], sO, deps=[m_w])
                        pf["PA" if h4 == 0 else "PB"] = m_w
                        for i4 in range(4):
                            i = h4 * 4 + i4
                            h = qh * 8 + i
                            bp = h // 2
                            lb = i // 2
                            ql = h % 2
                            yc = lb * 128
                            if not CUT & 4:
                                top(lambda e, h=h, i4=i4, ql=ql, yc=yc: e.matmul(
                                    PY[ql * 64:ql * 64 + 64, yc:yc + 128],
                                    XSb[:, h * 64:(h + 1) * 64],
                                    ZIP[:, 0:128] if NO_INTRA else
                                    WTs[:, i4 * 128:(i4 + 1) * 128],
                                    start=True, stop=False),
                                    deps=[m_w, pf["PY"]])
                                top(lambda e, i=i, ql=ql, yc=yc: e.matmul(
                                    PY[ql * 64:ql * 64 + 64, yc:yc + 128],
                                    XQA[:, i * 64:(i + 1) * 64],
                                    ZIP[:, 0:128] if NO_SKIP else IDB[:, :],
                                    start=False, stop=False),
                                    deps=[m_xq])
                            lbq = (bp % 4)
                            if not CUT & 8:
                                top(lambda e, ql=ql, yc=yc, bp=bp, lbq=lbq: e.matmul(
                                    PY[ql * 64:ql * 64 + 64, yc:yc + 128],
                                    SPARE[ql * 64:ql * 64 + 64,
                                          128 + lbq * 64:128 + lbq * 64 + 64],
                                    ZIP[ql * 64:ql * 64 + 64, 0:128] if NO_INTER
                                    else CB_FM[ql * 64:ql * 64 + 64,
                                               bp * 128:bp * 128 + 128],
                                    start=False, stop=True),
                                    deps=[m_fm, m_hstb, m_cbf])
                        m_ycons = b.mark("tensor", sT)
                        st["ycons"] = m_ycons
                    m_y = m_ycons
                    for i, h in enumerate(hset):
                        if CUT & 16:
                            break
                        lb = i // 2
                        ql = h % 2
                        cc = (qh % 2) * 256 + lb * 64
                        top(lambda e, h=h, i=i, ql=ql, cc=cc: e.matmul(
                            PE2[ql * 64:ql * 64 + 64, cc:cc + 64],
                            BT_TM[:, h * 64:(h + 1) * 64],
                            XQB[:, i * 64:(i + 1) * 64],
                            start=True, stop=True),
                            deps=[m_xq, m_brope, pe2_free[qh % 2], st["laststt"]])
                    m_st = b.mark("tensor", sT)
                    p0 = qh * 4
                    vop(lambda e, p0=p0, gci=gci: e.tensor_mul(
                        out=YGb.rearrange("p (m t) -> p m t", m=16)
                        [:, p0:p0 + 4, gci * 128:gci * 128 + 128],
                        in0=PY[:, 0:512].rearrange("p (m t) -> p m t", m=4),
                        in1=SILUZ.rearrange("p (m t) -> p m t", m=16)
                        [:, p0:p0 + 4, gci * 128:gci * 128 + 128]),
                        deps=[m_y, st["siluz"], st["tap_yg"]])
                    pf["PY"] = b.mark("vector", sV)
                    for lb in range(4):
                        bp = qh * 4 + lb
                        cc = (qh % 2) * 256 + lb * 64
                        vop(lambda e, bp=bp, cc=cc: e.scalar_tensor_tensor(
                            out=HST[:, bp * 64:bp * 64 + 64],
                            in0=HST[:, bp * 64:bp * 64 + 64],
                            scalar=DECB[:, bp:bp + 1],
                            in1=PE2[:, cc:cc + 64],
                            op0=ALU.mult, op1=ALU.add),
                            deps=[m_st, m_decb])
                        hst_ok[bp] = b.mark("vector", sV)
                    pe2_free[qh % 2] = hst_ok[qh * 4 + 3]
                    st["laststt"] = hst_ok[qh * 4 + 3]
                else:
                    vop(lambda e, qh=qh: e.tensor_mul(
                        out=XQB[:, 0:512].rearrange("p (h n) -> p h n", h=8),
                        in0=XSb[:, qh * 512:(qh + 1) * 512]
                        .rearrange("p (h n) -> p h n", h=8),
                        in1=QTM[:, qh * 8:(qh + 1) * 8]
                        .rearrange("p (h o) -> p h o", o=1)
                        .broadcast_to([128, 8, 64])),
                        deps=[m_tm, m_ev, m_brope])
                    m_xq = b.mark("vector", sV)
                    for i, h in enumerate(hset):
                        lb = i // 2
                        ql = h % 2
                        cc = (qh % 2) * 256 + lb * 64
                        top(lambda e, h=h, i=i, ql=ql, cc=cc: e.matmul(
                            PE2[ql * 64:ql * 64 + 64, cc:cc + 64],
                            BT_TM[:, h * 64:(h + 1) * 64],
                            XQB[:, i * 64:(i + 1) * 64],
                            start=True, stop=True),
                            deps=[m_xq, m_brope, pe2_free[qh % 2], st["laststt"]])
                    m_st = b.mark("tensor", sT)
                    for lb in range(4):
                        bp = qh * 4 + lb
                        cc = (qh % 2) * 256 + lb * 64
                        vop(lambda e, bp=bp, cc=cc: e.scalar_tensor_tensor(
                            out=HST[:, bp * 64:bp * 64 + 64],
                            in0=HST[:, bp * 64:bp * 64 + 64],
                            scalar=DECB[:, bp:bp + 1],
                            in1=PE2[:, cc:cc + 64],
                            op0=ALU.mult, op1=ALU.add),
                            deps=[m_st, m_decb])
                        hst_ok[bp] = b.mark("vector", sV)
                    pe2_free[qh % 2] = hst_ok[qh * 4 + 3]
                    st["laststt"] = hst_ok[qh * 4 + 3]

            if own:
                vop(lambda e, gci=gci: e.tensor_copy(
                    out=SQW[:, 0:2048],
                    in_=YGb.rearrange("p (m t) -> p m t", m=16)
                    [:, :, gci * 128:gci * 128 + 128]), deps=[pf["PY"], st["wo"]])
                m_c = b.mark("vector", sV)
                sop(lambda e: e.activation(out=SQW[:, 0:2048], in_=SQW[:, 0:2048],
                                           func=AF.Square, bias=0.0, scale=1.0),
                    deps=[m_c])
                m_sq2 = b.mark("scalar", sS)
                for mc in range(16):
                    if CUT & 32:
                        break
                    top(lambda e, mc=mc: e.matmul(
                        PS[0:1, 0:128], ONEB[:, :],
                        SQW[:, mc * 128:(mc + 1) * 128],
                        start=(mc == 0), stop=(mc == 15)),
                        deps=([m_sq2, pf["PS"]] if mc == 0 else ()))
                m_rmsmm = b.mark("tensor", sT)
                vop(lambda e, gci=gci: e.tensor_copy(
                    out=RMSA[0:1, gci * 128:gci * 128 + 128], in_=PS[0:1, 0:128]),
                    deps=[m_rmsmm, st["r2w"]])
                pf["PS"] = b.mark("vector", sV)
                return pf["PS"]
            return b.mark("vector", sV)

        def cast_dma(dst, src, sem=None):
            sem = sem or sC
            sem.inc(16)
            b.ops["gpsimd"].append(lambda e, d=dst, sr=src, sm=sem: e.dma_start(
                out=d, in_=sr).then_inc(sm.h, 16))

        for wh in range(2):
            cast_dma(woutb_d[wh * 1024:(wh + 1) * 1024, :],
                     wout_d[wh * 1024:(wh + 1) * 1024, :], sCW)
        sc_wout = sCW.n
        cast_jobs = []
        for ei in range(NE):
            cast_jobs.append((w13b_d[ei], w13_d[ei]))
        for ei in range(NE):
            cast_jobs.append((w2b_d[ei], w2e_d[ei]))

        for c in range(NPC if PH >= 1 else 0):
            if cast_jobs:
                cast_dma(*cast_jobs.pop(0))
            d_xp = b.dma("gpsimd", XPB.rearrange("p (k t) -> p k t", k=8),
                         xpre[:, c * Q:(c + 1) * Q]
                         .rearrange("(k p) t -> p k t", p=128), sG,
                         deps=[st["hpre"], m_winz])
            m_h = emit_ln(lambda kc: XPB[:, kc * 128:(kc + 1) * 128], 128,
                          lambda kc: HG[:, kc * 128:(kc + 1) * 128],
                          ca1_col, sa_col, c % 2, SQW, deps=[d_xp],
                          x_wide=XPB.rearrange("p (k n) -> p k n", k=8))
            st["hpre"] = m_h
            emit_chunk(lambda kc: HG[:, kc * 128:(kc + 1) * 128], m_h, c, own=False)


        for g in range(NGM if PH >= 3 else (1 if PH >= 2 else 0)):
            t0 = g * GM
            m_h = emit_ln(lambda kc, t0=t0: X[:, kc * LSEG + t0:kc * LSEG + t0 + GM], GM,
                          lambda kc: HG[:, kc * GM:(kc + 1) * GM],
                          ca1_col, sa_col, g % 2, SQW,
                          deps=[d_x, st["tap_h"], st["wo"]],
                          x_wide=X.rearrange("p (k t) -> p k t", k=8)[:, :, t0:t0 + GM])
            if debug:
                for kc in range(8):
                    st["tap_h"] = b.dma(
                        "sync", taps["h_dbg"][kc * 128:(kc + 1) * 128, t0:t0 + GM],
                        HG[:, kc * GM:(kc + 1) * GM], sO, deps=[m_h])
            for mz in range(16):
                pz = PE2 if mz % 2 == 0 else PY
                pzf = pe2_free[0] if mz % 2 == 0 else pf["PY"]
                for kc in range(8):
                    if CUT & 64:
                        break
                    top(lambda e, mz=mz, kc=kc, pz=pz: e.matmul(
                        pz[:, 0:GM],
                        WINZ[:, kc * DI + mz * 128:kc * DI + (mz + 1) * 128],
                        HG[:, kc * GM:(kc + 1) * GM],
                        start=(kc == 0), stop=(kc == 7)),
                        deps=[m_h, m_winz, pzf, pe2_free[1] if mz % 2 == 0 else None])
                m_z = b.mark("tensor", sT)
                sop(lambda e, mz=mz, pz=pz: e.activation(
                    out=SILUZ[:, mz * GM:(mz + 1) * GM], in_=pz[:, 0:GM],
                    func=AF.Silu, bias=0.0, scale=1.0), deps=[m_z])
                m_sz = b.mark("scalar", sS)
                if mz % 2 == 0:
                    pe2_free[0] = m_sz
                    pe2_free[1] = m_sz
                else:
                    pf["PY"] = m_sz
            st["siluz"] = b.mark("scalar", sS)
            m_rms = None
            for gci in range(2):
                cglob = NPC + g * 2 + gci
                m_rms = emit_chunk(
                    lambda kc, gci=gci: HG[:, kc * GM + gci * 128:kc * GM + gci * 128 + 128],
                    m_h, cglob, own=True, gci=gci)
            sop(lambda e: e.activation(out=R3[:, 0:GM], in_=RMSA[0:1, 0:GM],
                                       func=AF.Ln, bias=EPS[0:1, :], scale=1.0 / DI),
                deps=[m_rms, st["r2w"]])
            sop(lambda e: e.activation(out=R3[:, 0:GM], in_=R3[:, 0:GM],
                                       func=AF.Exp, bias=0.0, scale=-0.5))
            m_r2 = b.mark("scalar", sS)
            m_wr2 = b.dma("sync", scr_r2[g % 2:g % 2 + 1, 0:GM], R3[:, 0:GM], sQ,
                          deps=[m_r2])
            st["r2w"] = m_wr2
            m_r2b = b.dma("sync", R2B[:, 0:GM],
                          scr_r2[g % 2:g % 2 + 1, 0:GM].broadcast_to([128, GM]), sQ,
                          deps=[m_wr2])
            if debug:
                for mz in range(16):
                    st["tap_yg"] = b.dma(
                        "sync", taps["yg_dbg"][mz * 128:(mz + 1) * 128, t0:t0 + GM],
                        YGb.rearrange("p (m t) -> p m t", m=16)[:, mz, 0:GM],
                        sO, deps=[m_rms])
            wo_marks = []
            for kc in range(16):
                buf = SQW[:, (kc % 2) * 1024:(kc % 2) * 1024 + 1024]
                d_wo = b.dma("gpsimd", buf, woutb_d[kc * 128:(kc + 1) * 128, :], sG,
                             deps=[wo_marks[kc - 2] if kc >= 2 else m_rms,
                                   (sCW, sc_wout),
                                   pf["PA"], pf["PB"], pf["PC"], pf["PD"]])
                vop(lambda e, buf=buf, kc=kc: e.tensor_scalar(
                    out=buf, in0=buf, scalar1=NRMW[:, kc:kc + 1], scalar2=None,
                    op0=ALU.mult), deps=[d_wo, m_cons])
                m_nw = b.mark("vector", sV)
                for m8 in range(8):
                    if CUT & 128:
                        break
                    pt = (PA, PB, PC, PD)[m8 % 4]
                    cc = (m8 // 4) * 256
                    stt_ = (kc == 0 and m8 < 4)
                    stp_ = (kc == 15 and m8 >= 4)
                    top(lambda e, buf=buf, m8=m8, pt=pt, cc=cc, kc=kc, stt_=stt_,
                        stp_=stp_: e.matmul(
                        pt[:, cc:cc + GM],
                        buf[:, m8 * 128:(m8 + 1) * 128],
                        YGb[:, kc * GM:(kc + 1) * GM],
                        start=stt_, stop=stp_),
                        deps=[m_nw, pf["PA"], pf["PB"], pf["PC"], pf["PD"]])
                wo_marks.append(b.mark("tensor", sT))
            m_wo = wo_marks[-1]
            st["wo"] = m_wo
            for m8 in range(8):
                pt = (PA, PB, PC, PD)[m8 % 4]
                cc = (m8 // 4) * 256
                vop(lambda e, pt=pt, cc=cc, m8=m8: e.scalar_tensor_tensor(
                    out=FFT[:, 0:256], in0=pt[:, cc:cc + GM],
                    scalar=ga_col(m8), in1=R2B[:, 0:GM],
                    op0=ALU.mult, op1=ALU.mult), deps=[m_wo, m_r2b, m_adln])
                vop(lambda e, m8=m8, t0=t0: e.tensor_add(
                    out=X[:, m8 * LSEG + t0:m8 * LSEG + t0 + GM],
                    in0=X[:, m8 * LSEG + t0:m8 * LSEG + t0 + GM],
                    in1=FFT[:, 0:256]))
            m_xu = b.mark("vector", sV)
            pf["PA"] = pf["PB"] = pf["PC"] = pf["PD"] = m_xu

        m_mixer = barrier()
        if debug:
            for kc in range(8):
                b.dma("sync", taps["xmid_dbg"][kc * 128:(kc + 1) * 128, :],
                      X[:, kc * LSEG:(kc + 1) * LSEG], sO, deps=[m_mixer[1]])
            m_mixer = barrier()

        NGF_RUN = NGF if PH >= 5 else (1 if PH >= 4 else 0)
        d_wg = b.dma("gpsimd", WGB.rearrange("p (k g) -> p k g", k=8),
                     wg_d.rearrange("(k p) g -> p k g", p=128), sG,
                     deps=[m_mixer[0]])
        for g in range(NGF_RUN):
            t0 = g * GF
            m_h2 = emit_ln(lambda kc, t0=t0: X[:, kc * LSEG + t0:kc * LSEG + t0 + GF], GF,
                           lambda kc: H2G[:, kc * GF:(kc + 1) * GF],
                           cb1_col, sb_col, g % 2, XBF, deps=[m_mixer[1]],
                           x_wide=X.rearrange("p (k t) -> p k t", k=8)
                           [:, :, t0:t0 + GF])
            for kc in range(8):
                top(lambda e, kc=kc: e.matmul(
                    PS[0:16, 0:512], WGB[:, kc * 16:(kc + 1) * 16],
                    H2G[:, kc * GF:(kc + 1) * GF],
                    start=(kc == 0), stop=(kc == 7)),
                    deps=[d_wg, m_h2, pf["PS"]])
            m_gs = b.mark("tensor", sT)
            sop(lambda e: e.activation(out=GTF[0:16, 0:512], in_=PS[0:16, 0:512],
                                       func=AF.Sigmoid, bias=0.0, scale=1.0),
                deps=[m_gs, st["cmbw"]])
            m_sig = b.mark("scalar", sS)
            pf["PS"] = m_sig
            for sc in range(4):
                top(lambda e, sc=sc: e.transpose(PS[0:128, sc * 16:(sc + 1) * 16],
                                                 GTF[0:16, sc * 128:(sc + 1) * 128],
                                                 IDF[0:16, 0:16]),
                    deps=[m_sig, pf["PS"]])
            m_sct = b.mark("tensor", sT)
            vop(lambda e: e.tensor_copy(out=GTF[:, 512:576], in_=PS[:, 0:64]),
                deps=[m_sct])
            m_sctm = b.mark("vector", sV)
            pf["PS"] = m_sctm
            for sc in range(4):
                S0 = GTF[:, 512 + sc * 16:512 + (sc + 1) * 16]
                W1 = GTF[:, 640:656]
                W2 = GTF[:, 656:672]
                G2 = GTF[:, 672:676]
                M2c = GTF[:, 676:677]
                T1c = GTF[:, 677:678]
                CMB = GTF[:, 576 + sc * 16:576 + (sc + 1) * 16]
                AM = GTF[:, 678:682]
                pairs = [(i, j) for i in range(4) for j in range(i + 1, 4)]
                for pi, (i, j) in enumerate(pairs):
                    vop(lambda e, i=i, j=j, pi=pi: e.tensor_add(
                        out=GTF[:, 682 + pi * 4:686 + pi * 4],
                        in0=S0.rearrange("p (g j) -> p g j", g=4)[:, :, i],
                        in1=S0.rearrange("p (g j) -> p g j", g=4)[:, :, j]),
                        deps=[m_sctm])
                vop(lambda e: e.tensor_copy(out=G2, in_=GTF[:, 682:686]))
                for pi in range(1, 6):
                    vop(lambda e, pi=pi: e.tensor_tensor(
                        out=G2, in0=G2, in1=GTF[:, 682 + pi * 4:686 + pi * 4],
                        op=ALU.max))
                gp = [(i, j) for i in range(4) for j in range(i + 1, 4)]
                for pi, (i, j) in enumerate(gp):
                    vop(lambda e, i=i, j=j, pi=pi: e.tensor_tensor(
                        out=GTF[:, 682 + pi:683 + pi], in0=G2[:, i:i + 1],
                        in1=G2[:, j:j + 1], op=ALU.min))
                vop(lambda e: e.tensor_copy(out=M2c, in_=GTF[:, 682:683]))
                for pi in range(1, 6):
                    vop(lambda e, pi=pi: e.tensor_tensor(
                        out=M2c, in0=M2c, in1=GTF[:, 682 + pi:683 + pi], op=ALU.max))
                vop(lambda e: e.tensor_scalar(out=AM, in0=G2, scalar1=M2c,
                                              scalar2=None, op0=ALU.is_ge))
                amx = AM.rearrange("p (g o) -> p g o", o=1).broadcast_to([128, 4, 4])
                vop(lambda e: e.tensor_mul(out=W1.rearrange("p (g j) -> p g j", g=4),
                                           in0=S0.rearrange("p (g j) -> p g j", g=4),
                                           in1=amx))
                vop(lambda e: e.tensor_scalar(
                    out=W2.rearrange("p (g j) -> p g j", g=4), in0=amx,
                    scalar1=1e30, scalar2=-1e30, op0=ALU.mult, op1=ALU.add))
                vop(lambda e: e.tensor_add(out=W1, in0=W1, in1=W2))
                for it in range(3):
                    vop(lambda e: e.reduce_max(out=T1c, in_=W1, axis=mybir.AxisListType.X))
                    vop(lambda e: e.tensor_scalar(out=W2, in0=W1, scalar1=T1c,
                                                  scalar2=None, op0=ALU.is_ge))
                    vop(lambda e: e.tensor_scalar(out=W2, in0=W2, scalar1=-1e30,
                                                  scalar2=None, op0=ALU.mult))
                    vop(lambda e: e.tensor_add(out=W1, in0=W1, in1=W2))
                vop(lambda e: e.reduce_max(out=T1c, in_=W1, axis=mybir.AxisListType.X))
                vop(lambda e: e.tensor_mul(out=W1.rearrange("p (g j) -> p g j", g=4),
                                           in0=S0.rearrange("p (g j) -> p g j", g=4),
                                           in1=amx))
                vop(lambda e: e.tensor_scalar(
                    out=W2.rearrange("p (g j) -> p g j", g=4), in0=amx,
                    scalar1=1e30, scalar2=-1e30, op0=ALU.mult, op1=ALU.add))
                vop(lambda e: e.tensor_add(out=W1, in0=W1, in1=W2))
                vop(lambda e: e.tensor_scalar(out=W2, in0=W1, scalar1=T1c,
                                              scalar2=None, op0=ALU.is_ge))
                vop(lambda e: e.tensor_mul(out=W1, in0=S0, in1=W2))
                vop(lambda e: e.reduce_sum(out=T1c, in_=W1, axis=mybir.AxisListType.X))
                vop(lambda e: e.tensor_scalar(out=T1c, in0=T1c, scalar1=1e-6,
                                              scalar2=None, op0=ALU.add))
                vop(lambda e: e.reciprocal(out=T1c, in_=T1c))
                vop(lambda e: e.tensor_scalar(out=CMB, in0=W1, scalar1=T1c,
                                              scalar2=None, op0=ALU.mult))
            m_cmb = b.mark("vector", sV)
            for sc in range(4):
                top(lambda e, sc=sc: e.transpose(PS[0:16, sc * 128:(sc + 1) * 128],
                                                 GTF[:, 576 + sc * 16:576 + (sc + 1) * 16],
                                                 IDF[0:128, 0:128]),
                    deps=[m_cmb, pf["PS"]])
            m_ct = b.mark("tensor", sT)
            vop(lambda e: e.tensor_copy(out=GTF[0:16, 0:512], in_=PS[0:16, 0:512]),
                deps=[m_ct])
            m_cfm = b.mark("vector", sV)
            pf["PS"] = m_cfm
            if debug and g == 0:
                b.dma("sync", taps["cmb2_dbg"][:, :], GTF[0:16, 0:512], sO,
                      deps=[m_cfm])
                for kkc in range(8):
                    b.dma("sync", taps["h2_dbg"][:, kkc * 512:(kkc + 1) * 512],
                          H2G[:, kkc * 512:(kkc + 1) * 512], sO, deps=[m_h2])
                for _e in ("vector", "scalar", "tensor"):
                    b.wait(_e, (sO, sO.n))
            m_cw = b.dma("sync", scr_cmb[0:1, :].rearrange("o (e t) -> (o e) t", e=16),
                         GTF[0:16, 0:512], sQ, deps=[m_cfm])
            st["cmbw"] = m_cw
            exp_marks = []
            out67_mark = st.get("ffn_free")
            for ei in range(NE):
                buf = EWB[ei % 2]
                dd = [exp_marks[ei - 2] if ei >= 2 else
                      (m_mixer[0] if g == 0 else st["exp_prev"][ei % 2])]
                d13 = b.dma("gpsimd", buf[:, 0:8192].rearrange("p (k f) -> p k f", k=8),
                            w13b_d[ei].rearrange("(k p) f -> p k f", p=128), sG,
                            deps=dd + [(sC, sC.n)])
                d2 = b.dma("gpsimd",
                           buf[:, 8192:12288].rearrange("p (k m) -> p k m", k=4),
                           w2b_d[ei].rearrange("(k p) m -> p k m", p=128), sG)
                m_cbc = None
                if ei < 16:
                    d_cb = b.dma("sync", CBTF[:, 0:512],
                                 scr_cmb[0:1, ei * 512:(ei + 1) * 512]
                                 .broadcast_to([128, 512]), sQ,
                                 deps=[m_cw, st["cbc"]])
                    vop(lambda e, ei=ei: e.tensor_copy(
                        out=CBTB[:, (ei % 2) * 512:(ei % 2) * 512 + 512],
                        in_=CBTF[:, 0:512]), deps=[d_cb])
                    m_cbc = b.mark("vector", sV)
                    st["cbc"] = m_cbc
                m_vfc = [None] * 4
                mul_marks = [None] * 4
                silu_marks = [None] * 4
                w2b = ("PA", "PB", "PC", "PD", "PE2")

                def emit_p1(fc):
                    for kc in range(8):
                        top(lambda e, fc=fc, kc=kc, buf=buf: e.matmul(
                            PS[:, 0:512],
                            buf[:, kc * 1024 + fc * 128:
                                kc * 1024 + fc * 128 + 128],
                            H2G[:, kc * GF:(kc + 1) * GF],
                            start=(kc == 0), stop=(kc == 7)),
                            deps=([d13, m_h2, pf["PS"]] if fc == 0 else
                                  ([silu_marks[fc - 1]] if kc == 0 else ())))
                    m_p1u = b.mark("tensor", sT)
                    sop(lambda e, fc=fc: e.activation(
                        out=M1T[:, fc * 512:(fc + 1) * 512], in_=PS[:, 0:512],
                        func=AF.Silu, bias=0.0, scale=1.0), deps=[m_p1u])
                    silu_marks[fc] = b.mark("scalar", sS)

                def emit_p3(fc):
                    for kc in range(8):
                        top(lambda e, fc=fc, kc=kc, buf=buf: e.matmul(
                            PY[:, 0:512],
                            buf[:, kc * 1024 + 512 + fc * 128:
                                kc * 1024 + 512 + fc * 128 + 128],
                            H2G[:, kc * GF:(kc + 1) * GF],
                            start=(kc == 0), stop=(kc == 7)),
                            deps=(([silu_marks[fc], pf["PY"]] if fc == 0 else
                                   [silu_marks[fc], mul_marks[fc - 1]])
                                  if kc == 0 else ()))
                    m_p3u = b.mark("tensor", sT)
                    vop(lambda e, fc=fc: e.tensor_mul(
                        out=TVT[:, fc * 512:(fc + 1) * 512], in0=PY[:, 0:512],
                        in1=M1T[:, fc * 512:(fc + 1) * 512]),
                        deps=[m_p3u, silu_marks[fc]])
                    if ei < 16:
                        vop(lambda e, fc=fc, ei=ei: e.tensor_mul(
                            out=TVT[:, fc * 512:(fc + 1) * 512],
                            in0=TVT[:, fc * 512:(fc + 1) * 512],
                            in1=CBTB[:, (ei % 2) * 512:(ei % 2) * 512 + 512]),
                            deps=[m_cbc])
                    mul_marks[fc] = b.mark("vector", sV)
                    m_vfc[fc] = mul_marks[fc]

                def emit_w2(fc):
                    for m8 in range(5):
                        tgt = (PA, PB, PC, PD, PE2)[m8]
                        stt_ = (ei == 0 and fc == 0)
                        stp_ = (ei == NE - 1 and fc == 3)
                        top(lambda e, m8=m8, fc=fc, buf=buf, tgt=tgt, stt_=stt_,
                            stp_=stp_: e.matmul(
                            tgt[:, 0:512],
                            buf[:, 8192 + fc * 1024 + m8 * 128:
                                8192 + fc * 1024 + m8 * 128 + 128],
                            TVT[:, fc * 512:(fc + 1) * 512],
                            start=stt_, stop=stp_),
                            deps=[d2, m_vfc[fc],
                                  (pf.get(w2b[m8])
                                   if (ei == 0 and fc == 0 and m8 != 4) else None),
                                  (pe2_free[0] if (m8 == 4 and ei == 0 and fc == 0)
                                   else None),
                                  (pe2_free[1] if (m8 == 4 and ei == 0 and fc == 0)
                                   else None)])

                emit_p1(0)
                emit_p3(0)
                emit_p1(1)
                emit_w2(0)
                emit_p3(1)
                emit_p1(2)
                emit_w2(1)
                emit_p3(2)
                emit_p1(3)
                emit_w2(2)
                emit_p3(3)
                emit_w2(3)
                if debug and g == 0 and ei == 0:
                    b.dma("sync", taps["v_dbg"][:, :], TVT[:, :], sO, deps=[m_vfc[3]])
                    b.dma("sync", taps["m1_dbg"][:, :], M1T[:, :], sO, deps=[m_vfc[3]])
                    for _e in ("vector", "scalar", "tensor"):
                        b.wait(_e, (sO, sO.n))
                evac5 = None
                for (m8, tgt, acc, a0) in ((5, PS, OUT3, 0), (6, PY, OUT67, 0),
                                           (7, PS, OUT67, 512)):
                    pdep = (silu_marks[3] if m8 == 5 else
                            (mul_marks[3] if m8 == 6 else evac5))
                    for fc in range(4):
                        top(lambda e, m8=m8, fc=fc, buf=buf, tgt=tgt: e.matmul(
                            tgt[:, 0:512],
                            buf[:, 8192 + fc * 1024 + m8 * 128:
                                8192 + fc * 1024 + m8 * 128 + 128],
                            TVT[:, fc * 512:(fc + 1) * 512],
                            start=(fc == 0), stop=(fc == 3)),
                            deps=([d2, m_vfc[fc]] + ([pdep] if fc == 0 else [])))
                    m_w2 = b.mark("tensor", sT)
                    if ei == 0:
                        vop(lambda e, acc=acc, a0=a0, tgt=tgt: e.tensor_copy(
                            out=acc[:, a0:a0 + 512], in_=tgt[:, 0:512]),
                            deps=[m_w2, out67_mark])
                    else:
                        vop(lambda e, acc=acc, a0=a0, tgt=tgt: e.tensor_add(
                            out=acc[:, a0:a0 + 512], in0=acc[:, a0:a0 + 512],
                            in1=tgt[:, 0:512]), deps=[m_w2])
                    mm = b.mark("vector", sV)
                    if m8 == 5:
                        evac5 = mm
                    elif m8 == 6:
                        pf["PY"] = mm
                    else:
                        pf["PS"] = mm
                exp_marks.append(b.mark("tensor", sT))
            st["exp_prev"] = [exp_marks[16], exp_marks[17]]
            m_exp = exp_marks[-1]
            m_o67 = b.mark("vector", sV)
            for m8 in range(8):
                if m8 < 5:
                    src = (PA, PB, PC, PD, PE2)[m8]
                    vop(lambda e, src=src, m8=m8: e.tensor_scalar(
                        out=FFT[:, 0:512], in0=src[:, 0:512], scalar1=gb_col(m8),
                        scalar2=None, op0=ALU.mult), deps=[m_exp, m_adln])
                else:
                    srcb = OUT3 if m8 == 5 else OUT67
                    a0 = 0 if m8 <= 6 else 512
                    vop(lambda e, srcb=srcb, a0=a0, m8=m8: e.tensor_scalar(
                        out=FFT[:, 0:512],
                        in0=srcb[:, a0:a0 + 512],
                        scalar1=gb_col(m8), scalar2=None, op0=ALU.mult),
                        deps=[m_o67, m_adln])
                vop(lambda e, m8=m8, t0=t0: e.tensor_add(
                    out=X[:, m8 * LSEG + t0:m8 * LSEG + t0 + GF],
                    in0=X[:, m8 * LSEG + t0:m8 * LSEG + t0 + GF],
                    in1=FFT[:, 0:512]))
            ffn_out_free = b.mark("vector", sV)
            st["ffn_free"] = ffn_out_free
            pf["PA"] = pf["PB"] = pf["PC"] = pf["PD"] = pf["PY"] = ffn_out_free
            pe2_free[0] = pe2_free[1] = ffn_out_free

        m_done = barrier()
        for kc in range(8):
            b.dma("sync", out_T[kc * 128:(kc + 1) * 128, :],
                  X[:, kc * LSEG:(kc + 1) * LSEG], sO, deps=[m_done[1]])
        b.emit("sync", lambda e, v=sO.n: e.wait_ge(sO.h, v))

        with nc.Block() as blk:
            blk.sync(lambda e: [f(e) for f in b.ops["sync"]])
            blk.tensor(lambda e: [f(e) for f in b.ops["tensor"]])
            blk.vector(lambda e: [f(e) for f in b.ops["vector"]])
            blk.scalar(lambda e: [f(e) for f in b.ops["scalar"]])
            blk.gpsimd(lambda e: [f(e) for f in b.ops["gpsimd"]])
    return nc


_PROG_CACHE = {}


def _get_prog(debug=False):
    if debug not in _PROG_CACHE:
        _PROG_CACHE[debug] = build_core_program(debug)
    return _PROG_CACHE[debug]


def _stage_inputs(inp):
    f = np.float32
    x = np.asarray(inp["L0_attn_x"], f)
    win = np.asarray(inp["L0_attn_W_in"], f)
    wout = np.ascontiguousarray(np.asarray(inp["L0_attn_W_out"], f))
    dtbias = np.asarray(inp["L0_attn_dt_bias"], f)
    alogv = np.asarray(inp["L0_attn_A_log"], f)
    dskip = np.asarray(inp["L0_attn_D_skip"], f)
    nrm = np.asarray(inp["L0_attn_norm_weight"], f)
    wgate = np.asarray(inp["L0_ffn_w_gate"], f)
    w1e = np.asarray(inp["L0_ffn_w1_e"], f)
    w3e = np.asarray(inp["L0_ffn_w3_e"], f)
    w2e = np.asarray(inp["L0_ffn_w2_e"], f)
    w1s = np.asarray(inp["L0_ffn_w1_s"], f)
    w3s = np.asarray(inp["L0_ffn_w3_s"], f)
    w2s = np.asarray(inp["L0_ffn_w2_s"], f)
    cv = np.asarray(inp["L0_c"], f)
    wmod = np.ascontiguousarray(np.asarray(inp["L0_wmod"], f))

    perm = np.concatenate([np.arange(0, 32, 2), np.arange(1, 32, 2),
                           np.arange(32, 64)])
    win_x = np.ascontiguousarray(np.concatenate(
        [win[:, 2048:4096], win[:, 4096:4160][:, perm],
         win[:, 4160:4224][:, perm], win[:, 4224:4256]], axis=1))
    win_z = np.ascontiguousarray(win[:, 0:2048])

    w13 = np.empty((NE, DM, 1024), f)
    w2a = np.empty((NE, 512, DM), f)
    for e in range(16):
        w13[e, :, 0:512] = w1e[e].T
        w13[e, :, 512:1024] = w3e[e].T
        w2a[e] = w2e[e].T
    for half in range(2):
        sl = slice(half * 512, (half + 1) * 512)
        w13[16 + half, :, 0:512] = w1s[sl, :].T
        w13[16 + half, :, 512:1024] = w3s[sl, :].T
        w2a[16 + half] = w2s[:, sl].T

    invfreq = ((1.0 / 10000.0 ** (np.arange(16, dtype=np.float64) / 16.0))
               / TWO_PI).astype(f)
    ifr = np.broadcast_to(np.tile(invfreq, 32)[None, :], (128, 512)).astype(f).copy()
    caus = np.where(np.arange(128)[None, :] >= np.arange(128)[:, None],
                    f(4.0), f(-1e30)).astype(f)
    common = dict(
        win_x=win_x, win_z=win_z, wout=wout,
        dtb=np.ascontiguousarray(dtbias[:, None]),
        alog=np.ascontiguousarray(alogv[:, None]),
        dcols=np.broadcast_to(dskip[None, :], (128, 32)).copy(),
        normw=np.ascontiguousarray(nrm.reshape(16, 128).T),
        wg=np.ascontiguousarray(wgate.T), w13=w13, w2e=w2a,
        wmod=wmod,
        identb=np.eye(128, dtype=BF), identf=np.eye(128, dtype=f),
        onesb=np.ones((128, 1), BF), ifreqt=ifr, causn=caus,
    )
    in_maps = []
    for core in range(8):
        bi = core // 4
        seg = core % 4
        s0 = seg * LSEG
        xs_T = np.ascontiguousarray(x[bi, s0:s0 + LSEG, :].T)
        npad = LPRE - s0
        xpre_T = np.zeros((DM, LPRE), f)
        if s0 > 0:
            xpre_T[:, npad:] = x[bi, 0:s0, :].T
        val = np.zeros((128, NPC), f)
        val[:, npad // Q:] = 1.0
        m = dict(common)
        m.update(xseg=xs_T, xpre=xpre_T, validd=val,
                 cvec=np.ascontiguousarray(cv[bi].reshape(8, 128).T))
        in_maps.append(m)
    return in_maps


def kernel(**inputs):
    import os
    debug = bool(os.environ.get("MK_DBG"))
    nc = _get_prog(debug)
    in_maps = _stage_inputs(inputs)
    res = run_bass_kernel_spmd(nc, in_maps, core_ids=list(range(8)))
    out = np.empty((2, 8192, DM), np.float32)
    for core in range(8):
        bi, seg = core // 4, core % 4
        out[bi, seg * LSEG:(seg + 1) * LSEG, :] = res.results[core]["out_T"].T
    if debug:
        kernel._dbg = res.results
    return out


# version 07
# speedup vs baseline: 1.1587x; 1.0680x over previous
import os as _os
NO_INTRA = bool(_os.environ.get("MK_NO_INTRA"))
NO_SKIP = bool(_os.environ.get("MK_NO_SKIP"))
NO_INTER = bool(_os.environ.get("MK_NO_INTER"))
PH = int(_os.environ.get("MK_PH", "99"))
CUT = int(_os.environ.get("MK_CUT", "0"))
"""Fused megakernel for mk_adaln_d1: adaLN block = mamba2-SSD mixer + MoE FFN.

One raw-Bass program per NeuronCore, 8 cores, ONE run_bass_kernel_spmd launch.

Sharding: 8 cores x 2048 contiguous tokens (2 sequences x 4 segments each).
The scan state entering a segment is recomputed on-core by a state-only pass
over the zero-padded prefix of its sequence; a 0/1 validity mask (per-core
data) zeroes the state injections of pad tokens, so the program is
SPMD-uniform.

Layouts: residual feature-major [128 feat x tokens] fp32 (host stages x
transposed; output transposed back on host - layout only). Matmuls bf16
operands / fp32 PSUM accumulation; elementwise fp32 except matmul-operand
level factors (rope cos/sin, decay/q factors) which are bf16.
Scan: chunked SSD (Q=128); RoPE-on-state handled exactly by derotating B/C
with cumulative angles; decay mask exp(min(rowrep(-a sr_i) + col(a sr_j +
ln dt_j), caus)) with caus = 4.0 for i>=j else -1e30 (causality + clamp).
"""
import math
import numpy as np
import ml_dtypes
import concourse.bass as bass
import concourse.mybir as mybir
from concourse.bass_utils import run_bass_kernel_spmd
from contextlib import ExitStack

F32 = mybir.dt.float32
BF16 = mybir.dt.bfloat16
I32 = mybir.dt.int32
AF = mybir.ActivationFunctionType
ALU = mybir.AluOpType

DM, DI, NH = 1024, 2048, 32
LSEG, LPRE, Q = 2048, 6144, 128
NCH, NPC = LSEG // Q, LPRE // Q
NE = 18
GM, NGM = 256, 8
GF, NGF = 512, 4
TWO_PI = 2.0 * math.pi
PI_CL = 3.1415925
NIX = 2208
ENGS = ("sync", "tensor", "vector", "scalar", "gpsimd")
BF = ml_dtypes.bfloat16


class Sem:
    def __init__(self, h, owner=None):
        self.h = h
        self.n = 0
        self.owner = owner
        self.rr = 0

    def inc(self, k=1):
        self.n += k
        return self.n


CHAINED = ("vector", "scalar", "gpsimd")


class Bld:
    def __init__(self, esem=None):
        self.ops = {k: [] for k in ENGS}
        self.waited = {k: {} for k in ENGS}
        self.last_unmarked = {k: None for k in ENGS}
        self.last_mark = {k: None for k in ENGS}
        self.esem = esem or {}

    def wait(self, eng, dep):
        if dep is None:
            return
        sem, val = dep
        if val <= 0:
            return
        if self.waited[eng].get(id(sem), -1) >= val:
            return
        self.waited[eng][id(sem)] = val
        self.ops[eng].append(lambda e, s=sem, v=val: e.wait_ge(s.h, v))

    def emit(self, eng, fn, deps=()):
        if eng in CHAINED:
            sem = self.esem[eng]
            self.wait(eng, (sem, sem.n))
            for d in deps:
                self.wait(eng, d)
            v = sem.inc(1)

            def wrapped(e, p=fn, s=sem):
                r = p(e)
                inst = r[-1] if isinstance(r, tuple) else r
                inst.then_inc(s.h, 1)
                return r
            self.ops[eng].append(wrapped)
            self.last_mark[eng] = (sem, v)
        else:
            for d in deps:
                self.wait(eng, d)
            self.ops[eng].append(fn)
            self.last_unmarked[eng] = len(self.ops[eng]) - 1

    def mark(self, eng, sem):
        if eng in CHAINED:
            return self.last_mark[eng]
        idx = self.last_unmarked[eng]
        if idx is None:
            return self.last_mark[eng]
        v = sem.inc(1)
        prev = self.ops[eng][idx]

        def wrapped(e, p=prev, s=sem):
            r = p(e)
            inst = r[-1] if isinstance(r, tuple) else r
            inst.then_inc(s.h, 1)
            return r
        self.ops[eng][idx] = wrapped
        self.last_unmarked[eng] = None
        self.last_mark[eng] = (sem, v)
        return (sem, v)

    def dma(self, eng, out, in_, sem, deps=(), slow=False):
        if isinstance(sem, list):
            pool = sem
            slot = pool[0].rr % len(pool)
            pool[0].rr += 1
            ssem = pool[slot]
        else:
            ssem = sem
        for d in deps:
            self.wait(eng, d)
        self.wait(eng, (ssem, ssem.n))
        v = ssem.inc(16)
        self.ops[eng].append(
            lambda e, o=out, i=in_, s=ssem, sl=slow: e.dma_start(
                out=o, in_=i, allow_slow_non_contiguous=sl).then_inc(s.h, 16))
        return (ssem, v)


def build_core_program(debug=False):
    nc = bass.Bass("TRN2", target_bir_lowering=False, debug=False)

    def din(name, shape, dtype=F32):
        return nc.dram_tensor(name, shape, dtype, kind="ExternalInput").ap()

    xseg = din("xseg", [DM, LSEG])
    xpre = din("xpre", [DM, LPRE])
    validd = din("validd", [128, NPC])
    win_x = din("win_x", [DM, NIX])
    win_z = din("win_z", [DM, DI])
    wout_d = din("wout", [DI, DM])
    dtb = din("dtb", [NH, 1])
    alog = din("alog", [NH, 1])
    dcols = din("dcols", [128, NH])
    normw_d = din("normw", [128, 16])
    wg_d = din("wg", [DM, 16])
    w13_d = din("w13", [NE, DM, 1024])
    w2e_d = din("w2e", [NE, 512, DM])
    cvec = din("cvec", [128, 8])
    wmod_d = din("wmod", [DM, 6 * DM])
    identb = din("identb", [128, 128], BF16)
    identf = din("identf", [128, 128], F32)
    onesb = din("onesb", [128, 1], BF16)
    ifreqt = din("ifreqt", [128, 512])
    causn = din("causn", [128, 128])

    out_T = nc.dram_tensor("out_T", [DM, LSEG], F32, kind="ExternalOutput").ap()
    taps = {}
    if debug:
        taps["h_dbg"] = nc.dram_tensor("h_dbg", [DM, LSEG], BF16, kind="ExternalOutput").ap()
        taps["yg_dbg"] = nc.dram_tensor("yg_dbg", [DI, LSEG], BF16, kind="ExternalOutput").ap()
        taps["xmid_dbg"] = nc.dram_tensor("xmid_dbg", [DM, LSEG], F32, kind="ExternalOutput").ap()
        taps["l_dbg"] = nc.dram_tensor("l_dbg", [128, 4096], F32, kind="ExternalOutput").ap()
        taps["wt_dbg"] = nc.dram_tensor("wt_dbg", [128, 4096], BF16, kind="ExternalOutput").ap()
        taps["g_dbg"] = nc.dram_tensor("g_dbg", [128, 4096], F32, kind="ExternalOutput").ap()
        taps["bfm_dbg"] = nc.dram_tensor("bfm_dbg", [128, 2048], BF16, kind="ExternalOutput").ap()
        taps["cfm_dbg"] = nc.dram_tensor("cfm_dbg", [128, 2048], BF16, kind="ExternalOutput").ap()
        taps["btm_dbg"] = nc.dram_tensor("btm_dbg", [128, 2048], BF16, kind="ExternalOutput").ap()
        taps["ctm_dbg"] = nc.dram_tensor("ctm_dbg", [128, 2048], BF16, kind="ExternalOutput").ap()
        taps["sin_dbg"] = nc.dram_tensor("sin_dbg", [128, 512], BF16, kind="ExternalOutput").ap()
        taps["cos_dbg"] = nc.dram_tensor("cos_dbg", [128, 512], BF16, kind="ExternalOutput").ap()
        taps["stm_dbg"] = nc.dram_tensor("stm_dbg", [128, 32], F32, kind="ExternalOutput").ap()
        taps["spr_dbg"] = nc.dram_tensor("spr_dbg", [128, 128], BF16, kind="ExternalOutput").ap()
        taps["h2_dbg"] = nc.dram_tensor("h2_dbg", [128, 4096], BF16, kind="ExternalOutput").ap()
        taps["cmb2_dbg"] = nc.dram_tensor("cmb2_dbg", [16, 512], F32, kind="ExternalOutput").ap()
        taps["v_dbg"] = nc.dram_tensor("v_dbg", [128, 2048], BF16, kind="ExternalOutput").ap()
        taps["m1_dbg"] = nc.dram_tensor("m1_dbg", [128, 2048], BF16, kind="ExternalOutput").ap()

    scr_row = nc.dram_tensor("scr_row", [2, NH * Q], F32, kind="Internal").ap()
    scr_dec = nc.dram_tensor("scr_dec", [2, NH], F32, kind="Internal").ap()
    scr_ln = nc.dram_tensor("scr_ln", [2, 1024], F32, kind="Internal").ap()
    scr_r2 = nc.dram_tensor("scr_r2", [2, GM], F32, kind="Internal").ap()
    scr_cmb = nc.dram_tensor("scr_cmb", [1, 16 * GF], F32, kind="Internal").ap()
    scr_mod = nc.dram_tensor("scr_mod", [1, 6 * DM], F32, kind="Internal").ap()
    w13b_d = nc.dram_tensor("w13b", [NE, DM, 1024], BF16, kind="Internal").ap()
    w2b_d = nc.dram_tensor("w2b", [NE, 512, DM], BF16, kind="Internal").ap()
    woutb_d = nc.dram_tensor("woutb", [DI, DM], BF16, kind="Internal").ap()

    b = Bld()

    with ExitStack() as es:
        EN = es.enter_context

        def sbt(name, shape, dtype=F32):
            return EN(nc.sbuf_tensor(name, shape, dtype))

        X = sbt("X", [128, 8 * LSEG])
        WAR = sbt("WAR", [128, 34048], BF16)
        WINX = WAR[:, 0:8 * NIX]
        WINZ = WAR[:, 8 * NIX:8 * NIX + 8 * DI]
        EWB = [WAR[:, i * 12288:(i + 1) * 12288] for i in range(2)]

        BFA = sbt("BFA", [128, 23552], BF16)
        BT_TM = BFA[:, 0:2048]
        CB_TM = BFA[:, 2048:4096]
        BT_FM = BFA[:, 4096:6144]
        CB_FM = BFA[:, 6144:8192]
        XSb = BFA[:, 8192:10240]
        XQA = BFA[:, 10240:10752]
        XQB = BFA[:, 10752:11264]
        WTs = BFA[:, 11264:11776]
        SINB = BFA[:, 11776:12288]
        COSB = BFA[:, 12288:12800]
        HG = BFA[:, 12800:14848]
        SILUZ = BFA[:, 14848:18944]
        YGb = BFA[:, 18944:23040]
        SPARE = BFA[:, 23040:23552]
        ZIP = SPARE[:, 384:512]
        WMBUF = [BFA[:, 0:4096], BFA[:, 4096:8192]]
        XPB = SILUZ[:, 0:1024]
        M1T = BFA[:, 0:2048]
        TVT = BFA[:, 2048:4096]
        CBTB = BFA[:, 4096:5120]
        H2G = BFA[:, 5120:9216]
        XBF = BFA[:, 14848:18944]
        WGB = BFA[:, 23040:23168]

        FA = sbt("FA", [128, 4096])
        ROWREP = FA[:, 0:1024]
        LBUF = FA[:, 1024:2048]
        ANG_A = FA[:, 2048:2304]
        ANG_B = FA[:, 2304:2560]
        YSUM = FA[:, 2048:2560]
        HST = FA[:, 2560:3584]
        MRB = FA[:, 1024:2048]
        R2B = FA[:, 3584:3840]
        R1 = FA[0:1, 1024:1536]
        R2 = FA[0:1, 2048:2560]
        R3 = FA[0:1, 3840:4096]
        RMSA = FA[0:1, 3840:4096]
        OUT67 = FA[:, 0:1024]
        OUT3 = FA[:, 2048:2560]
        CBTF = FA[:, 1024:1536]
        FFT = FA[:, 1536:2048]
        GTF = FA[:, 2560:3584]

        KI = sbt("KI", [128, 512], I32)
        IDB = sbt("IDB", [128, 128], BF16)
        IDF = sbt("IDF", [128, 128], F32)
        ONEB = sbt("ONEB", [128, 1], BF16)
        ONESR = sbt("ONESR", [1, 128], F32)
        IFRQ = sbt("IFRQ", [128, 512])
        CAUS = sbt("CAUS", [128, 128])
        ADLN = sbt("ADLN", [128, 48])
        ADL1 = sbt("ADL1", [128, 16])
        NRMW = sbt("NRMW", [128, 16])
        DCOL = sbt("DCOL", [128, NH])
        VALID = sbt("VALID", [128, NPC])
        CV = sbt("CV", [128, 8])
        CVB = sbt("CVB", [128, 8], BF16)
        T32 = sbt("T32", [NH, 5 * Q])
        TDEC = sbt("TDEC", [NH, 1])
        TCOL = sbt("TCOL", [NH, 2])
        BASE = sbt("BASE", [NH, 1])
        DTBN = sbt("DTBN", [NH, 1])
        ACOL = sbt("ACOL", [NH, 1])
        EPS = sbt("EPS", [128, 1])
        STM = sbt("STM", [128, 32])
        JLTM = sbt("JLTM", [128, 32])
        QTM = sbt("QTM", [128, 32])
        CSCTM = sbt("CSCTM", [128, 32])
        DECB = sbt("DECB", [128, 32])
        QTM2 = sbt("QTM2", [128, 32])
        SQW = sbt("SQW", [128, 2048], BF16)

        def t32(i):
            return T32[:, i * Q:(i + 1) * Q]

        PA = EN(nc.psum_tensor("PA", [128, 512], F32))
        PB = EN(nc.psum_tensor("PB", [128, 512], F32))
        PC = EN(nc.psum_tensor("PC", [128, 512], F32))
        PD = EN(nc.psum_tensor("PD", [128, 512], F32))
        PE2 = EN(nc.psum_tensor("PE2", [128, 512], F32))
        PY = EN(nc.psum_tensor("PY", [128, 512], F32))
        PS = EN(nc.psum_tensor("PS", [128, 512], F32))
        PT = EN(nc.psum_tensor("PT", [128, 1024], BF16))

        sQ = [Sem(EN(nc.semaphore(f"sQ{i}"))) for i in range(8)]
        sG = [Sem(EN(nc.semaphore(f"sG{i}"))) for i in range(8)]
        sB = [Sem(EN(nc.semaphore(f"sB{i}"))) for i in range(8)]
        sT = Sem(EN(nc.semaphore("sT")), "tensor")
        sV = Sem(EN(nc.semaphore("sV")), "vector")
        sS = Sem(EN(nc.semaphore("sS")), "scalar")
        sO = Sem(EN(nc.semaphore("sO")))
        sP = Sem(EN(nc.semaphore("sP")), "gpsimd")
        sC = Sem(EN(nc.semaphore("sC")))
        sCW = Sem(EN(nc.semaphore("sCW")))
        ESEM = {"tensor": sT, "vector": sV, "scalar": sS, "gpsimd": sP}
        b.esem = {"vector": sV, "scalar": sS, "gpsimd": sP}

        def vop(fn, deps=()):
            b.emit("vector", fn, deps)

        def sop(fn, deps=()):
            b.emit("scalar", fn, deps)

        def top(fn, deps=()):
            b.emit("tensor", fn, deps)

        def barrier():
            ms = [b.mark(e, ESEM[e]) for e in ("tensor", "vector", "scalar")]
            dsems = sQ + sG + sB + [sO]
            for e in ("tensor", "vector", "scalar"):
                for m in ms:
                    b.wait(e, m)
                for ds in dsems:
                    b.wait(e, (ds, ds.n))
            return ms

        pf = {k: None for k in ("PA", "PB", "PC", "PD", "PY", "PS", "PT")}
        pe2_free = [None, None]
        hst_ok = [None] * 16
        st = {"rowrep": None, "siluz": None, "hpre": None, "ipro": None,
              "sbm": [None, None], "sbv": [None, None],
              "ycons": None, "laststt": None, "cbc": None,
              "exp_prev": [None, None], "lnw": None, "r2w": None,
              "rowrep_dma": None, "rowbounce": None, "decw": None,
              "cmbw": None, "tap_h": None, "tap_yg": None, "modw": None,
              "wo": None, "tap_wt": None, "tap_fm": None}

        m_cons = None
        for t, d in ((IDB, identb), (IDF, identf), (ONEB, onesb), (IFRQ, ifreqt),
                     (CAUS, causn), (VALID, validd), (CV, cvec), (DTBN, dtb),
                     (ACOL, alog), (NRMW, normw_d), (DCOL, dcols)):
            m_cons = b.dma("sync", t[:, :], d, sQ[0])
        d_x = None
        for kc in range(8):
            d_x = b.dma("sync", X[:, kc * LSEG:(kc + 1) * LSEG],
                        xseg[kc * 128:(kc + 1) * 128, :], sQ[1])
        m_winx = None
        for kc in range(8):
            m_winx = b.dma("gpsimd", WINX[:, kc * NIX:(kc + 1) * NIX],
                           win_x[kc * 128:(kc + 1) * 128, :], sG[0])
        m_winz = None
        for kc in range(8):
            m_winz = b.dma("gpsimd", WINZ[:, kc * DI:(kc + 1) * DI],
                           win_z[kc * 128:(kc + 1) * 128, :], sG[1])

        vop(lambda e: e.memset(EPS[:, :], 1e-5))
        vop(lambda e: e.memset(ONESR[:, :], 1.0))
        vop(lambda e: e.memset(BASE[:, :], 0.0))
        vop(lambda e: e.memset(HST[:, :], 0.0), deps=[m_cons])
        vop(lambda e: e.memset(ZIP[:, :], 0.0))
        vop(lambda e: e.tensor_scalar(out=DTBN[:, :], in0=DTBN[:, :], scalar1=-1.0,
                                      scalar2=None, op0=ALU.mult), deps=[m_cons])
        sop(lambda e: e.activation(out=ACOL[:, :], in_=ACOL[:, :], func=AF.Exp,
                                   bias=0.0, scale=1.0), deps=[m_cons])
        sop(lambda e: e.activation(out=CVB[:, :], in_=CV[:, :], func=AF.Silu,
                                   bias=0.0, scale=1.0), deps=[m_cons])
        m_hst0 = b.mark("vector", sV)
        m_silu = b.mark("scalar", sS)
        for i in range(16):
            hst_ok[i] = m_hst0

        m_mm_hist = []
        m_modw = None
        for n in range(12):
            buf = WMBUF[n % 2]
            deps = [m_silu] if n < 2 else [m_mm_hist[n - 2]]
            d_wm = b.dma("gpsimd", buf.rearrange("p (k m) -> p k m", k=8),
                         wmod_d[:, n * 512:(n + 1) * 512]
                         .rearrange("(k p) m -> p k m", p=128), sG, deps=deps)
            for kc in range(8):
                top(lambda e, buf=buf, kc=kc: e.matmul(
                    PS[0:1, 0:512], CVB[:, kc:kc + 1],
                    buf[:, kc * 512:(kc + 1) * 512],
                    start=(kc == 0), stop=(kc == 7)),
                    deps=[d_wm, m_silu, pf["PS"]])
            m_mm = b.mark("tensor", sT)
            m_mm_hist.append(m_mm)
            vop(lambda e: e.tensor_copy(out=R1[:, :], in_=PS[0:1, 0:512]),
                deps=[m_mm, st["modw"]])
            if n in (4, 5, 10, 11):
                for hcol in (slice(0, 256), slice(256, 512)):
                    vop(lambda e, hc=hcol: e.tensor_single_scalar(
                        out=R3[:, 0:256], in_=R1[:, hc], scalar=0.0, op=ALU.is_ge))
                    vop(lambda e: e.tensor_scalar(out=R3[:, 0:256], in0=R3[:, 0:256],
                                                  scalar1=2.0, scalar2=-1.0,
                                                  op0=ALU.mult, op1=ALU.add))
                    vop(lambda e, hc=hcol: e.tensor_mul(out=R1[:, hc], in0=R1[:, hc],
                                                        in1=R3[:, 0:256]))
                    vop(lambda e, hc=hcol: e.tensor_scalar(
                        out=R1[:, hc], in0=R1[:, hc], scalar1=0.25, scalar2=None,
                        op0=ALU.max))
                    vop(lambda e, hc=hcol: e.tensor_mul(out=R1[:, hc], in0=R1[:, hc],
                                                        in1=R3[:, 0:256]))
            m_row = b.mark("vector", sV)
            pf["PS"] = m_row
            m_modw = b.dma("sync", scr_mod[0:1, n * 512:(n + 1) * 512], R1[:, :],
                           sQ[2], deps=[m_row])
            st["modw"] = m_modw
        d_adln = b.dma("sync", ADLN.rearrange("p (v k) -> p v k", v=6),
                       scr_mod.rearrange("o (v k p) -> (o p) v k", v=6, k=8, p=128),
                       sQ, deps=[m_modw], slow=True)
        vop(lambda e: e.tensor_scalar(out=ADL1[:, 0:8], in0=ADLN[:, 8:16], scalar1=1.0,
                                      scalar2=None, op0=ALU.add), deps=[d_adln])
        vop(lambda e: e.tensor_scalar(out=ADL1[:, 8:16], in0=ADLN[:, 32:40], scalar1=1.0,
                                      scalar2=None, op0=ALU.add))
        m_adln = b.mark("vector", sV)

        def sa_col(kc):
            return ADLN[:, 0 + kc:1 + kc]

        def ga_col(kc):
            return ADLN[:, 16 + kc:17 + kc]

        def sb_col(kc):
            return ADLN[:, 24 + kc:25 + kc]

        def gb_col(kc):
            return ADLN[:, 40 + kc:41 + kc]

        def ca1_col(kc):
            return ADL1[:, kc:kc + 1]

        def cb1_col(kc):
            return ADL1[:, 8 + kc:9 + kc]

        barrier()

        def emit_ln(x_tile, ncols, dst, c1col, s0col, parity, castbuf, deps,
                    x_wide=None):
            if x_wide is not None:
                vop(lambda e: e.tensor_copy(
                    out=castbuf[:, 0:8 * ncols].rearrange("p (k n) -> p k n", k=8),
                    in_=x_wide), deps=list(deps))
            else:
                for kc in range(8):
                    vop(lambda e, kc=kc: e.tensor_copy(
                        out=castbuf[:, kc * ncols:(kc + 1) * ncols], in_=x_tile(kc)),
                        deps=list(deps))
            m_cast = b.mark("vector", sV)
            for kc in range(8):
                if CUT & 256:
                    break
                top(lambda e, kc=kc: e.matmul(PS[0:1, 0:ncols], ONEB[:, :],
                                              castbuf[:, kc * ncols:(kc + 1) * ncols],
                                              start=(kc == 0), stop=(kc == 7)),
                    deps=([m_cast, pf["PS"], pf["PY"]] if kc == 0 else ()))
            m_s1 = b.mark("tensor", sT)
            sop(lambda e: e.activation(out=castbuf[:, 0:8 * ncols],
                                       in_=castbuf[:, 0:8 * ncols],
                                       func=AF.Square, bias=0.0, scale=1.0), deps=[m_s1])
            m_sq = b.mark("scalar", sS)
            for kc in range(8):
                if CUT & 256:
                    break
                top(lambda e, kc=kc: e.matmul(PY[0:1, 0:ncols], ONEB[:, :],
                                              castbuf[:, kc * ncols:(kc + 1) * ncols],
                                              start=(kc == 0), stop=(kc == 7)),
                    deps=([m_sq] if kc == 0 else ()))
            m_s2 = b.mark("tensor", sT)
            sop(lambda e: e.activation(out=R1[:, 0:ncols], in_=PS[0:1, 0:ncols],
                                       func=AF.Copy, bias=0.0, scale=1.0 / DM),
                deps=[m_s1, st["lnw"], st["modw"]])
            sop(lambda e: e.activation(out=R2[:, 0:ncols], in_=PY[0:1, 0:ncols],
                                       func=AF.Copy, bias=0.0, scale=1.0 / DM),
                deps=[m_s2])
            if ncols <= 256:
                sop(lambda e: e.activation(out=R3[:, 0:ncols], in_=R1[:, 0:ncols],
                                           func=AF.Square, bias=0.0, scale=1.0),
                    deps=[st["r2w"]])
            m_sc = b.mark("scalar", sS)
            pf["PS"] = m_sc
            pf["PY"] = m_sc
            if ncols <= 256:
                vop(lambda e: e.tensor_sub(out=R2[:, 0:ncols], in0=R2[:, 0:ncols],
                                           in1=R3[:, 0:ncols]), deps=[m_sc])
            else:
                vop(lambda e: e.tensor_mul(out=R3[:, 0:256], in0=R1[:, 0:256],
                                           in1=R1[:, 0:256]), deps=[m_sc])
                vop(lambda e: e.tensor_sub(out=R2[:, 0:256], in0=R2[:, 0:256],
                                           in1=R3[:, 0:256]))
                vop(lambda e: e.tensor_mul(out=R3[:, 0:256], in0=R1[:, 256:512],
                                           in1=R1[:, 256:512]))
                vop(lambda e: e.tensor_sub(out=R2[:, 256:512], in0=R2[:, 256:512],
                                           in1=R3[:, 0:256]))
            m_var = b.mark("vector", sV)
            sop(lambda e: e.activation(out=R2[:, 0:ncols], in_=R2[:, 0:ncols],
                                       func=AF.Ln, bias=EPS[0:1, :], scale=1.0),
                deps=[m_var])
            sop(lambda e: e.activation(out=R2[:, 0:ncols], in_=R2[:, 0:ncols],
                                       func=AF.Exp, bias=0.0, scale=-0.5))
            m_r = b.mark("scalar", sS)
            top(lambda e, ncols=ncols: e.matmul(PS[:, 0:ncols], ONESR[0:1, :],
                                                R1[0:1, 0:ncols],
                                                start=True, stop=True), deps=[m_r])
            top(lambda e, ncols=ncols: e.matmul(PY[:, 0:ncols], ONESR[0:1, :],
                                                R2[0:1, 0:ncols],
                                                start=True, stop=True))
            m_bc = b.mark("tensor", sT)
            st["lnw"] = m_bc
            for kc in range(8):
                vop(lambda e, kc=kc: e.tensor_sub(out=ROWREP[:, 0:ncols],
                                                  in0=x_tile(kc), in1=PS[:, 0:ncols]),
                    deps=[m_bc, m_adln, st["rowrep_dma"]])
                vop(lambda e, kc=kc: e.tensor_mul(out=ROWREP[:, 0:ncols],
                                                  in0=ROWREP[:, 0:ncols],
                                                  in1=PY[:, 0:ncols]))
                vop(lambda e, kc=kc: e.tensor_scalar(out=dst(kc), in0=ROWREP[:, 0:ncols],
                                                     scalar1=c1col(kc), scalar2=s0col(kc),
                                                     op0=ALU.mult, op1=ALU.add))
            mh = b.mark("vector", sV)
            st["rowrep"] = mh
            pf["PS"] = mh
            pf["PY"] = mh
            return mh

        def emit_chunk(h_src, h_dep, cglob, own, gci=0):
            par = cglob % 2
            nix_t = [(PA, "PA", 0, 512), (PB, "PB", 512, 512), (PC, "PC", 1024, 512),
                     (PD, "PD", 1536, 512), (PE2, None, 2048, 160)]
            ip_deps = [h_dep, m_winx, pf["PA"], pf["PB"], pf["PC"], pf["PD"],
                       pe2_free[0], pe2_free[1]]
            for kc in range(8 if not CUT & 1024 else 0):
                for ti, (pt, _nm, c0, w) in enumerate(nix_t):
                    top(lambda e, kc=kc, pt=pt, c0=c0, w=w: e.matmul(
                        pt[:, 0:w], h_src(kc),
                        WINX[:, kc * NIX + c0:kc * NIX + c0 + w],
                        start=(kc == 0), stop=(kc == 7)),
                        deps=(ip_deps if (kc == 0 and ti == 0) else ()))
            m_ip = b.mark("tensor", sT)
            for i, pt in enumerate((PA, PB, PC, PD)):
                vop(lambda e, i=i, pt=pt: e.tensor_copy(
                    out=XSb[:, i * 512:(i + 1) * 512], in_=pt[:, :]), deps=[m_ip])
            vop(lambda e: e.tensor_copy(out=SPARE[:, 0:128], in_=PE2[:, 0:128]))
            vop(lambda e: e.tensor_copy(out=ROWREP[:, 512:544], in_=PE2[:, 128:160]),
                deps=[st["rowrep_dma"]])
            m_ev = b.mark("vector", sV)
            pf["PA"] = pf["PB"] = pf["PC"] = pf["PD"] = m_ev
            pe2_free[0] = pe2_free[1] = m_ev

            top(lambda e: e.transpose(PS[0:32, 0:128], ROWREP[:, 512:544], IDF[:, :]),
                deps=[m_ev, pf["PS"]])
            m_t1 = b.mark("tensor", sT)
            vop(lambda e: e.tensor_copy(out=t32(0), in_=PS[0:32, 0:128]),
                deps=[m_t1, st["rowbounce"]])
            m_v1 = b.mark("vector", sV)
            pf["PS"] = m_v1
            sop(lambda e: e.activation(out=t32(1), in_=t32(0), func=AF.Sigmoid,
                                       bias=DTBN[:, :], scale=-1.0), deps=[m_v1])
            sop(lambda e: e.activation(out=t32(1), in_=t32(1), func=AF.Ln,
                                       bias=0.0, scale=1.0))
            m_s1 = b.mark("scalar", sS)
            vop(lambda e: e.tensor_scalar(out=t32(0), in0=t32(1), scalar1=-1.0,
                                          scalar2=None, op0=ALU.mult), deps=[m_s1])
            m_dt = b.mark("vector", sV)
            sop(lambda e: e.activation(out=t32(2), in_=t32(0), func=AF.Ln,
                                       bias=0.0, scale=1.0), deps=[m_dt])
            m_lndt = b.mark("scalar", sS)
            vop(lambda e: e.tensor_tensor_scan(out=t32(1), data0=t32(0), data1=t32(0),
                                               initial=0.0, op0=ALU.add, op1=ALU.bypass))
            vop(lambda e: e.tensor_copy(out=TCOL[:, 0:1], in_=t32(1)[:, 127:128]))
            vop(lambda e: e.tensor_scalar(out=t32(3), in0=t32(1), scalar1=BASE[:, :],
                                          scalar2=None, op0=ALU.add))
            vop(lambda e: e.tensor_add(out=BASE[:, :], in0=BASE[:, :], in1=TCOL[:, 0:1]))
            vop(lambda e: e.tensor_scalar(out=t32(1), in0=t32(1), scalar1=ACOL[:, :],
                                          scalar2=None, op0=ALU.mult))
            vop(lambda e: e.tensor_scalar(out=TCOL[:, 1:2], in0=TCOL[:, 0:1],
                                          scalar1=ACOL[:, :], scalar2=None,
                                          op0=ALU.mult))
            vop(lambda e: e.scalar_tensor_tensor(out=t32(4), in0=t32(1),
                                                 scalar=TCOL[:, 1:2], in1=t32(2),
                                                 op0=ALU.subtract, op1=ALU.add),
                deps=[m_lndt])
            if own:
                vop(lambda e: e.tensor_add(out=t32(2), in0=t32(1), in1=t32(2)))
            m_v3 = b.mark("vector", sV)
            sop(lambda e: e.activation(out=t32(4), in_=t32(4), func=AF.Exp,
                                       bias=0.0, scale=1.0), deps=[m_v3])
            sop(lambda e: e.activation(out=TDEC[:, :], in_=TCOL[:, 1:2], func=AF.Exp,
                                       bias=0.0, scale=-1.0), deps=[m_v3, st["decw"]])
            m_q = b.mark("scalar", sS)
            vop(lambda e: e.tensor_scalar(out=t32(0), in0=t32(1), scalar1=-1.0,
                                          scalar2=None, op0=ALU.mult))
            m_nasr = b.mark("vector", sV)
            d_r2m = None
            if own:
                b.dma("scalar",
                      scr_row[par:par + 1, 0:2048].rearrange("o (h t) -> (o h) t", h=16),
                      t32(0)[0:16, :], sB[0], deps=[m_nasr])
                d_r2m = b.dma("scalar",
                              scr_row[par:par + 1, 2048:4096].rearrange("o (h t) -> (o h) t", h=16),
                              t32(0)[16:32, :], sB[0])
                st["rowbounce"] = d_r2m
            d_dc = b.dma("scalar", scr_dec[par:par + 1, :].rearrange("o n -> n o"),
                         TDEC[:, :], sB[1], deps=[m_q])
            st["decw"] = d_dc
            dview = scr_dec[par:par + 1, :].rearrange("o (bp two) -> (o two) bp", two=2)
            b.dma("scalar", DECB[0:64, par * 16:(par + 1) * 16],
                  dview[0:1, :].broadcast_to([64, 16]),
                  sB[2], deps=[d_dc, st["sbv"][par]], slow=True)
            m_decb = b.dma("scalar", DECB[64:128, par * 16:(par + 1) * 16],
                           dview[1:2, :].broadcast_to([64, 16]), sB[2], slow=True)
            if own:
                sop(lambda e: e.activation(out=t32(0), in_=t32(1), func=AF.Exp,
                                           bias=0.0, scale=-1.0),
                    deps=[d_r2m, m_nasr])
            m_csc = b.mark("scalar", sS)
            top(lambda e: e.transpose(PS[0:128, 0:32], t32(3), IDF[0:32, 0:32]),
                deps=[m_v3, pf["PS"]])
            top(lambda e: e.transpose(PS[0:128, 64:96], t32(4), IDF[0:32, 0:32]),
                deps=[m_q])
            if own:
                top(lambda e: e.transpose(PS[0:128, 32:64], t32(2), IDF[0:32, 0:32]))
                top(lambda e: e.transpose(PS[0:128, 96:128], t32(0), IDF[0:32, 0:32]),
                    deps=[m_csc])
            m_t2 = b.mark("tensor", sT)
            vop(lambda e: e.tensor_copy(out=STM[:, :], in_=PS[:, 0:32]), deps=[m_t2])
            vop(lambda e: e.tensor_copy(out=QTM[:, :], in_=PS[:, 64:96]))
            if own:
                vop(lambda e: e.tensor_copy(out=JLTM[:, :], in_=PS[:, 32:64]))
                vop(lambda e: e.tensor_copy(out=CSCTM[:, :], in_=PS[:, 96:128]))
            else:
                vop(lambda e, c=cglob: e.tensor_scalar(
                    out=QTM[:, :], in0=QTM[:, :], scalar1=VALID[:, c:c + 1],
                    scalar2=None, op0=ALU.mult))
            m_tm = b.mark("vector", sV)
            pf["PS"] = m_tm

            ANG_A5 = ROWREP[:, 0:512]
            ANG_B5 = ROWREP[:, 512:1024]
            vop(lambda e: e.tensor_mul(
                out=ANG_A5.rearrange("p (h k) -> p h k", h=32),
                in0=STM[:, 0:32].rearrange("p (h o) -> p h o", o=1)
                .broadcast_to([128, 32, 16]),
                in1=IFRQ[:, 0:512].rearrange("p (h k) -> p h k", h=32)),
                deps=[m_tm, st.get("trig")])
            m_trig = None
            for trig, dstT in ((0, SINB), (1, COSB)):
                if trig == 1:
                    vop(lambda e: e.tensor_scalar(out=ANG_A5, in0=ANG_A5,
                                                  scalar1=0.25, scalar2=None,
                                                  op0=ALU.add), deps=[m_trig])
                vop(lambda e: e.tensor_copy(out=KI[:, 0:512], in_=ANG_A5))
                vop(lambda e: e.tensor_copy(out=ANG_B5, in_=KI[:, 0:512]))
                vop(lambda e: e.tensor_sub(out=ANG_B5, in0=ANG_A5, in1=ANG_B5))
                vop(lambda e: e.tensor_scalar(out=ANG_B5, in0=ANG_B5,
                                              scalar1=TWO_PI, scalar2=None,
                                              op0=ALU.mult))
                vop(lambda e: e.tensor_single_scalar(out=MRB[:, 0:512], in_=ANG_B5,
                                                     scalar=PI_CL, op=ALU.is_gt))
                vop(lambda e: e.tensor_single_scalar(out=MRB[:, 512:1024], in_=ANG_B5,
                                                     scalar=-PI_CL, op=ALU.is_lt))
                vop(lambda e: e.tensor_sub(out=MRB[:, 0:512], in0=MRB[:, 0:512],
                                           in1=MRB[:, 512:1024]))
                vop(lambda e: e.scalar_tensor_tensor(out=ANG_B5, in0=MRB[:, 0:512],
                                                     scalar=-TWO_PI, in1=ANG_B5,
                                                     op0=ALU.mult, op1=ALU.add))
                vop(lambda e: e.tensor_scalar(out=ANG_B5, in0=ANG_B5,
                                              scalar1=PI_CL, scalar2=-PI_CL,
                                              op0=ALU.min, op1=ALU.max))
                m_red = b.mark("vector", sV)
                sop(lambda e, dstT=dstT, sc=(-1.0 if trig == 0 else 1.0):
                    e.activation(out=dstT[:, 0:512], in_=ANG_B5,
                                 func=AF.Sin, bias=0.0, scale=sc), deps=[m_red])
                m_trig = b.mark("scalar", sS)
            st["trig"] = m_trig

            Bt = SPARE[:, 0:64]
            Ct = SPARE[:, 64:128]
            bview = BT_TM.rearrange("p (h n) -> p h n", h=32)
            cview = CB_TM.rearrange("p (h n) -> p h n", h=32)
            sview = SINB.rearrange("p (h k) -> p h k", h=32)
            cosv = COSB.rearrange("p (h k) -> p h k", h=32)
            wview = WTs.rearrange("p (h k) -> p h k", h=32)

            def b_bc(c0, c1):
                return (Bt[:, c0:c1].rearrange("p (o n) -> p o n", o=1)
                        .broadcast_to([128, 32, c1 - c0]))

            def c_bc(c0, c1):
                return (Ct[:, c0:c1].rearrange("p (o n) -> p o n", o=1)
                        .broadcast_to([128, 32, c1 - c0]))

            vop(lambda e: e.tensor_mul(out=wview, in0=cosv, in1=b_bc(0, 16)),
                deps=[m_trig, m_ev, st["ycons"], st.get("tap_wt")])
            vop(lambda e: e.tensor_mul(out=bview[:, :, 0:16], in0=sview,
                                       in1=b_bc(16, 32)))
            vop(lambda e: e.tensor_sub(out=bview[:, :, 0:16], in0=wview,
                                       in1=bview[:, :, 0:16]))
            vop(lambda e: e.tensor_mul(out=wview, in0=sview, in1=b_bc(0, 16)))
            vop(lambda e: e.tensor_mul(out=bview[:, :, 16:32], in0=cosv,
                                       in1=b_bc(16, 32)))
            vop(lambda e: e.tensor_add(out=bview[:, :, 16:32], in0=wview,
                                       in1=bview[:, :, 16:32]))
            vop(lambda e: e.tensor_copy(out=bview[:, :, 32:64], in_=b_bc(32, 64)))
            m_brope = b.mark("vector", sV)
            if own:
                vop(lambda e: e.tensor_mul(out=wview, in0=cosv, in1=c_bc(0, 16)))
                vop(lambda e: e.tensor_mul(out=cview[:, :, 0:16], in0=sview,
                                           in1=c_bc(16, 32)))
                vop(lambda e: e.tensor_sub(out=cview[:, :, 0:16], in0=wview,
                                           in1=cview[:, :, 0:16]))
                vop(lambda e: e.tensor_mul(out=wview, in0=sview, in1=c_bc(0, 16)))
                vop(lambda e: e.tensor_mul(out=cview[:, :, 16:32], in0=cosv,
                                           in1=c_bc(16, 32)))
                vop(lambda e: e.tensor_add(out=cview[:, :, 16:32], in0=wview,
                                           in1=cview[:, :, 16:32]))
                vop(lambda e: e.tensor_copy(out=cview[:, :, 32:64], in_=c_bc(32, 64)))
            m_rope = b.mark("vector", sV)

            srcs = ([(BT_TM, BT_FM), (CB_TM, CB_FM)]
                    if (own and not CUT & 512) else ([(BT_TM, BT_FM)] if own else []))
            for (srcT, dstF) in srcs:
                for half in range(2):
                    for j in range(8):
                        jj = half * 8 + j
                        top(lambda e, jj=jj, j=j, srcT=srcT: e.transpose(
                            PT[:, j * 128:(j + 1) * 128],
                            srcT[:, jj * 128:(jj + 1) * 128], IDB[:, :]),
                            deps=[m_rope, pf["PT"]])
                    m_pt = b.mark("tensor", sT)
                    vop(lambda e, half=half, dstF=dstF: e.tensor_copy(
                        out=dstF[:, half * 1024:(half + 1) * 1024],
                        in_=PT[:, 0:1024]), deps=[m_pt])
                    pf["PT"] = b.mark("vector", sV)
            m_fm = b.mark("vector", sV)
            if debug and own and cglob == NPC:
                b.dma("sync", taps["sin_dbg"][:, :], SINB[:, :], sO, deps=[m_trig])
                b.dma("sync", taps["cos_dbg"][:, :], COSB[:, :], sO, deps=[m_trig])
                b.dma("sync", taps["stm_dbg"][:, :], STM[:, :], sO, deps=[m_tm])
                b.dma("sync", taps["spr_dbg"][:, :], SPARE[:, 0:128], sO, deps=[m_ev])
                b.dma("sync", taps["btm_dbg"][:, :], BT_TM[:, :], sO, deps=[m_rope])
                b.dma("sync", taps["ctm_dbg"][:, :], CB_TM[:, :], sO, deps=[m_rope])
                b.dma("sync", taps["bfm_dbg"][:, :], BT_FM[:, :], sO, deps=[m_fm])
                st["tap_fm"] = b.dma("sync", taps["cfm_dbg"][:, :], CB_FM[:, :], sO,
                                     deps=[m_fm])
                for _e in ("vector", "scalar", "tensor"):
                    b.wait(_e, (sO, sO.n))

            for qh in range(4):
                hset = list(range(qh * 8, qh * 8 + 8))
                if own:
                    d_rr = b.dma("scalar", ROWREP[:, 0:1024],
                                 scr_row[par:par + 1, qh * 1024:(qh + 1) * 1024]
                                 .broadcast_to([128, 1024]), sB,
                                 deps=[d_r2m, st["rowrep"], st["trig"]])
                    st["rowrep_dma"] = d_rr
                    for i, h in enumerate(hset):
                        vop(lambda e, i=i, h=h: e.scalar_tensor_tensor(
                            out=LBUF[:, i * 128:(i + 1) * 128],
                            in0=ROWREP[:, i * 128:(i + 1) * 128],
                            scalar=JLTM[:, h:h + 1], in1=CAUS[:, :],
                            op0=ALU.add, op1=ALU.min),
                            deps=[d_rr, m_tm, m_cons])
                    st["rowrep"] = b.mark("vector", sV)
                    sop(lambda e: e.activation(out=LBUF[:, 0:1024], in_=LBUF[:, 0:1024],
                                               func=AF.Exp, bias=0.0, scale=1.0),
                        deps=[st["rowrep"]])
                    m_msk = b.mark("scalar", sS)
                    if debug and cglob == NPC:
                        b.dma("sync", taps["l_dbg"][:, qh * 1024:(qh + 1) * 1024],
                              LBUF[:, 0:1024], sO, deps=[m_msk])
                    for i in (0, 2, 4, 6, None, 1, 3, 5, 7):
                        if CUT & 1:
                            break
                        if i is None:
                            top(lambda e: e.transpose(PT[:, 0:128], IDB[:, :],
                                                      IDB[:, :]), deps=[pf["PT"]])
                            continue
                        h = hset[i]
                        pg = PA if i < 4 else PB
                        ql = h % 2
                        top(lambda e, h=h, pg=pg, i=i, ql=ql: e.matmul(
                            pg[:, (i % 4) * 128:(i % 4) * 128 + 128],
                            BT_FM[ql * 64:ql * 64 + 64,
                                  (h // 2) * 128:(h // 2) * 128 + 128],
                            CB_FM[ql * 64:ql * 64 + 64,
                                  (h // 2) * 128:(h // 2) * 128 + 128],
                            start=True, stop=True),
                            deps=[m_fm, pf["PA" if i < 4 else "PB"]])
                    m_g = b.mark("tensor", sT)
                    vop(lambda e, qh=qh: e.tensor_mul(
                        out=CB_TM[:, qh * 512:(qh + 1) * 512]
                        .rearrange("p (h n) -> p h n", h=8),
                        in0=CB_TM[:, qh * 512:(qh + 1) * 512]
                        .rearrange("p (h n) -> p h n", h=8),
                        in1=CSCTM[:, qh * 8:(qh + 1) * 8]
                        .rearrange("p (h o) -> p h o", o=1)
                        .broadcast_to([128, 8, 64])),
                        deps=[m_g, m_tm])
                    m_cbs = b.mark("vector", sV)
                    for lb in range(4):
                        if CUT & 2:
                            break
                        bp = qh * 4 + lb
                        top(lambda e, bp=bp, lb=lb: e.transpose(
                            PT[:, lb * 128:(lb + 1) * 128],
                            CB_TM[:, bp * 128:bp * 128 + 128], IDB[:, :]),
                            deps=[m_cbs, pf["PT"]])
                    m_cbt = b.mark("tensor", sT)
                    vop(lambda e, qh=qh: e.tensor_copy(
                        out=CB_FM[:, qh * 512:(qh + 1) * 512], in_=PT[:, 0:512]),
                        deps=[m_cbt, st.get("tap_fm")])
                    pf["PT"] = b.mark("vector", sV)
                    m_cbf = pf["PT"]
                    for lb in range(4):
                        bp = qh * 4 + lb
                        vop(lambda e, lb=lb, bp=bp: e.tensor_copy(
                            out=SPARE[:, 128 + lb * 64:128 + lb * 64 + 64],
                            in_=HST[:, bp * 64:bp * 64 + 64]),
                            deps=[hst_ok[bp]])
                    m_hstb = b.mark("vector", sV)
                    vop(lambda e, qh=qh: e.tensor_mul(
                        out=XQA[:, 0:512].rearrange("p (h n) -> p h n", h=8),
                        in0=XSb[:, qh * 512:(qh + 1) * 512]
                        .rearrange("p (h n) -> p h n", h=8),
                        in1=DCOL[:, qh * 8:(qh + 1) * 8]
                        .rearrange("p (h o) -> p h o", o=1)
                        .broadcast_to([128, 8, 64])),
                        deps=[m_cons, m_ev, m_rope])
                    vop(lambda e, qh=qh: e.tensor_mul(
                        out=XQB[:, 0:512].rearrange("p (h n) -> p h n", h=8),
                        in0=XSb[:, qh * 512:(qh + 1) * 512]
                        .rearrange("p (h n) -> p h n", h=8),
                        in1=QTM[:, qh * 8:(qh + 1) * 8]
                        .rearrange("p (h o) -> p h o", o=1)
                        .broadcast_to([128, 8, 64])),
                        deps=[m_tm])
                    m_xq = b.mark("vector", sV)
                    m_ycons = st["ycons"]
                    for h4 in range(2):
                        pg = PA if h4 == 0 else PB
                        vop(lambda e, pg=pg, h4=h4: e.tensor_mul(
                            out=WTs[:, 0:512], in0=pg[:, :],
                            in1=LBUF[:, h4 * 512:h4 * 512 + 512]),
                            deps=[m_g, m_msk, m_ycons, st.get("tap_wt")])
                        m_w = b.mark("vector", sV)
                        if debug and cglob == NPC:
                            st["tap_wt"] = b.dma(
                                "sync", taps["wt_dbg"]
                                [:, qh * 1024 + h4 * 512:qh * 1024 + h4 * 512 + 512],
                                WTs[:, 0:512], sO, deps=[m_w])
                        pf["PA" if h4 == 0 else "PB"] = m_w
                        for i4 in range(4):
                            i = h4 * 4 + i4
                            h = qh * 8 + i
                            bp = h // 2
                            lb = i // 2
                            ql = h % 2
                            yc = lb * 128
                            if not CUT & 4:
                                top(lambda e, h=h, i4=i4, ql=ql, yc=yc: e.matmul(
                                    PY[ql * 64:ql * 64 + 64, yc:yc + 128],
                                    XSb[:, h * 64:(h + 1) * 64],
                                    ZIP[:, 0:128] if NO_INTRA else
                                    WTs[:, i4 * 128:(i4 + 1) * 128],
                                    start=True, stop=False),
                                    deps=[m_w, pf["PY"]])
                                top(lambda e, i=i, ql=ql, yc=yc: e.matmul(
                                    PY[ql * 64:ql * 64 + 64, yc:yc + 128],
                                    XQA[:, i * 64:(i + 1) * 64],
                                    ZIP[:, 0:128] if NO_SKIP else IDB[:, :],
                                    start=False, stop=False),
                                    deps=[m_xq])
                            lbq = (bp % 4)
                            if not CUT & 8:
                                top(lambda e, ql=ql, yc=yc, bp=bp, lbq=lbq: e.matmul(
                                    PY[ql * 64:ql * 64 + 64, yc:yc + 128],
                                    SPARE[ql * 64:ql * 64 + 64,
                                          128 + lbq * 64:128 + lbq * 64 + 64],
                                    ZIP[ql * 64:ql * 64 + 64, 0:128] if NO_INTER
                                    else CB_FM[ql * 64:ql * 64 + 64,
                                               bp * 128:bp * 128 + 128],
                                    start=False, stop=True),
                                    deps=[m_fm, m_hstb, m_cbf])
                        m_ycons = b.mark("tensor", sT)
                        st["ycons"] = m_ycons
                    m_y = m_ycons
                    for i, h in enumerate(hset):
                        if CUT & 16:
                            break
                        lb = i // 2
                        ql = h % 2
                        cc = (qh % 2) * 256 + lb * 64
                        top(lambda e, h=h, i=i, ql=ql, cc=cc: e.matmul(
                            PE2[ql * 64:ql * 64 + 64, cc:cc + 64],
                            BT_TM[:, h * 64:(h + 1) * 64],
                            XQB[:, i * 64:(i + 1) * 64],
                            start=True, stop=True),
                            deps=[m_xq, m_brope, pe2_free[qh % 2], st["laststt"]])
                    m_st = b.mark("tensor", sT)
                    p0 = qh * 4
                    vop(lambda e, p0=p0, gci=gci: e.tensor_mul(
                        out=YGb.rearrange("p (m t) -> p m t", m=16)
                        [:, p0:p0 + 4, gci * 128:gci * 128 + 128],
                        in0=PY[:, 0:512].rearrange("p (m t) -> p m t", m=4),
                        in1=SILUZ.rearrange("p (m t) -> p m t", m=16)
                        [:, p0:p0 + 4, gci * 128:gci * 128 + 128]),
                        deps=[m_y, st["siluz"], st["tap_yg"]])
                    pf["PY"] = b.mark("vector", sV)
                    for lb in range(4):
                        bp = qh * 4 + lb
                        cc = (qh % 2) * 256 + lb * 64
                        vop(lambda e, bp=bp, cc=cc, par=par: e.scalar_tensor_tensor(
                            out=HST[:, bp * 64:bp * 64 + 64],
                            in0=HST[:, bp * 64:bp * 64 + 64],
                            scalar=DECB[:, par * 16 + bp:par * 16 + bp + 1],
                            in1=PE2[:, cc:cc + 64],
                            op0=ALU.mult, op1=ALU.add),
                            deps=[m_st, m_decb])
                        hst_ok[bp] = b.mark("vector", sV)
                    pe2_free[qh % 2] = hst_ok[qh * 4 + 3]
                    st["laststt"] = hst_ok[qh * 4 + 3]
                else:
                    vop(lambda e, qh=qh: e.tensor_mul(
                        out=XQB[:, 0:512].rearrange("p (h n) -> p h n", h=8),
                        in0=XSb[:, qh * 512:(qh + 1) * 512]
                        .rearrange("p (h n) -> p h n", h=8),
                        in1=QTM[:, qh * 8:(qh + 1) * 8]
                        .rearrange("p (h o) -> p h o", o=1)
                        .broadcast_to([128, 8, 64])),
                        deps=[m_tm, m_ev, m_brope])
                    m_xq = b.mark("vector", sV)
                    for i, h in enumerate(hset):
                        lb = i // 2
                        ql = h % 2
                        cc = (qh % 2) * 256 + lb * 64
                        top(lambda e, h=h, i=i, ql=ql, cc=cc: e.matmul(
                            PE2[ql * 64:ql * 64 + 64, cc:cc + 64],
                            BT_TM[:, h * 64:(h + 1) * 64],
                            XQB[:, i * 64:(i + 1) * 64],
                            start=True, stop=True),
                            deps=[m_xq, m_brope, pe2_free[qh % 2], st["laststt"]])
                    m_st = b.mark("tensor", sT)
                    for lb in range(4):
                        bp = qh * 4 + lb
                        cc = (qh % 2) * 256 + lb * 64
                        vop(lambda e, bp=bp, cc=cc, par=par: e.scalar_tensor_tensor(
                            out=HST[:, bp * 64:bp * 64 + 64],
                            in0=HST[:, bp * 64:bp * 64 + 64],
                            scalar=DECB[:, par * 16 + bp:par * 16 + bp + 1],
                            in1=PE2[:, cc:cc + 64],
                            op0=ALU.mult, op1=ALU.add),
                            deps=[m_st, m_decb])
                        hst_ok[bp] = b.mark("vector", sV)
                    pe2_free[qh % 2] = hst_ok[qh * 4 + 3]
                    st["laststt"] = hst_ok[qh * 4 + 3]

            if own:
                vop(lambda e, gci=gci: e.tensor_copy(
                    out=SQW[:, 0:2048],
                    in_=YGb.rearrange("p (m t) -> p m t", m=16)
                    [:, :, gci * 128:gci * 128 + 128]), deps=[pf["PY"], st["wo"]])
                m_c = b.mark("vector", sV)
                sop(lambda e: e.activation(out=SQW[:, 0:2048], in_=SQW[:, 0:2048],
                                           func=AF.Square, bias=0.0, scale=1.0),
                    deps=[m_c])
                m_sq2 = b.mark("scalar", sS)
                for mc in range(16):
                    if CUT & 32:
                        break
                    top(lambda e, mc=mc: e.matmul(
                        PS[0:1, 0:128], ONEB[:, :],
                        SQW[:, mc * 128:(mc + 1) * 128],
                        start=(mc == 0), stop=(mc == 15)),
                        deps=([m_sq2, pf["PS"]] if mc == 0 else ()))
                m_rmsmm = b.mark("tensor", sT)
                vop(lambda e, gci=gci: e.tensor_copy(
                    out=RMSA[0:1, gci * 128:gci * 128 + 128], in_=PS[0:1, 0:128]),
                    deps=[m_rmsmm, st["r2w"]])
                pf["PS"] = b.mark("vector", sV)
                return pf["PS"]
            return b.mark("vector", sV)

        def cast_dma(dst, src, sem=None):
            sem = sem or sC
            sem.inc(16)
            b.ops["gpsimd"].append(lambda e, d=dst, sr=src, sm=sem: e.dma_start(
                out=d, in_=sr).then_inc(sm.h, 16))

        for wh in range(2):
            cast_dma(woutb_d[wh * 1024:(wh + 1) * 1024, :],
                     wout_d[wh * 1024:(wh + 1) * 1024, :], sCW)
        sc_wout = sCW.n
        cast_jobs = []
        for ei in range(NE):
            cast_jobs.append((w13b_d[ei], w13_d[ei]))
        for ei in range(NE):
            cast_jobs.append((w2b_d[ei], w2e_d[ei]))

        XSB2 = BFA[:, 2048:4096]
        BT2 = BFA[:, 4096:6144]
        sa_out = {}

        def pre_sa(c):
            par = c % 2
            XSbP = XSb if par == 0 else XSB2
            BTP = BT_TM if par == 0 else BT2
            QTMP = QTM if par == 0 else QTM2
            if cast_jobs:
                cast_dma(*cast_jobs.pop(0))
            d_xp = b.dma("gpsimd", XPB.rearrange("p (k t) -> p k t", k=8),
                         xpre[:, c * Q:(c + 1) * Q]
                         .rearrange("(k p) t -> p k t", p=128), sG,
                         deps=[st["hpre"], m_winz])
            m_h = emit_ln(lambda kc: XPB[:, kc * 128:(kc + 1) * 128], 128,
                          lambda kc: HG[:, kc * 128:(kc + 1) * 128],
                          ca1_col, sa_col, par, SQW,
                          deps=[d_xp, st["ipro"]],
                          x_wide=XPB.rearrange("p (k n) -> p k n", k=8))
            st["hpre"] = m_h
            yield
            nix_t = [(PA, 0, 512), (PB, 512, 512), (PC, 1024, 512),
                     (PD, 1536, 512), (PY, 2048, 160)]
            ip_deps = [m_h, m_winx, pf["PA"], pf["PB"], pf["PC"], pf["PD"],
                       pf["PY"]]
            for kc in range(8):
                for ti, (pt, c0, w) in enumerate(nix_t):
                    top(lambda e, kc=kc, pt=pt, c0=c0, w=w: e.matmul(
                        pt[:, 0:w],
                        HG[:, kc * 128:(kc + 1) * 128],
                        WINX[:, kc * NIX + c0:kc * NIX + c0 + w],
                        start=(kc == 0), stop=(kc == 7)),
                        deps=(ip_deps if (kc == 0 and ti == 0) else ()))
            m_ip = b.mark("tensor", sT)
            st["ipro"] = m_ip
            for i, pt in enumerate((PA, PB, PC, PD)):
                vop(lambda e, i=i, pt=pt, XSbP=XSbP: e.tensor_copy(
                    out=XSbP[:, i * 512:(i + 1) * 512], in_=pt[:, :]), deps=[m_ip])
            vop(lambda e: e.tensor_copy(out=SPARE[:, 0:128], in_=PY[:, 0:128]))
            vop(lambda e: e.tensor_copy(out=ROWREP[:, 512:544], in_=PY[:, 128:160]))
            m_ev = b.mark("vector", sV)
            pf["PA"] = pf["PB"] = pf["PC"] = pf["PD"] = m_ev
            pf["PY"] = m_ev
            top(lambda e: e.transpose(PS[0:32, 0:128], ROWREP[:, 512:544], IDF[:, :]),
                deps=[m_ev, pf["PS"]])
            m_t1 = b.mark("tensor", sT)
            vop(lambda e: e.tensor_copy(out=t32(0), in_=PS[0:32, 0:128]), deps=[m_t1])
            m_v1 = b.mark("vector", sV)
            pf["PS"] = m_v1
            sop(lambda e: e.activation(out=t32(1), in_=t32(0), func=AF.Sigmoid,
                                       bias=DTBN[:, :], scale=-1.0), deps=[m_v1])
            sop(lambda e: e.activation(out=t32(1), in_=t32(1), func=AF.Ln,
                                       bias=0.0, scale=1.0))
            m_s1 = b.mark("scalar", sS)
            vop(lambda e: e.tensor_scalar(out=t32(0), in0=t32(1), scalar1=-1.0,
                                          scalar2=None, op0=ALU.mult), deps=[m_s1])
            m_dt = b.mark("vector", sV)
            sop(lambda e: e.activation(out=t32(2), in_=t32(0), func=AF.Ln,
                                       bias=0.0, scale=1.0), deps=[m_dt])
            m_lndt = b.mark("scalar", sS)
            vop(lambda e: e.tensor_tensor_scan(out=t32(1), data0=t32(0), data1=t32(0),
                                               initial=0.0, op0=ALU.add,
                                               op1=ALU.bypass))
            vop(lambda e: e.tensor_copy(out=TCOL[:, 0:1], in_=t32(1)[:, 127:128]))
            vop(lambda e: e.tensor_scalar(out=t32(3), in0=t32(1), scalar1=BASE[:, :],
                                          scalar2=None, op0=ALU.add))
            vop(lambda e: e.tensor_add(out=BASE[:, :], in0=BASE[:, :],
                                       in1=TCOL[:, 0:1]))
            vop(lambda e: e.tensor_scalar(out=t32(1), in0=t32(1), scalar1=ACOL[:, :],
                                          scalar2=None, op0=ALU.mult))
            vop(lambda e: e.tensor_scalar(out=TCOL[:, 1:2], in0=TCOL[:, 0:1],
                                          scalar1=ACOL[:, :], scalar2=None,
                                          op0=ALU.mult))
            vop(lambda e: e.scalar_tensor_tensor(out=t32(4), in0=t32(1),
                                                 scalar=TCOL[:, 1:2], in1=t32(2),
                                                 op0=ALU.subtract, op1=ALU.add),
                deps=[m_lndt])
            m_v3 = b.mark("vector", sV)
            sop(lambda e: e.activation(out=t32(4), in_=t32(4), func=AF.Exp,
                                       bias=0.0, scale=1.0), deps=[m_v3])
            sop(lambda e: e.activation(out=TDEC[:, :], in_=TCOL[:, 1:2], func=AF.Exp,
                                       bias=0.0, scale=-1.0), deps=[m_v3, st["decw"]])
            m_q = b.mark("scalar", sS)
            d_dc = b.dma("scalar", scr_dec[par:par + 1, :].rearrange("o n -> n o"),
                         TDEC[:, :], sB[1], deps=[m_q])
            st["decw"] = d_dc
            dview = scr_dec[par:par + 1, :].rearrange("o (bp two) -> (o two) bp",
                                                      two=2)
            b.dma("scalar", DECB[0:64, par * 16:(par + 1) * 16],
                  dview[0:1, :].broadcast_to([64, 16]),
                  sB[2], deps=[d_dc, st["sbv"][par]], slow=True)
            m_decb = b.dma("scalar", DECB[64:128, par * 16:(par + 1) * 16],
                           dview[1:2, :].broadcast_to([64, 16]), sB[2], slow=True)
            yield
            top(lambda e: e.transpose(PS[0:128, 0:32], t32(3), IDF[0:32, 0:32]),
                deps=[m_v3, pf["PS"]])
            top(lambda e: e.transpose(PS[0:128, 64:96], t32(4), IDF[0:32, 0:32]),
                deps=[m_q])
            m_t2 = b.mark("tensor", sT)
            vop(lambda e: e.tensor_copy(out=STM[:, :], in_=PS[:, 0:32]),
                deps=[m_t2])
            vop(lambda e, c=c, QTMP=QTMP: e.tensor_scalar(
                out=QTMP[:, :], in0=PS[:, 64:96], scalar1=VALID[:, c:c + 1],
                scalar2=None, op0=ALU.mult))
            m_tm = b.mark("vector", sV)
            pf["PS"] = m_tm
            ANG_A5 = ROWREP[:, 0:512]
            ANG_B5 = ROWREP[:, 512:1024]
            vop(lambda e: e.tensor_mul(
                out=ANG_A5.rearrange("p (h k) -> p h k", h=32),
                in0=STM[:, 0:32].rearrange("p (h o) -> p h o", o=1)
                .broadcast_to([128, 32, 16]),
                in1=IFRQ[:, 0:512].rearrange("p (h k) -> p h k", h=32)),
                deps=[m_tm, st.get("trig")])
            m_trig = None
            for trig, dstT in ((0, SINB), (1, COSB)):
                if trig == 1:
                    vop(lambda e: e.tensor_scalar(out=ANG_A5, in0=ANG_A5,
                                                  scalar1=0.25, scalar2=None,
                                                  op0=ALU.add), deps=[m_trig])
                vop(lambda e: e.tensor_copy(out=KI[:, 0:512], in_=ANG_A5))
                vop(lambda e: e.tensor_copy(out=ANG_B5, in_=KI[:, 0:512]))
                vop(lambda e: e.tensor_sub(out=ANG_B5, in0=ANG_A5, in1=ANG_B5))
                vop(lambda e: e.tensor_scalar(out=ANG_B5, in0=ANG_B5,
                                              scalar1=TWO_PI, scalar2=None,
                                              op0=ALU.mult))
                vop(lambda e: e.tensor_single_scalar(out=MRB[:, 0:512], in_=ANG_B5,
                                                     scalar=PI_CL, op=ALU.is_gt))
                vop(lambda e: e.tensor_single_scalar(out=MRB[:, 512:1024],
                                                     in_=ANG_B5,
                                                     scalar=-PI_CL, op=ALU.is_lt))
                vop(lambda e: e.tensor_sub(out=MRB[:, 0:512], in0=MRB[:, 0:512],
                                           in1=MRB[:, 512:1024]))
                vop(lambda e: e.scalar_tensor_tensor(out=ANG_B5, in0=MRB[:, 0:512],
                                                     scalar=-TWO_PI, in1=ANG_B5,
                                                     op0=ALU.mult, op1=ALU.add))
                vop(lambda e: e.tensor_scalar(out=ANG_B5, in0=ANG_B5,
                                              scalar1=PI_CL, scalar2=-PI_CL,
                                              op0=ALU.min, op1=ALU.max))
                m_red = b.mark("vector", sV)
                sop(lambda e, dstT=dstT, sc=(-1.0 if trig == 0 else 1.0):
                    e.activation(out=dstT[:, 0:512], in_=ANG_B5,
                                 func=AF.Sin, bias=0.0, scale=sc), deps=[m_red])
                m_trig = b.mark("scalar", sS)
            st["trig"] = m_trig
            yield
            Bt = SPARE[:, 0:64]
            bview = BTP.rearrange("p (h n) -> p h n", h=32)
            sview = SINB.rearrange("p (h k) -> p h k", h=32)
            cosv = COSB.rearrange("p (h k) -> p h k", h=32)
            wview = WTs.rearrange("p (h k) -> p h k", h=32)

            def b_bc(c0, c1):
                return (Bt[:, c0:c1].rearrange("p (o n) -> p o n", o=1)
                        .broadcast_to([128, 32, c1 - c0]))

            vop(lambda e: e.tensor_mul(out=wview, in0=cosv, in1=b_bc(0, 16)),
                deps=[m_trig, m_ev, st["sbm"][par]])
            vop(lambda e: e.tensor_mul(out=bview[:, :, 0:16], in0=sview,
                                       in1=b_bc(16, 32)))
            vop(lambda e: e.tensor_sub(out=bview[:, :, 0:16], in0=wview,
                                       in1=bview[:, :, 0:16]))
            vop(lambda e: e.tensor_mul(out=wview, in0=sview, in1=b_bc(0, 16)))
            vop(lambda e: e.tensor_mul(out=bview[:, :, 16:32], in0=cosv,
                                       in1=b_bc(16, 32)))
            vop(lambda e: e.tensor_add(out=bview[:, :, 16:32], in0=wview,
                                       in1=bview[:, :, 16:32]))
            vop(lambda e: e.tensor_copy(out=bview[:, :, 32:64], in_=b_bc(32, 64)))
            m_brope = b.mark("vector", sV)
            sa_out[c] = (m_brope, m_decb)
            yield

        def pre_sb(c, gen):
            par = c % 2
            XSbP = XSb if par == 0 else XSB2
            BTP = BT_TM if par == 0 else BT2
            QTMP = QTM if par == 0 else QTM2
            m_brope, m_decb = sa_out.pop(c)
            m_st = None
            for qh in range(4):
                vop(lambda e, qh=qh, XSbP=XSbP, QTMP=QTMP: e.tensor_mul(
                    out=XQB[:, 0:512].rearrange("p (h n) -> p h n", h=8),
                    in0=XSbP[:, qh * 512:(qh + 1) * 512]
                    .rearrange("p (h n) -> p h n", h=8),
                    in1=QTMP[:, qh * 8:(qh + 1) * 8]
                    .rearrange("p (h o) -> p h o", o=1)
                    .broadcast_to([128, 8, 64])),
                    deps=[m_brope])
                m_xq = b.mark("vector", sV)
                for i in range(8):
                    h = qh * 8 + i
                    lb = i // 2
                    ql = h % 2
                    cc = (qh % 2) * 256 + lb * 64
                    top(lambda e, h=h, i=i, ql=ql, cc=cc, BTP=BTP: e.matmul(
                        PE2[ql * 64:ql * 64 + 64, cc:cc + 64],
                        BTP[:, h * 64:(h + 1) * 64],
                        XQB[:, i * 64:(i + 1) * 64],
                        start=True, stop=True),
                        deps=[m_xq, m_brope, pe2_free[qh % 2]])
                m_st = b.mark("tensor", sT)
                if gen is not None:
                    next(gen, None)
                for lb in range(4):
                    bp = qh * 4 + lb
                    cc = (qh % 2) * 256 + lb * 64
                    vop(lambda e, bp=bp, cc=cc, par=par: e.scalar_tensor_tensor(
                        out=HST[:, bp * 64:bp * 64 + 64],
                        in0=HST[:, bp * 64:bp * 64 + 64],
                        scalar=DECB[:, par * 16 + bp:par * 16 + bp + 1],
                        in1=PE2[:, cc:cc + 64],
                        op0=ALU.mult, op1=ALU.add),
                        deps=[m_st, m_decb])
                    hst_ok[bp] = b.mark("vector", sV)
                pe2_free[qh % 2] = hst_ok[qh * 4 + 3]
                st["laststt"] = hst_ok[qh * 4 + 3]
            st["sbm"][par] = m_st
            st["sbv"][par] = hst_ok[15]

        if PH >= 1:
            gen = pre_sa(0)
            for _ in range(6):
                next(gen, None)
            for c in range(NPC):
                gen = pre_sa(c + 1) if c + 1 < NPC else None
                pre_sb(c, gen)
                if gen is not None:
                    for _ in range(6):
                        next(gen, None)

        for g in range(NGM if PH >= 3 else (1 if PH >= 2 else 0)):
            t0 = g * GM
            m_h = emit_ln(lambda kc, t0=t0: X[:, kc * LSEG + t0:kc * LSEG + t0 + GM], GM,
                          lambda kc: HG[:, kc * GM:(kc + 1) * GM],
                          ca1_col, sa_col, g % 2, SQW,
                          deps=[d_x, st["tap_h"], st["wo"], st["ipro"]],
                          x_wide=X.rearrange("p (k t) -> p k t", k=8)[:, :, t0:t0 + GM])
            if debug:
                for kc in range(8):
                    st["tap_h"] = b.dma(
                        "sync", taps["h_dbg"][kc * 128:(kc + 1) * 128, t0:t0 + GM],
                        HG[:, kc * GM:(kc + 1) * GM], sO, deps=[m_h])
            for mz in range(16):
                pz = PE2 if mz % 2 == 0 else PY
                pzf = pe2_free[0] if mz % 2 == 0 else pf["PY"]
                for kc in range(8):
                    if CUT & 64:
                        break
                    top(lambda e, mz=mz, kc=kc, pz=pz: e.matmul(
                        pz[:, 0:GM],
                        WINZ[:, kc * DI + mz * 128:kc * DI + (mz + 1) * 128],
                        HG[:, kc * GM:(kc + 1) * GM],
                        start=(kc == 0), stop=(kc == 7)),
                        deps=[m_h, m_winz, pzf, pe2_free[1] if mz % 2 == 0 else None])
                m_z = b.mark("tensor", sT)
                sop(lambda e, mz=mz, pz=pz: e.activation(
                    out=SILUZ[:, mz * GM:(mz + 1) * GM], in_=pz[:, 0:GM],
                    func=AF.Silu, bias=0.0, scale=1.0), deps=[m_z])
                m_sz = b.mark("scalar", sS)
                if mz % 2 == 0:
                    pe2_free[0] = m_sz
                    pe2_free[1] = m_sz
                else:
                    pf["PY"] = m_sz
            st["siluz"] = b.mark("scalar", sS)
            m_rms = None
            for gci in range(2):
                cglob = NPC + g * 2 + gci
                m_rms = emit_chunk(
                    lambda kc, gci=gci: HG[:, kc * GM + gci * 128:kc * GM + gci * 128 + 128],
                    m_h, cglob, own=True, gci=gci)
            sop(lambda e: e.activation(out=R3[:, 0:GM], in_=RMSA[0:1, 0:GM],
                                       func=AF.Ln, bias=EPS[0:1, :], scale=1.0 / DI),
                deps=[m_rms, st["r2w"]])
            sop(lambda e: e.activation(out=R3[:, 0:GM], in_=R3[:, 0:GM],
                                       func=AF.Exp, bias=0.0, scale=-0.5))
            m_r2 = b.mark("scalar", sS)
            m_wr2 = b.dma("sync", scr_r2[g % 2:g % 2 + 1, 0:GM], R3[:, 0:GM], sQ,
                          deps=[m_r2])
            st["r2w"] = m_wr2
            m_r2b = b.dma("sync", R2B[:, 0:GM],
                          scr_r2[g % 2:g % 2 + 1, 0:GM].broadcast_to([128, GM]), sQ,
                          deps=[m_wr2])
            if debug:
                for mz in range(16):
                    st["tap_yg"] = b.dma(
                        "sync", taps["yg_dbg"][mz * 128:(mz + 1) * 128, t0:t0 + GM],
                        YGb.rearrange("p (m t) -> p m t", m=16)[:, mz, 0:GM],
                        sO, deps=[m_rms])
            wo_marks = []
            for kc in range(16):
                buf = SQW[:, (kc % 2) * 1024:(kc % 2) * 1024 + 1024]
                d_wo = b.dma("gpsimd", buf, woutb_d[kc * 128:(kc + 1) * 128, :], sG,
                             deps=[wo_marks[kc - 2] if kc >= 2 else m_rms,
                                   (sCW, sc_wout),
                                   pf["PA"], pf["PB"], pf["PC"], pf["PD"]])
                vop(lambda e, buf=buf, kc=kc: e.tensor_scalar(
                    out=buf, in0=buf, scalar1=NRMW[:, kc:kc + 1], scalar2=None,
                    op0=ALU.mult), deps=[d_wo, m_cons])
                m_nw = b.mark("vector", sV)
                for m8 in range(8):
                    if CUT & 128:
                        break
                    pt = (PA, PB, PC, PD)[m8 % 4]
                    cc = (m8 // 4) * 256
                    stt_ = (kc == 0 and m8 < 4)
                    stp_ = (kc == 15 and m8 >= 4)
                    top(lambda e, buf=buf, m8=m8, pt=pt, cc=cc, kc=kc, stt_=stt_,
                        stp_=stp_: e.matmul(
                        pt[:, cc:cc + GM],
                        buf[:, m8 * 128:(m8 + 1) * 128],
                        YGb[:, kc * GM:(kc + 1) * GM],
                        start=stt_, stop=stp_),
                        deps=[m_nw, pf["PA"], pf["PB"], pf["PC"], pf["PD"]])
                wo_marks.append(b.mark("tensor", sT))
            m_wo = wo_marks[-1]
            st["wo"] = m_wo
            for m8 in range(8):
                pt = (PA, PB, PC, PD)[m8 % 4]
                cc = (m8 // 4) * 256
                vop(lambda e, pt=pt, cc=cc, m8=m8: e.scalar_tensor_tensor(
                    out=FFT[:, 0:256], in0=pt[:, cc:cc + GM],
                    scalar=ga_col(m8), in1=R2B[:, 0:GM],
                    op0=ALU.mult, op1=ALU.mult), deps=[m_wo, m_r2b, m_adln])
                vop(lambda e, m8=m8, t0=t0: e.tensor_add(
                    out=X[:, m8 * LSEG + t0:m8 * LSEG + t0 + GM],
                    in0=X[:, m8 * LSEG + t0:m8 * LSEG + t0 + GM],
                    in1=FFT[:, 0:256]))
            m_xu = b.mark("vector", sV)
            pf["PA"] = pf["PB"] = pf["PC"] = pf["PD"] = m_xu

        m_mixer = barrier()
        if debug:
            for kc in range(8):
                b.dma("sync", taps["xmid_dbg"][kc * 128:(kc + 1) * 128, :],
                      X[:, kc * LSEG:(kc + 1) * LSEG], sO, deps=[m_mixer[1]])
            m_mixer = barrier()

        NGF_RUN = NGF if PH >= 5 else (1 if PH >= 4 else 0)
        d_wg = b.dma("gpsimd", WGB.rearrange("p (k g) -> p k g", k=8),
                     wg_d.rearrange("(k p) g -> p k g", p=128), sG,
                     deps=[m_mixer[0]])
        for g in range(NGF_RUN):
            t0 = g * GF
            m_h2 = emit_ln(lambda kc, t0=t0: X[:, kc * LSEG + t0:kc * LSEG + t0 + GF], GF,
                           lambda kc: H2G[:, kc * GF:(kc + 1) * GF],
                           cb1_col, sb_col, g % 2, XBF, deps=[m_mixer[1]],
                           x_wide=X.rearrange("p (k t) -> p k t", k=8)
                           [:, :, t0:t0 + GF])
            for kc in range(8):
                top(lambda e, kc=kc: e.matmul(
                    PS[0:16, 0:512], WGB[:, kc * 16:(kc + 1) * 16],
                    H2G[:, kc * GF:(kc + 1) * GF],
                    start=(kc == 0), stop=(kc == 7)),
                    deps=[d_wg, m_h2, pf["PS"]])
            m_gs = b.mark("tensor", sT)
            sop(lambda e: e.activation(out=GTF[0:16, 0:512], in_=PS[0:16, 0:512],
                                       func=AF.Sigmoid, bias=0.0, scale=1.0),
                deps=[m_gs, st["cmbw"]])
            m_sig = b.mark("scalar", sS)
            pf["PS"] = m_sig
            for sc in range(4):
                top(lambda e, sc=sc: e.transpose(PS[0:128, sc * 16:(sc + 1) * 16],
                                                 GTF[0:16, sc * 128:(sc + 1) * 128],
                                                 IDF[0:16, 0:16]),
                    deps=[m_sig, pf["PS"]])
            m_sct = b.mark("tensor", sT)
            vop(lambda e: e.tensor_copy(out=GTF[:, 512:576], in_=PS[:, 0:64]),
                deps=[m_sct])
            m_sctm = b.mark("vector", sV)
            pf["PS"] = m_sctm
            for sc in range(4):
                S0 = GTF[:, 512 + sc * 16:512 + (sc + 1) * 16]
                W1 = GTF[:, 640:656]
                W2 = GTF[:, 656:672]
                G2 = GTF[:, 672:676]
                M2c = GTF[:, 676:677]
                T1c = GTF[:, 677:678]
                CMB = GTF[:, 576 + sc * 16:576 + (sc + 1) * 16]
                AM = GTF[:, 678:682]
                pairs = [(i, j) for i in range(4) for j in range(i + 1, 4)]
                for pi, (i, j) in enumerate(pairs):
                    vop(lambda e, i=i, j=j, pi=pi: e.tensor_add(
                        out=GTF[:, 682 + pi * 4:686 + pi * 4],
                        in0=S0.rearrange("p (g j) -> p g j", g=4)[:, :, i],
                        in1=S0.rearrange("p (g j) -> p g j", g=4)[:, :, j]),
                        deps=[m_sctm])
                vop(lambda e: e.tensor_copy(out=G2, in_=GTF[:, 682:686]))
                for pi in range(1, 6):
                    vop(lambda e, pi=pi: e.tensor_tensor(
                        out=G2, in0=G2, in1=GTF[:, 682 + pi * 4:686 + pi * 4],
                        op=ALU.max))
                gp = [(i, j) for i in range(4) for j in range(i + 1, 4)]
                for pi, (i, j) in enumerate(gp):
                    vop(lambda e, i=i, j=j, pi=pi: e.tensor_tensor(
                        out=GTF[:, 682 + pi:683 + pi], in0=G2[:, i:i + 1],
                        in1=G2[:, j:j + 1], op=ALU.min))
                vop(lambda e: e.tensor_copy(out=M2c, in_=GTF[:, 682:683]))
                for pi in range(1, 6):
                    vop(lambda e, pi=pi: e.tensor_tensor(
                        out=M2c, in0=M2c, in1=GTF[:, 682 + pi:683 + pi], op=ALU.max))
                vop(lambda e: e.tensor_scalar(out=AM, in0=G2, scalar1=M2c,
                                              scalar2=None, op0=ALU.is_ge))
                amx = AM.rearrange("p (g o) -> p g o", o=1).broadcast_to([128, 4, 4])
                vop(lambda e: e.tensor_mul(out=W1.rearrange("p (g j) -> p g j", g=4),
                                           in0=S0.rearrange("p (g j) -> p g j", g=4),
                                           in1=amx))
                vop(lambda e: e.tensor_scalar(
                    out=W2.rearrange("p (g j) -> p g j", g=4), in0=amx,
                    scalar1=1e30, scalar2=-1e30, op0=ALU.mult, op1=ALU.add))
                vop(lambda e: e.tensor_add(out=W1, in0=W1, in1=W2))
                for it in range(3):
                    vop(lambda e: e.reduce_max(out=T1c, in_=W1, axis=mybir.AxisListType.X))
                    vop(lambda e: e.tensor_scalar(out=W2, in0=W1, scalar1=T1c,
                                                  scalar2=None, op0=ALU.is_ge))
                    vop(lambda e: e.tensor_scalar(out=W2, in0=W2, scalar1=-1e30,
                                                  scalar2=None, op0=ALU.mult))
                    vop(lambda e: e.tensor_add(out=W1, in0=W1, in1=W2))
                vop(lambda e: e.reduce_max(out=T1c, in_=W1, axis=mybir.AxisListType.X))
                vop(lambda e: e.tensor_mul(out=W1.rearrange("p (g j) -> p g j", g=4),
                                           in0=S0.rearrange("p (g j) -> p g j", g=4),
                                           in1=amx))
                vop(lambda e: e.tensor_scalar(
                    out=W2.rearrange("p (g j) -> p g j", g=4), in0=amx,
                    scalar1=1e30, scalar2=-1e30, op0=ALU.mult, op1=ALU.add))
                vop(lambda e: e.tensor_add(out=W1, in0=W1, in1=W2))
                vop(lambda e: e.tensor_scalar(out=W2, in0=W1, scalar1=T1c,
                                              scalar2=None, op0=ALU.is_ge))
                vop(lambda e: e.tensor_mul(out=W1, in0=S0, in1=W2))
                vop(lambda e: e.reduce_sum(out=T1c, in_=W1, axis=mybir.AxisListType.X))
                vop(lambda e: e.tensor_scalar(out=T1c, in0=T1c, scalar1=1e-6,
                                              scalar2=None, op0=ALU.add))
                vop(lambda e: e.reciprocal(out=T1c, in_=T1c))
                vop(lambda e: e.tensor_scalar(out=CMB, in0=W1, scalar1=T1c,
                                              scalar2=None, op0=ALU.mult))
            m_cmb = b.mark("vector", sV)
            for sc in range(4):
                top(lambda e, sc=sc: e.transpose(PS[0:16, sc * 128:(sc + 1) * 128],
                                                 GTF[:, 576 + sc * 16:576 + (sc + 1) * 16],
                                                 IDF[0:128, 0:128]),
                    deps=[m_cmb, pf["PS"]])
            m_ct = b.mark("tensor", sT)
            vop(lambda e: e.tensor_copy(out=GTF[0:16, 0:512], in_=PS[0:16, 0:512]),
                deps=[m_ct])
            m_cfm = b.mark("vector", sV)
            pf["PS"] = m_cfm
            if debug and g == 0:
                b.dma("sync", taps["cmb2_dbg"][:, :], GTF[0:16, 0:512], sO,
                      deps=[m_cfm])
                for kkc in range(8):
                    b.dma("sync", taps["h2_dbg"][:, kkc * 512:(kkc + 1) * 512],
                          H2G[:, kkc * 512:(kkc + 1) * 512], sO, deps=[m_h2])
                for _e in ("vector", "scalar", "tensor"):
                    b.wait(_e, (sO, sO.n))
            m_cw = b.dma("sync", scr_cmb[0:1, :].rearrange("o (e t) -> (o e) t", e=16),
                         GTF[0:16, 0:512], sQ, deps=[m_cfm])
            st["cmbw"] = m_cw
            exp_marks = []
            out67_mark = st.get("ffn_free")
            for ei in range(NE):
                buf = EWB[ei % 2]
                dd = [exp_marks[ei - 2] if ei >= 2 else
                      (m_mixer[0] if g == 0 else st["exp_prev"][ei % 2])]
                d13 = b.dma("gpsimd", buf[:, 0:8192].rearrange("p (k f) -> p k f", k=8),
                            w13b_d[ei].rearrange("(k p) f -> p k f", p=128), sG,
                            deps=dd + [(sC, sC.n)])
                d2 = b.dma("gpsimd",
                           buf[:, 8192:12288].rearrange("p (k m) -> p k m", k=4),
                           w2b_d[ei].rearrange("(k p) m -> p k m", p=128), sG)
                m_cbc = None
                if ei < 16:
                    d_cb = b.dma("sync", CBTF[:, 0:512],
                                 scr_cmb[0:1, ei * 512:(ei + 1) * 512]
                                 .broadcast_to([128, 512]), sQ,
                                 deps=[m_cw, st["cbc"]])
                    vop(lambda e, ei=ei: e.tensor_copy(
                        out=CBTB[:, (ei % 2) * 512:(ei % 2) * 512 + 512],
                        in_=CBTF[:, 0:512]), deps=[d_cb])
                    m_cbc = b.mark("vector", sV)
                    st["cbc"] = m_cbc
                m_vfc = [None] * 4
                mul_marks = [None] * 4
                silu_marks = [None] * 4
                w2b = ("PA", "PB", "PC", "PD", "PE2")

                def emit_p1(fc):
                    for kc in range(8):
                        top(lambda e, fc=fc, kc=kc, buf=buf: e.matmul(
                            PS[:, 0:512],
                            buf[:, kc * 1024 + fc * 128:
                                kc * 1024 + fc * 128 + 128],
                            H2G[:, kc * GF:(kc + 1) * GF],
                            start=(kc == 0), stop=(kc == 7)),
                            deps=([d13, m_h2, pf["PS"]] if fc == 0 else
                                  ([silu_marks[fc - 1]] if kc == 0 else ())))
                    m_p1u = b.mark("tensor", sT)
                    sop(lambda e, fc=fc: e.activation(
                        out=M1T[:, fc * 512:(fc + 1) * 512], in_=PS[:, 0:512],
                        func=AF.Silu, bias=0.0, scale=1.0), deps=[m_p1u])
                    silu_marks[fc] = b.mark("scalar", sS)

                def emit_p3(fc):
                    for kc in range(8):
                        top(lambda e, fc=fc, kc=kc, buf=buf: e.matmul(
                            PY[:, 0:512],
                            buf[:, kc * 1024 + 512 + fc * 128:
                                kc * 1024 + 512 + fc * 128 + 128],
                            H2G[:, kc * GF:(kc + 1) * GF],
                            start=(kc == 0), stop=(kc == 7)),
                            deps=(([silu_marks[fc], pf["PY"]] if fc == 0 else
                                   [silu_marks[fc], mul_marks[fc - 1]])
                                  if kc == 0 else ()))
                    m_p3u = b.mark("tensor", sT)
                    vop(lambda e, fc=fc: e.tensor_mul(
                        out=TVT[:, fc * 512:(fc + 1) * 512], in0=PY[:, 0:512],
                        in1=M1T[:, fc * 512:(fc + 1) * 512]),
                        deps=[m_p3u, silu_marks[fc]])
                    if ei < 16:
                        vop(lambda e, fc=fc, ei=ei: e.tensor_mul(
                            out=TVT[:, fc * 512:(fc + 1) * 512],
                            in0=TVT[:, fc * 512:(fc + 1) * 512],
                            in1=CBTB[:, (ei % 2) * 512:(ei % 2) * 512 + 512]),
                            deps=[m_cbc])
                    mul_marks[fc] = b.mark("vector", sV)
                    m_vfc[fc] = mul_marks[fc]

                def emit_w2(fc):
                    for m8 in range(5):
                        tgt = (PA, PB, PC, PD, PE2)[m8]
                        stt_ = (ei == 0 and fc == 0)
                        stp_ = (ei == NE - 1 and fc == 3)
                        top(lambda e, m8=m8, fc=fc, buf=buf, tgt=tgt, stt_=stt_,
                            stp_=stp_: e.matmul(
                            tgt[:, 0:512],
                            buf[:, 8192 + fc * 1024 + m8 * 128:
                                8192 + fc * 1024 + m8 * 128 + 128],
                            TVT[:, fc * 512:(fc + 1) * 512],
                            start=stt_, stop=stp_),
                            deps=[d2, m_vfc[fc],
                                  (pf.get(w2b[m8])
                                   if (ei == 0 and fc == 0 and m8 != 4) else None),
                                  (pe2_free[0] if (m8 == 4 and ei == 0 and fc == 0)
                                   else None),
                                  (pe2_free[1] if (m8 == 4 and ei == 0 and fc == 0)
                                   else None)])

                emit_p1(0)
                emit_p3(0)
                emit_p1(1)
                emit_w2(0)
                emit_p3(1)
                emit_p1(2)
                emit_w2(1)
                emit_p3(2)
                emit_p1(3)
                emit_w2(2)
                emit_p3(3)
                emit_w2(3)
                if debug and g == 0 and ei == 0:
                    b.dma("sync", taps["v_dbg"][:, :], TVT[:, :], sO, deps=[m_vfc[3]])
                    b.dma("sync", taps["m1_dbg"][:, :], M1T[:, :], sO, deps=[m_vfc[3]])
                    for _e in ("vector", "scalar", "tensor"):
                        b.wait(_e, (sO, sO.n))
                evac5 = None
                for (m8, tgt, acc, a0) in ((5, PS, OUT3, 0), (6, PY, OUT67, 0),
                                           (7, PS, OUT67, 512)):
                    pdep = (silu_marks[3] if m8 == 5 else
                            (mul_marks[3] if m8 == 6 else evac5))
                    for fc in range(4):
                        top(lambda e, m8=m8, fc=fc, buf=buf, tgt=tgt: e.matmul(
                            tgt[:, 0:512],
                            buf[:, 8192 + fc * 1024 + m8 * 128:
                                8192 + fc * 1024 + m8 * 128 + 128],
                            TVT[:, fc * 512:(fc + 1) * 512],
                            start=(fc == 0), stop=(fc == 3)),
                            deps=([d2, m_vfc[fc]] + ([pdep] if fc == 0 else [])))
                    m_w2 = b.mark("tensor", sT)
                    if ei == 0:
                        vop(lambda e, acc=acc, a0=a0, tgt=tgt: e.tensor_copy(
                            out=acc[:, a0:a0 + 512], in_=tgt[:, 0:512]),
                            deps=[m_w2, out67_mark])
                    else:
                        vop(lambda e, acc=acc, a0=a0, tgt=tgt: e.tensor_add(
                            out=acc[:, a0:a0 + 512], in0=acc[:, a0:a0 + 512],
                            in1=tgt[:, 0:512]), deps=[m_w2])
                    mm = b.mark("vector", sV)
                    if m8 == 5:
                        evac5 = mm
                    elif m8 == 6:
                        pf["PY"] = mm
                    else:
                        pf["PS"] = mm
                exp_marks.append(b.mark("tensor", sT))
            st["exp_prev"] = [exp_marks[16], exp_marks[17]]
            m_exp = exp_marks[-1]
            m_o67 = b.mark("vector", sV)
            for m8 in range(8):
                if m8 < 5:
                    src = (PA, PB, PC, PD, PE2)[m8]
                    vop(lambda e, src=src, m8=m8: e.tensor_scalar(
                        out=FFT[:, 0:512], in0=src[:, 0:512], scalar1=gb_col(m8),
                        scalar2=None, op0=ALU.mult), deps=[m_exp, m_adln])
                else:
                    srcb = OUT3 if m8 == 5 else OUT67
                    a0 = 0 if m8 <= 6 else 512
                    vop(lambda e, srcb=srcb, a0=a0, m8=m8: e.tensor_scalar(
                        out=FFT[:, 0:512],
                        in0=srcb[:, a0:a0 + 512],
                        scalar1=gb_col(m8), scalar2=None, op0=ALU.mult),
                        deps=[m_o67, m_adln])
                vop(lambda e, m8=m8, t0=t0: e.tensor_add(
                    out=X[:, m8 * LSEG + t0:m8 * LSEG + t0 + GF],
                    in0=X[:, m8 * LSEG + t0:m8 * LSEG + t0 + GF],
                    in1=FFT[:, 0:512]))
            ffn_out_free = b.mark("vector", sV)
            st["ffn_free"] = ffn_out_free
            pf["PA"] = pf["PB"] = pf["PC"] = pf["PD"] = pf["PY"] = ffn_out_free
            pe2_free[0] = pe2_free[1] = ffn_out_free

        m_done = barrier()
        for kc in range(8):
            b.dma("sync", out_T[kc * 128:(kc + 1) * 128, :],
                  X[:, kc * LSEG:(kc + 1) * LSEG], sO, deps=[m_done[1]])
        b.emit("sync", lambda e, v=sO.n: e.wait_ge(sO.h, v))

        with nc.Block() as blk:
            blk.sync(lambda e: [f(e) for f in b.ops["sync"]])
            blk.tensor(lambda e: [f(e) for f in b.ops["tensor"]])
            blk.vector(lambda e: [f(e) for f in b.ops["vector"]])
            blk.scalar(lambda e: [f(e) for f in b.ops["scalar"]])
            blk.gpsimd(lambda e: [f(e) for f in b.ops["gpsimd"]])
    return nc


_PROG_CACHE = {}


def _get_prog(debug=False):
    if debug not in _PROG_CACHE:
        _PROG_CACHE[debug] = build_core_program(debug)
    return _PROG_CACHE[debug]


def _stage_inputs(inp):
    f = np.float32
    x = np.asarray(inp["L0_attn_x"], f)
    win = np.asarray(inp["L0_attn_W_in"], f)
    wout = np.ascontiguousarray(np.asarray(inp["L0_attn_W_out"], f))
    dtbias = np.asarray(inp["L0_attn_dt_bias"], f)
    alogv = np.asarray(inp["L0_attn_A_log"], f)
    dskip = np.asarray(inp["L0_attn_D_skip"], f)
    nrm = np.asarray(inp["L0_attn_norm_weight"], f)
    wgate = np.asarray(inp["L0_ffn_w_gate"], f)
    w1e = np.asarray(inp["L0_ffn_w1_e"], f)
    w3e = np.asarray(inp["L0_ffn_w3_e"], f)
    w2e = np.asarray(inp["L0_ffn_w2_e"], f)
    w1s = np.asarray(inp["L0_ffn_w1_s"], f)
    w3s = np.asarray(inp["L0_ffn_w3_s"], f)
    w2s = np.asarray(inp["L0_ffn_w2_s"], f)
    cv = np.asarray(inp["L0_c"], f)
    wmod = np.ascontiguousarray(np.asarray(inp["L0_wmod"], f))

    perm = np.concatenate([np.arange(0, 32, 2), np.arange(1, 32, 2),
                           np.arange(32, 64)])
    win_x = np.ascontiguousarray(np.concatenate(
        [win[:, 2048:4096], win[:, 4096:4160][:, perm],
         win[:, 4160:4224][:, perm], win[:, 4224:4256]], axis=1))
    win_z = np.ascontiguousarray(win[:, 0:2048])

    w13 = np.empty((NE, DM, 1024), f)
    w2a = np.empty((NE, 512, DM), f)
    for e in range(16):
        w13[e, :, 0:512] = w1e[e].T
        w13[e, :, 512:1024] = w3e[e].T
        w2a[e] = w2e[e].T
    for half in range(2):
        sl = slice(half * 512, (half + 1) * 512)
        w13[16 + half, :, 0:512] = w1s[sl, :].T
        w13[16 + half, :, 512:1024] = w3s[sl, :].T
        w2a[16 + half] = w2s[:, sl].T

    invfreq = ((1.0 / 10000.0 ** (np.arange(16, dtype=np.float64) / 16.0))
               / TWO_PI).astype(f)
    ifr = np.broadcast_to(np.tile(invfreq, 32)[None, :], (128, 512)).astype(f).copy()
    caus = np.where(np.arange(128)[None, :] >= np.arange(128)[:, None],
                    f(4.0), f(-1e30)).astype(f)
    common = dict(
        win_x=win_x, win_z=win_z, wout=wout,
        dtb=np.ascontiguousarray(dtbias[:, None]),
        alog=np.ascontiguousarray(alogv[:, None]),
        dcols=np.broadcast_to(dskip[None, :], (128, 32)).copy(),
        normw=np.ascontiguousarray(nrm.reshape(16, 128).T),
        wg=np.ascontiguousarray(wgate.T), w13=w13, w2e=w2a,
        wmod=wmod,
        identb=np.eye(128, dtype=BF), identf=np.eye(128, dtype=f),
        onesb=np.ones((128, 1), BF), ifreqt=ifr, causn=caus,
    )
    in_maps = []
    for core in range(8):
        bi = core // 4
        seg = core % 4
        s0 = seg * LSEG
        xs_T = np.ascontiguousarray(x[bi, s0:s0 + LSEG, :].T)
        npad = LPRE - s0
        xpre_T = np.zeros((DM, LPRE), f)
        if s0 > 0:
            xpre_T[:, npad:] = x[bi, 0:s0, :].T
        val = np.zeros((128, NPC), f)
        val[:, npad // Q:] = 1.0
        m = dict(common)
        m.update(xseg=xs_T, xpre=xpre_T, validd=val,
                 cvec=np.ascontiguousarray(cv[bi].reshape(8, 128).T))
        in_maps.append(m)
    return in_maps


def kernel(**inputs):
    import os
    debug = bool(os.environ.get("MK_DBG"))
    nc = _get_prog(debug)
    in_maps = _stage_inputs(inputs)
    res = run_bass_kernel_spmd(nc, in_maps, core_ids=list(range(8)))
    out = np.empty((2, 8192, DM), np.float32)
    for core in range(8):
        bi, seg = core // 4, core % 4
        out[bi, seg * LSEG:(seg + 1) * LSEG, :] = res.results[core]["out_T"].T
    if debug:
        kernel._dbg = res.results
    return out


# version 08
# speedup vs baseline: 1.2073x; 1.0420x over previous
import os as _os
NO_INTRA = bool(_os.environ.get("MK_NO_INTRA"))
NO_SKIP = bool(_os.environ.get("MK_NO_SKIP"))
NO_INTER = bool(_os.environ.get("MK_NO_INTER"))
PH = int(_os.environ.get("MK_PH", "99"))
CUT = int(_os.environ.get("MK_CUT", "0"))
"""Fused megakernel for mk_adaln_d1: adaLN block = mamba2-SSD mixer + MoE FFN.

One raw-Bass program per NeuronCore, 8 cores, ONE run_bass_kernel_spmd launch.

Sharding: 8 cores x 2048 contiguous tokens (2 sequences x 4 segments each).
The scan state entering a segment is recomputed on-core by a state-only pass
over the zero-padded prefix of its sequence; a 0/1 validity mask (per-core
data) zeroes the state injections of pad tokens, so the program is
SPMD-uniform.

Layouts: residual feature-major [128 feat x tokens] fp32 (host stages x
transposed; output transposed back on host - layout only). Matmuls bf16
operands / fp32 PSUM accumulation; elementwise fp32 except matmul-operand
level factors (rope cos/sin, decay/q factors) which are bf16.
Scan: chunked SSD (Q=128); RoPE-on-state handled exactly by derotating B/C
with cumulative angles; decay mask exp(min(rowrep(-a sr_i) + col(a sr_j +
ln dt_j), caus)) with caus = 4.0 for i>=j else -1e30 (causality + clamp).
"""
import math
import numpy as np
import ml_dtypes
import concourse.bass as bass
import concourse.mybir as mybir
from concourse.bass_utils import run_bass_kernel_spmd
from contextlib import ExitStack

F32 = mybir.dt.float32
BF16 = mybir.dt.bfloat16
I32 = mybir.dt.int32
AF = mybir.ActivationFunctionType
ALU = mybir.AluOpType

DM, DI, NH = 1024, 2048, 32
LSEG, LPRE, Q = 2048, 6144, 128
NCH, NPC = LSEG // Q, LPRE // Q
NE = 18
GM, NGM = 256, 8
GF, NGF = 512, 4
TWO_PI = 2.0 * math.pi
PI_CL = 3.1415925
NIX = 2208
ENGS = ("sync", "tensor", "vector", "scalar", "gpsimd")
BF = ml_dtypes.bfloat16


class Sem:
    def __init__(self, h, owner=None):
        self.h = h
        self.n = 0
        self.owner = owner
        self.rr = 0

    def inc(self, k=1):
        self.n += k
        return self.n


CHAINED = ("vector", "scalar", "gpsimd")


class Bld:
    def __init__(self, esem=None):
        self.ops = {k: [] for k in ENGS}
        self.waited = {k: {} for k in ENGS}
        self.last_unmarked = {k: None for k in ENGS}
        self.last_mark = {k: None for k in ENGS}
        self.esem = esem or {}

    def wait(self, eng, dep):
        if dep is None:
            return
        sem, val = dep
        if val <= 0:
            return
        if self.waited[eng].get(id(sem), -1) >= val:
            return
        self.waited[eng][id(sem)] = val
        self.ops[eng].append(lambda e, s=sem, v=val: e.wait_ge(s.h, v))

    def emit(self, eng, fn, deps=()):
        if eng in CHAINED:
            sem = self.esem[eng]
            self.wait(eng, (sem, sem.n))
            for d in deps:
                self.wait(eng, d)
            v = sem.inc(1)

            def wrapped(e, p=fn, s=sem):
                r = p(e)
                inst = r[-1] if isinstance(r, tuple) else r
                inst.then_inc(s.h, 1)
                return r
            self.ops[eng].append(wrapped)
            self.last_mark[eng] = (sem, v)
        else:
            for d in deps:
                self.wait(eng, d)
            self.ops[eng].append(fn)
            self.last_unmarked[eng] = len(self.ops[eng]) - 1

    def mark(self, eng, sem):
        if eng in CHAINED:
            return self.last_mark[eng]
        idx = self.last_unmarked[eng]
        if idx is None:
            return self.last_mark[eng]
        v = sem.inc(1)
        prev = self.ops[eng][idx]

        def wrapped(e, p=prev, s=sem):
            r = p(e)
            inst = r[-1] if isinstance(r, tuple) else r
            inst.then_inc(s.h, 1)
            return r
        self.ops[eng][idx] = wrapped
        self.last_unmarked[eng] = None
        self.last_mark[eng] = (sem, v)
        return (sem, v)

    def dma(self, eng, out, in_, sem, deps=(), slow=False):
        if isinstance(sem, list):
            pool = sem
            slot = pool[0].rr % len(pool)
            pool[0].rr += 1
            ssem = pool[slot]
        else:
            ssem = sem
        for d in deps:
            self.wait(eng, d)
        self.wait(eng, (ssem, ssem.n))
        v = ssem.inc(16)
        self.ops[eng].append(
            lambda e, o=out, i=in_, s=ssem, sl=slow: e.dma_start(
                out=o, in_=i, allow_slow_non_contiguous=sl).then_inc(s.h, 16))
        return (ssem, v)


def build_core_program(debug=False):
    nc = bass.Bass("TRN2", target_bir_lowering=False, debug=False)

    def din(name, shape, dtype=F32):
        return nc.dram_tensor(name, shape, dtype, kind="ExternalInput").ap()

    xseg = din("xseg", [DM, LSEG])
    xpre = din("xpre", [DM, LPRE])
    validd = din("validd", [128, NPC])
    win_x = din("win_x", [DM, NIX])
    win_z = din("win_z", [DM, DI])
    wout_d = din("wout", [DI, DM])
    dtb = din("dtb", [NH, 1])
    alog = din("alog", [NH, 1])
    dcols = din("dcols", [128, NH])
    normw_d = din("normw", [128, 16])
    wg_d = din("wg", [DM, 16])
    w13_d = din("w13", [NE, DM, 1024])
    w2e_d = din("w2e", [NE, 512, DM])
    cvec = din("cvec", [128, 8])
    wmod_d = din("wmod", [DM, 6 * DM])
    identb = din("identb", [128, 128], BF16)
    identf = din("identf", [128, 128], F32)
    onesb = din("onesb", [128, 1], BF16)
    ifreqt = din("ifreqt", [128, 512])
    causn = din("causn", [128, 128])

    out_T = nc.dram_tensor("out_T", [DM, LSEG], F32, kind="ExternalOutput").ap()
    taps = {}
    if debug:
        taps["h_dbg"] = nc.dram_tensor("h_dbg", [DM, LSEG], BF16, kind="ExternalOutput").ap()
        taps["yg_dbg"] = nc.dram_tensor("yg_dbg", [DI, LSEG], BF16, kind="ExternalOutput").ap()
        taps["xmid_dbg"] = nc.dram_tensor("xmid_dbg", [DM, LSEG], F32, kind="ExternalOutput").ap()
        taps["l_dbg"] = nc.dram_tensor("l_dbg", [128, 4096], F32, kind="ExternalOutput").ap()
        taps["wt_dbg"] = nc.dram_tensor("wt_dbg", [128, 4096], BF16, kind="ExternalOutput").ap()
        taps["g_dbg"] = nc.dram_tensor("g_dbg", [128, 4096], F32, kind="ExternalOutput").ap()
        taps["bfm_dbg"] = nc.dram_tensor("bfm_dbg", [128, 2048], BF16, kind="ExternalOutput").ap()
        taps["cfm_dbg"] = nc.dram_tensor("cfm_dbg", [128, 2048], BF16, kind="ExternalOutput").ap()
        taps["btm_dbg"] = nc.dram_tensor("btm_dbg", [128, 2048], BF16, kind="ExternalOutput").ap()
        taps["ctm_dbg"] = nc.dram_tensor("ctm_dbg", [128, 2048], BF16, kind="ExternalOutput").ap()
        taps["sin_dbg"] = nc.dram_tensor("sin_dbg", [128, 512], BF16, kind="ExternalOutput").ap()
        taps["cos_dbg"] = nc.dram_tensor("cos_dbg", [128, 512], BF16, kind="ExternalOutput").ap()
        taps["stm_dbg"] = nc.dram_tensor("stm_dbg", [128, 32], F32, kind="ExternalOutput").ap()
        taps["spr_dbg"] = nc.dram_tensor("spr_dbg", [128, 128], BF16, kind="ExternalOutput").ap()
        taps["h2_dbg"] = nc.dram_tensor("h2_dbg", [128, 4096], BF16, kind="ExternalOutput").ap()
        taps["cmb2_dbg"] = nc.dram_tensor("cmb2_dbg", [16, 512], F32, kind="ExternalOutput").ap()
        taps["v_dbg"] = nc.dram_tensor("v_dbg", [128, 2048], BF16, kind="ExternalOutput").ap()
        taps["m1_dbg"] = nc.dram_tensor("m1_dbg", [128, 2048], BF16, kind="ExternalOutput").ap()

    scr_row = nc.dram_tensor("scr_row", [2, NH * Q], F32, kind="Internal").ap()
    scr_dec = nc.dram_tensor("scr_dec", [2, NH], F32, kind="Internal").ap()
    scr_ln = nc.dram_tensor("scr_ln", [2, 1024], F32, kind="Internal").ap()
    scr_r2 = nc.dram_tensor("scr_r2", [2, GM], F32, kind="Internal").ap()
    scr_cmb = nc.dram_tensor("scr_cmb", [1, 16 * GF], F32, kind="Internal").ap()
    scr_mod = nc.dram_tensor("scr_mod", [1, 6 * DM], F32, kind="Internal").ap()
    w13b_d = nc.dram_tensor("w13b", [NE, DM, 1024], BF16, kind="Internal").ap()
    w2b_d = nc.dram_tensor("w2b", [NE, 512, DM], BF16, kind="Internal").ap()
    woutb_d = nc.dram_tensor("woutb", [DI, DM], BF16, kind="Internal").ap()

    b = Bld()

    with ExitStack() as es:
        EN = es.enter_context

        def sbt(name, shape, dtype=F32):
            return EN(nc.sbuf_tensor(name, shape, dtype))

        X = sbt("X", [128, 8 * LSEG])
        WAR = sbt("WAR", [128, 34048], BF16)
        WINX = WAR[:, 0:8 * NIX]
        WINZ = WAR[:, 8 * NIX:8 * NIX + 8 * DI]
        EWB = [WAR[:, i * 12288:(i + 1) * 12288] for i in range(2)]

        BFA = sbt("BFA", [128, 23552], BF16)
        BT_TM = BFA[:, 0:2048]
        CB_TM = BFA[:, 2048:4096]
        BT_FM = BFA[:, 4096:6144]
        CB_FM = BFA[:, 6144:8192]
        XSb = BFA[:, 8192:10240]
        XQA = BFA[:, 10240:10752]
        XQB = BFA[:, 10752:11264]
        WTs = BFA[:, 11264:11776]
        SINB = BFA[:, 11776:12288]
        COSB = BFA[:, 12288:12800]
        HG = BFA[:, 12800:14848]
        SILUZ = BFA[:, 14848:18944]
        YGb = BFA[:, 18944:23040]
        SPARE = BFA[:, 23040:23552]
        ZIP = SPARE[:, 384:512]
        WMBUF = [BFA[:, 0:4096], BFA[:, 4096:8192]]
        XPB = SILUZ[:, 0:1024]
        M1T = BFA[:, 0:2048]
        TVT = BFA[:, 2048:4096]
        CBTB = BFA[:, 4096:5120]
        H2G = BFA[:, 5120:9216]
        XBF = BFA[:, 14848:18944]
        WGB = BFA[:, 23040:23168]

        FA = sbt("FA", [128, 4096])
        ROWREP = FA[:, 0:1024]
        LBUF = FA[:, 1024:2048]
        ANG_A = FA[:, 2048:2304]
        ANG_B = FA[:, 2304:2560]
        YSUM = FA[:, 2048:2560]
        HST = FA[:, 2560:3584]
        MRB = FA[:, 1024:2048]
        R2B = FA[:, 3584:3840]
        R1 = FA[0:1, 1024:1536]
        R2 = FA[0:1, 2048:2560]
        R3 = FA[0:1, 3840:4096]
        RMSA = FA[0:1, 3840:4096]
        OUT67 = FA[:, 0:1024]
        OUT3 = FA[:, 2048:2560]
        CBTF = FA[:, 1024:1536]
        FFT = FA[:, 1536:2048]
        GTF = FA[:, 2560:3584]

        KI = sbt("KI", [128, 512], I32)
        IDB = sbt("IDB", [128, 128], BF16)
        IDF = sbt("IDF", [128, 128], F32)
        ONEB = sbt("ONEB", [128, 1], BF16)
        ONESR = sbt("ONESR", [1, 128], F32)
        IFRQ = sbt("IFRQ", [128, 512])
        CAUS = sbt("CAUS", [128, 128])
        ADLN = sbt("ADLN", [128, 48])
        ADL1 = sbt("ADL1", [128, 16])
        NRMW = sbt("NRMW", [128, 16])
        DCOL = sbt("DCOL", [128, NH])
        VALID = sbt("VALID", [128, NPC])
        CV = sbt("CV", [128, 8])
        CVB = sbt("CVB", [128, 8], BF16)
        T32 = sbt("T32", [NH, 5 * Q])
        TDEC = sbt("TDEC", [NH, 1])
        TCOL = sbt("TCOL", [NH, 2])
        BASE = sbt("BASE", [NH, 1])
        DTBN = sbt("DTBN", [NH, 1])
        ACOL = sbt("ACOL", [NH, 1])
        EPS = sbt("EPS", [128, 1])
        STM = sbt("STM", [128, 32])
        JLTM = sbt("JLTM", [128, 32])
        QTM = sbt("QTM", [128, 32])
        CSCTM = sbt("CSCTM", [128, 32])
        DECB = sbt("DECB", [128, 32])
        QTM2 = sbt("QTM2", [128, 32])
        SQW = sbt("SQW", [128, 2048], BF16)

        def t32(i):
            return T32[:, i * Q:(i + 1) * Q]

        PA = EN(nc.psum_tensor("PA", [128, 512], F32))
        PB = EN(nc.psum_tensor("PB", [128, 512], F32))
        PC = EN(nc.psum_tensor("PC", [128, 512], F32))
        PD = EN(nc.psum_tensor("PD", [128, 512], F32))
        PE2 = EN(nc.psum_tensor("PE2", [128, 512], F32))
        PY = EN(nc.psum_tensor("PY", [128, 512], F32))
        PS = EN(nc.psum_tensor("PS", [128, 512], F32))
        PT = EN(nc.psum_tensor("PT", [128, 1024], BF16))

        sQ = [Sem(EN(nc.semaphore(f"sQ{i}"))) for i in range(8)]
        sG = [Sem(EN(nc.semaphore(f"sG{i}"))) for i in range(8)]
        sB = [Sem(EN(nc.semaphore(f"sB{i}"))) for i in range(8)]
        sT = Sem(EN(nc.semaphore("sT")), "tensor")
        sV = Sem(EN(nc.semaphore("sV")), "vector")
        sS = Sem(EN(nc.semaphore("sS")), "scalar")
        sO = Sem(EN(nc.semaphore("sO")))
        sP = Sem(EN(nc.semaphore("sP")), "gpsimd")
        sC = Sem(EN(nc.semaphore("sC")))
        sCW = Sem(EN(nc.semaphore("sCW")))
        ESEM = {"tensor": sT, "vector": sV, "scalar": sS, "gpsimd": sP}
        b.esem = {"vector": sV, "scalar": sS, "gpsimd": sP}

        def vop(fn, deps=()):
            b.emit("vector", fn, deps)

        def sop(fn, deps=()):
            b.emit("scalar", fn, deps)

        def top(fn, deps=()):
            b.emit("tensor", fn, deps)

        def barrier():
            ms = [b.mark(e, ESEM[e]) for e in ("tensor", "vector", "scalar")]
            dsems = sQ + sG + sB + [sO]
            for e in ("tensor", "vector", "scalar"):
                for m in ms:
                    b.wait(e, m)
                for ds in dsems:
                    b.wait(e, (ds, ds.n))
            return ms

        pf = {k: None for k in ("PA", "PB", "PC", "PD", "PY", "PS", "PT")}
        pe2_free = [None, None]
        hst_ok = [None] * 16
        st = {"rowrep": None, "siluz": None, "hpre": None, "ipro": None,
              "sbm": [None, None], "sbv": [None, None],
              "ycons": None, "laststt": None, "cbc": None,
              "exp_prev": [None, None], "lnw": None, "r2w": None,
              "rowrep_dma": None, "rowbounce": None, "decw": None,
              "cmbw": None, "tap_h": None, "tap_yg": None, "modw": None,
              "wo": None, "tap_wt": None, "tap_fm": None}

        m_cons = None
        for t, d in ((IDB, identb), (IDF, identf), (ONEB, onesb), (IFRQ, ifreqt),
                     (CAUS, causn), (VALID, validd), (CV, cvec), (DTBN, dtb),
                     (ACOL, alog), (NRMW, normw_d), (DCOL, dcols)):
            m_cons = b.dma("sync", t[:, :], d, sQ[0])
        d_x = None
        for kc in range(8):
            d_x = b.dma("sync", X[:, kc * LSEG:(kc + 1) * LSEG],
                        xseg[kc * 128:(kc + 1) * 128, :], sQ[1])
        m_winx = None
        for kc in range(8):
            m_winx = b.dma("gpsimd", WINX[:, kc * NIX:(kc + 1) * NIX],
                           win_x[kc * 128:(kc + 1) * 128, :], sG[0])
        m_winz = None
        for kc in range(8):
            m_winz = b.dma("gpsimd", WINZ[:, kc * DI:(kc + 1) * DI],
                           win_z[kc * 128:(kc + 1) * 128, :], sG[1])

        vop(lambda e: e.memset(EPS[:, :], 1e-5))
        vop(lambda e: e.memset(ONESR[:, :], 1.0))
        vop(lambda e: e.memset(BASE[:, :], 0.0))
        vop(lambda e: e.memset(HST[:, :], 0.0), deps=[m_cons])
        vop(lambda e: e.memset(ZIP[:, :], 0.0))
        vop(lambda e: e.tensor_scalar(out=DTBN[:, :], in0=DTBN[:, :], scalar1=-1.0,
                                      scalar2=None, op0=ALU.mult), deps=[m_cons])
        sop(lambda e: e.activation(out=ACOL[:, :], in_=ACOL[:, :], func=AF.Exp,
                                   bias=0.0, scale=1.0), deps=[m_cons])
        sop(lambda e: e.activation(out=CVB[:, :], in_=CV[:, :], func=AF.Silu,
                                   bias=0.0, scale=1.0), deps=[m_cons])
        m_hst0 = b.mark("vector", sV)
        m_silu = b.mark("scalar", sS)
        for i in range(16):
            hst_ok[i] = m_hst0

        m_mm_hist = []
        m_modw = None
        for n in range(12):
            buf = WMBUF[n % 2]
            deps = [m_silu] if n < 2 else [m_mm_hist[n - 2]]
            d_wm = b.dma("gpsimd", buf.rearrange("p (k m) -> p k m", k=8),
                         wmod_d[:, n * 512:(n + 1) * 512]
                         .rearrange("(k p) m -> p k m", p=128), sG, deps=deps)
            for kc in range(8):
                top(lambda e, buf=buf, kc=kc: e.matmul(
                    PS[0:1, 0:512], CVB[:, kc:kc + 1],
                    buf[:, kc * 512:(kc + 1) * 512],
                    start=(kc == 0), stop=(kc == 7)),
                    deps=[d_wm, m_silu, pf["PS"]])
            m_mm = b.mark("tensor", sT)
            m_mm_hist.append(m_mm)
            vop(lambda e: e.tensor_copy(out=R1[:, :], in_=PS[0:1, 0:512]),
                deps=[m_mm, st["modw"]])
            if n in (4, 5, 10, 11):
                for hcol in (slice(0, 256), slice(256, 512)):
                    vop(lambda e, hc=hcol: e.tensor_single_scalar(
                        out=R3[:, 0:256], in_=R1[:, hc], scalar=0.0, op=ALU.is_ge))
                    vop(lambda e: e.tensor_scalar(out=R3[:, 0:256], in0=R3[:, 0:256],
                                                  scalar1=2.0, scalar2=-1.0,
                                                  op0=ALU.mult, op1=ALU.add))
                    vop(lambda e, hc=hcol: e.tensor_mul(out=R1[:, hc], in0=R1[:, hc],
                                                        in1=R3[:, 0:256]))
                    vop(lambda e, hc=hcol: e.tensor_scalar(
                        out=R1[:, hc], in0=R1[:, hc], scalar1=0.25, scalar2=None,
                        op0=ALU.max))
                    vop(lambda e, hc=hcol: e.tensor_mul(out=R1[:, hc], in0=R1[:, hc],
                                                        in1=R3[:, 0:256]))
            m_row = b.mark("vector", sV)
            pf["PS"] = m_row
            m_modw = b.dma("sync", scr_mod[0:1, n * 512:(n + 1) * 512], R1[:, :],
                           sQ[2], deps=[m_row])
            st["modw"] = m_modw
        d_adln = b.dma("sync", ADLN.rearrange("p (v k) -> p v k", v=6),
                       scr_mod.rearrange("o (v k p) -> (o p) v k", v=6, k=8, p=128),
                       sQ, deps=[m_modw], slow=True)
        vop(lambda e: e.tensor_scalar(out=ADL1[:, 0:8], in0=ADLN[:, 8:16], scalar1=1.0,
                                      scalar2=None, op0=ALU.add), deps=[d_adln])
        vop(lambda e: e.tensor_scalar(out=ADL1[:, 8:16], in0=ADLN[:, 32:40], scalar1=1.0,
                                      scalar2=None, op0=ALU.add))
        m_adln = b.mark("vector", sV)

        def sa_col(kc):
            return ADLN[:, 0 + kc:1 + kc]

        def ga_col(kc):
            return ADLN[:, 16 + kc:17 + kc]

        def sb_col(kc):
            return ADLN[:, 24 + kc:25 + kc]

        def gb_col(kc):
            return ADLN[:, 40 + kc:41 + kc]

        def ca1_col(kc):
            return ADL1[:, kc:kc + 1]

        def cb1_col(kc):
            return ADL1[:, 8 + kc:9 + kc]

        barrier()

        def emit_ln(x_tile, ncols, dst, c1col, s0col, parity, castbuf, deps,
                    x_wide=None, c1w=None, s0w=None, dst_wide=None):
            if x_wide is not None:
                vop(lambda e: e.tensor_copy(
                    out=castbuf[:, 0:8 * ncols].rearrange("p (k n) -> p k n", k=8),
                    in_=x_wide), deps=list(deps))
            else:
                for kc in range(8):
                    vop(lambda e, kc=kc: e.tensor_copy(
                        out=castbuf[:, kc * ncols:(kc + 1) * ncols], in_=x_tile(kc)),
                        deps=list(deps))
            m_cast = b.mark("vector", sV)
            for kc in range(8):
                if CUT & 256:
                    break
                top(lambda e, kc=kc: e.matmul(PS[0:1, 0:ncols], ONEB[:, :],
                                              castbuf[:, kc * ncols:(kc + 1) * ncols],
                                              start=(kc == 0), stop=(kc == 7)),
                    deps=([m_cast, pf["PS"], pf["PY"]] if kc == 0 else ()))
            m_s1 = b.mark("tensor", sT)
            sop(lambda e: e.activation(out=castbuf[:, 0:8 * ncols],
                                       in_=castbuf[:, 0:8 * ncols],
                                       func=AF.Square, bias=0.0, scale=1.0), deps=[m_s1])
            m_sq = b.mark("scalar", sS)
            for kc in range(8):
                if CUT & 256:
                    break
                top(lambda e, kc=kc: e.matmul(PY[0:1, 0:ncols], ONEB[:, :],
                                              castbuf[:, kc * ncols:(kc + 1) * ncols],
                                              start=(kc == 0), stop=(kc == 7)),
                    deps=([m_sq] if kc == 0 else ()))
            m_s2 = b.mark("tensor", sT)
            sop(lambda e: e.activation(out=R1[:, 0:ncols], in_=PS[0:1, 0:ncols],
                                       func=AF.Copy, bias=0.0, scale=1.0 / DM),
                deps=[m_s1, st["lnw"], st["modw"]])
            sop(lambda e: e.activation(out=R2[:, 0:ncols], in_=PY[0:1, 0:ncols],
                                       func=AF.Copy, bias=0.0, scale=1.0 / DM),
                deps=[m_s2])
            if ncols <= 256:
                sop(lambda e: e.activation(out=R3[:, 0:ncols], in_=R1[:, 0:ncols],
                                           func=AF.Square, bias=0.0, scale=1.0),
                    deps=[st["r2w"]])
            m_sc = b.mark("scalar", sS)
            pf["PS"] = m_sc
            pf["PY"] = m_sc
            if ncols <= 256:
                vop(lambda e: e.tensor_sub(out=R2[:, 0:ncols], in0=R2[:, 0:ncols],
                                           in1=R3[:, 0:ncols]), deps=[m_sc])
            else:
                vop(lambda e: e.tensor_mul(out=R3[:, 0:256], in0=R1[:, 0:256],
                                           in1=R1[:, 0:256]), deps=[m_sc])
                vop(lambda e: e.tensor_sub(out=R2[:, 0:256], in0=R2[:, 0:256],
                                           in1=R3[:, 0:256]))
                vop(lambda e: e.tensor_mul(out=R3[:, 0:256], in0=R1[:, 256:512],
                                           in1=R1[:, 256:512]))
                vop(lambda e: e.tensor_sub(out=R2[:, 256:512], in0=R2[:, 256:512],
                                           in1=R3[:, 0:256]))
            m_var = b.mark("vector", sV)
            sop(lambda e: e.activation(out=R2[:, 0:ncols], in_=R2[:, 0:ncols],
                                       func=AF.Ln, bias=EPS[0:1, :], scale=1.0),
                deps=[m_var])
            sop(lambda e: e.activation(out=R2[:, 0:ncols], in_=R2[:, 0:ncols],
                                       func=AF.Exp, bias=0.0, scale=-0.5))
            m_r = b.mark("scalar", sS)
            top(lambda e, ncols=ncols: e.matmul(PS[:, 0:ncols], ONESR[0:1, :],
                                                R1[0:1, 0:ncols],
                                                start=True, stop=True), deps=[m_r])
            top(lambda e, ncols=ncols: e.matmul(PY[:, 0:ncols], ONESR[0:1, :],
                                                R2[0:1, 0:ncols],
                                                start=True, stop=True))
            m_bc = b.mark("tensor", sT)
            st["lnw"] = m_bc
            if dst_wide is not None and ncols <= 256:
                w3 = FA[:, 0:8 * ncols].rearrange("p (k n) -> p k n", k=8)

                def _rb(t):
                    return (t.rearrange("p (o n) -> p o n", o=1)
                            .broadcast_to([128, 8, ncols]))

                def _cb(t):
                    return (t.rearrange("p (k o) -> p k o", o=1)
                            .broadcast_to([128, 8, ncols]))

                vop(lambda e: e.tensor_sub(out=w3, in0=x_wide,
                                           in1=_rb(PS[:, 0:ncols])),
                    deps=[m_bc, m_adln, st["rowrep_dma"]])
                vop(lambda e: e.tensor_mul(out=w3, in0=w3,
                                           in1=_rb(PY[:, 0:ncols])))
                vop(lambda e: e.tensor_mul(out=w3, in0=w3, in1=_cb(c1w)))
                vop(lambda e: e.tensor_add(out=dst_wide, in0=w3, in1=_cb(s0w)))
            else:
                for kc in range(8):
                    vop(lambda e, kc=kc: e.tensor_sub(
                        out=ROWREP[:, 0:ncols],
                        in0=x_tile(kc), in1=PS[:, 0:ncols]),
                        deps=[m_bc, m_adln, st["rowrep_dma"]])
                    vop(lambda e, kc=kc: e.tensor_mul(out=ROWREP[:, 0:ncols],
                                                      in0=ROWREP[:, 0:ncols],
                                                      in1=PY[:, 0:ncols]))
                    vop(lambda e, kc=kc: e.tensor_scalar(
                        out=dst(kc), in0=ROWREP[:, 0:ncols],
                        scalar1=c1col(kc), scalar2=s0col(kc),
                        op0=ALU.mult, op1=ALU.add))
            mh = b.mark("vector", sV)
            st["rowrep"] = mh
            pf["PS"] = mh
            pf["PY"] = mh
            return mh

        def emit_chunk(h_src, h_dep, cglob, own, gci=0):
            par = cglob % 2
            nix_t = [(PA, "PA", 0, 512), (PB, "PB", 512, 512), (PC, "PC", 1024, 512),
                     (PD, "PD", 1536, 512), (PE2, None, 2048, 160)]
            ip_deps = [h_dep, m_winx, pf["PA"], pf["PB"], pf["PC"], pf["PD"],
                       pe2_free[0], pe2_free[1]]
            for kc in range(8 if not CUT & 1024 else 0):
                for ti, (pt, _nm, c0, w) in enumerate(nix_t):
                    top(lambda e, kc=kc, pt=pt, c0=c0, w=w: e.matmul(
                        pt[:, 0:w], h_src(kc),
                        WINX[:, kc * NIX + c0:kc * NIX + c0 + w],
                        start=(kc == 0), stop=(kc == 7)),
                        deps=(ip_deps if (kc == 0 and ti == 0) else ()))
            m_ip = b.mark("tensor", sT)
            for i, pt in enumerate((PA, PB, PC, PD)):
                vop(lambda e, i=i, pt=pt: e.tensor_copy(
                    out=XSb[:, i * 512:(i + 1) * 512], in_=pt[:, :]), deps=[m_ip])
            vop(lambda e: e.tensor_copy(out=SPARE[:, 0:128], in_=PE2[:, 0:128]))
            vop(lambda e: e.tensor_copy(out=ROWREP[:, 512:544], in_=PE2[:, 128:160]),
                deps=[st["rowrep_dma"]])
            m_ev = b.mark("vector", sV)
            pf["PA"] = pf["PB"] = pf["PC"] = pf["PD"] = m_ev
            pe2_free[0] = pe2_free[1] = m_ev

            top(lambda e: e.transpose(PS[0:32, 0:128], ROWREP[:, 512:544], IDF[:, :]),
                deps=[m_ev, pf["PS"]])
            m_t1 = b.mark("tensor", sT)
            vop(lambda e: e.tensor_copy(out=t32(0), in_=PS[0:32, 0:128]),
                deps=[m_t1, st["rowbounce"]])
            m_v1 = b.mark("vector", sV)
            pf["PS"] = m_v1
            sop(lambda e: e.activation(out=t32(1), in_=t32(0), func=AF.Sigmoid,
                                       bias=DTBN[:, :], scale=-1.0), deps=[m_v1])
            sop(lambda e: e.activation(out=t32(1), in_=t32(1), func=AF.Ln,
                                       bias=0.0, scale=1.0))
            sop(lambda e: e.activation(out=t32(0), in_=t32(1), func=AF.Copy,
                                       bias=0.0, scale=-1.0))
            sop(lambda e: e.activation(out=t32(2), in_=t32(0), func=AF.Ln,
                                       bias=0.0, scale=1.0))
            m_lndt = b.mark("scalar", sS)
            vop(lambda e: e.tensor_tensor_scan(out=t32(1), data0=t32(0), data1=t32(0),
                                               initial=0.0, op0=ALU.add, op1=ALU.bypass), deps=[m_lndt])
            vop(lambda e: e.tensor_copy(out=TCOL[:, 0:1], in_=t32(1)[:, 127:128]))
            vop(lambda e: e.tensor_scalar(out=t32(3), in0=t32(1), scalar1=BASE[:, :],
                                          scalar2=None, op0=ALU.add))
            vop(lambda e: e.tensor_add(out=BASE[:, :], in0=BASE[:, :], in1=TCOL[:, 0:1]))
            vop(lambda e: e.tensor_scalar(out=t32(1), in0=t32(1), scalar1=ACOL[:, :],
                                          scalar2=None, op0=ALU.mult))
            vop(lambda e: e.tensor_scalar(out=TCOL[:, 1:2], in0=TCOL[:, 0:1],
                                          scalar1=ACOL[:, :], scalar2=None,
                                          op0=ALU.mult))
            vop(lambda e: e.scalar_tensor_tensor(out=t32(4), in0=t32(1),
                                                 scalar=TCOL[:, 1:2], in1=t32(2),
                                                 op0=ALU.subtract, op1=ALU.add),
                deps=[m_lndt])
            if own:
                vop(lambda e: e.tensor_add(out=t32(2), in0=t32(1), in1=t32(2)))
            m_v3 = b.mark("vector", sV)
            sop(lambda e: e.activation(out=t32(4), in_=t32(4), func=AF.Exp,
                                       bias=0.0, scale=1.0), deps=[m_v3])
            sop(lambda e: e.activation(out=TDEC[:, :], in_=TCOL[:, 1:2], func=AF.Exp,
                                       bias=0.0, scale=-1.0), deps=[m_v3, st["decw"]])
            m_q = b.mark("scalar", sS)
            vop(lambda e: e.tensor_scalar(out=t32(0), in0=t32(1), scalar1=-1.0,
                                          scalar2=None, op0=ALU.mult))
            m_nasr = b.mark("vector", sV)
            d_r2m = None
            if own:
                b.dma("scalar",
                      scr_row[par:par + 1, 0:2048].rearrange("o (h t) -> (o h) t", h=16),
                      t32(0)[0:16, :], sB[0], deps=[m_nasr])
                d_r2m = b.dma("scalar",
                              scr_row[par:par + 1, 2048:4096].rearrange("o (h t) -> (o h) t", h=16),
                              t32(0)[16:32, :], sB[0])
                st["rowbounce"] = d_r2m
            d_dc = b.dma("scalar", scr_dec[par:par + 1, :].rearrange("o n -> n o"),
                         TDEC[:, :], sB[1], deps=[m_q])
            st["decw"] = d_dc
            dview = scr_dec[par:par + 1, :].rearrange("o (bp two) -> (o two) bp", two=2)
            b.dma("scalar", DECB[0:64, par * 16:(par + 1) * 16],
                  dview[0:1, :].broadcast_to([64, 16]),
                  sB[2], deps=[d_dc, st["sbv"][par]], slow=True)
            m_decb = b.dma("scalar", DECB[64:128, par * 16:(par + 1) * 16],
                           dview[1:2, :].broadcast_to([64, 16]), sB[2], slow=True)
            if own:
                sop(lambda e: e.activation(out=t32(0), in_=t32(1), func=AF.Exp,
                                           bias=0.0, scale=-1.0),
                    deps=[d_r2m, m_nasr])
            m_csc = b.mark("scalar", sS)
            top(lambda e: e.transpose(PS[0:128, 0:32], t32(3), IDF[0:32, 0:32]),
                deps=[m_v3, pf["PS"]])
            top(lambda e: e.transpose(PS[0:128, 64:96], t32(4), IDF[0:32, 0:32]),
                deps=[m_q])
            if own:
                top(lambda e: e.transpose(PS[0:128, 32:64], t32(2), IDF[0:32, 0:32]))
                top(lambda e: e.transpose(PS[0:128, 96:128], t32(0), IDF[0:32, 0:32]),
                    deps=[m_csc])
            m_t2 = b.mark("tensor", sT)
            vop(lambda e: e.tensor_copy(out=STM[:, :], in_=PS[:, 0:32]), deps=[m_t2])
            vop(lambda e: e.tensor_copy(out=QTM[:, :], in_=PS[:, 64:96]))
            if own:
                vop(lambda e: e.tensor_copy(out=JLTM[:, :], in_=PS[:, 32:64]))
                vop(lambda e: e.tensor_copy(out=CSCTM[:, :], in_=PS[:, 96:128]))
            else:
                vop(lambda e, c=cglob: e.tensor_scalar(
                    out=QTM[:, :], in0=QTM[:, :], scalar1=VALID[:, c:c + 1],
                    scalar2=None, op0=ALU.mult))
            m_tm = b.mark("vector", sV)
            pf["PS"] = m_tm

            ANG_A5 = ROWREP[:, 0:512]
            ANG_B5 = ROWREP[:, 512:1024]
            vop(lambda e: e.tensor_mul(
                out=ANG_A5.rearrange("p (h k) -> p h k", h=32),
                in0=STM[:, 0:32].rearrange("p (h o) -> p h o", o=1)
                .broadcast_to([128, 32, 16]),
                in1=IFRQ[:, 0:512].rearrange("p (h k) -> p h k", h=32)),
                deps=[m_tm, st.get("trig")])
            m_trig = None
            for trig, dstT in ((0, SINB), (1, COSB)):
                if trig == 1:
                    vop(lambda e: e.tensor_scalar(out=ANG_A5, in0=ANG_A5,
                                                  scalar1=0.25, scalar2=None,
                                                  op0=ALU.add), deps=[m_trig])
                vop(lambda e: e.tensor_copy(out=KI[:, 0:512], in_=ANG_A5))
                vop(lambda e: e.tensor_copy(out=ANG_B5, in_=KI[:, 0:512]))
                vop(lambda e: e.tensor_sub(out=ANG_B5, in0=ANG_A5, in1=ANG_B5))
                vop(lambda e: e.tensor_scalar(out=ANG_B5, in0=ANG_B5,
                                              scalar1=TWO_PI, scalar2=None,
                                              op0=ALU.mult))
                vop(lambda e: e.tensor_single_scalar(out=MRB[:, 0:512], in_=ANG_B5,
                                                     scalar=PI_CL, op=ALU.is_gt))
                vop(lambda e: e.tensor_single_scalar(out=MRB[:, 512:1024], in_=ANG_B5,
                                                     scalar=-PI_CL, op=ALU.is_lt))
                vop(lambda e: e.tensor_sub(out=MRB[:, 0:512], in0=MRB[:, 0:512],
                                           in1=MRB[:, 512:1024]))
                vop(lambda e: e.scalar_tensor_tensor(out=ANG_B5, in0=MRB[:, 0:512],
                                                     scalar=-TWO_PI, in1=ANG_B5,
                                                     op0=ALU.mult, op1=ALU.add))
                vop(lambda e: e.tensor_scalar(out=ANG_B5, in0=ANG_B5,
                                              scalar1=PI_CL, scalar2=-PI_CL,
                                              op0=ALU.min, op1=ALU.max))
                m_red = b.mark("vector", sV)
                sop(lambda e, dstT=dstT, sc=(-1.0 if trig == 0 else 1.0):
                    e.activation(out=dstT[:, 0:512], in_=ANG_B5,
                                 func=AF.Sin, bias=0.0, scale=sc), deps=[m_red])
                m_trig = b.mark("scalar", sS)
            st["trig"] = m_trig

            Bt = SPARE[:, 0:64]
            Ct = SPARE[:, 64:128]
            bview = BT_TM.rearrange("p (h n) -> p h n", h=32)
            cview = CB_TM.rearrange("p (h n) -> p h n", h=32)
            sview = SINB.rearrange("p (h k) -> p h k", h=32)
            cosv = COSB.rearrange("p (h k) -> p h k", h=32)
            wview = WTs.rearrange("p (h k) -> p h k", h=32)

            def b_bc(c0, c1):
                return (Bt[:, c0:c1].rearrange("p (o n) -> p o n", o=1)
                        .broadcast_to([128, 32, c1 - c0]))

            def c_bc(c0, c1):
                return (Ct[:, c0:c1].rearrange("p (o n) -> p o n", o=1)
                        .broadcast_to([128, 32, c1 - c0]))

            vop(lambda e: e.tensor_mul(out=wview, in0=cosv, in1=b_bc(0, 16)),
                deps=[m_trig, m_ev, st["ycons"], st.get("tap_wt")])
            vop(lambda e: e.tensor_mul(out=bview[:, :, 0:16], in0=sview,
                                       in1=b_bc(16, 32)))
            vop(lambda e: e.tensor_sub(out=bview[:, :, 0:16], in0=wview,
                                       in1=bview[:, :, 0:16]))
            vop(lambda e: e.tensor_mul(out=wview, in0=sview, in1=b_bc(0, 16)))
            vop(lambda e: e.tensor_mul(out=bview[:, :, 16:32], in0=cosv,
                                       in1=b_bc(16, 32)))
            vop(lambda e: e.tensor_add(out=bview[:, :, 16:32], in0=wview,
                                       in1=bview[:, :, 16:32]))
            vop(lambda e: e.tensor_copy(out=bview[:, :, 32:64], in_=b_bc(32, 64)))
            m_brope = b.mark("vector", sV)
            if own:
                vop(lambda e: e.tensor_mul(out=wview, in0=cosv, in1=c_bc(0, 16)))
                vop(lambda e: e.tensor_mul(out=cview[:, :, 0:16], in0=sview,
                                           in1=c_bc(16, 32)))
                vop(lambda e: e.tensor_sub(out=cview[:, :, 0:16], in0=wview,
                                           in1=cview[:, :, 0:16]))
                vop(lambda e: e.tensor_mul(out=wview, in0=sview, in1=c_bc(0, 16)))
                vop(lambda e: e.tensor_mul(out=cview[:, :, 16:32], in0=cosv,
                                           in1=c_bc(16, 32)))
                vop(lambda e: e.tensor_add(out=cview[:, :, 16:32], in0=wview,
                                           in1=cview[:, :, 16:32]))
                vop(lambda e: e.tensor_copy(out=cview[:, :, 32:64], in_=c_bc(32, 64)))
            m_rope = b.mark("vector", sV)

            srcs = ([(BT_TM, BT_FM), (CB_TM, CB_FM)]
                    if (own and not CUT & 512) else ([(BT_TM, BT_FM)] if own else []))
            for (srcT, dstF) in srcs:
                for half in range(2):
                    for j in range(8):
                        jj = half * 8 + j
                        top(lambda e, jj=jj, j=j, srcT=srcT: e.transpose(
                            PT[:, j * 128:(j + 1) * 128],
                            srcT[:, jj * 128:(jj + 1) * 128], IDB[:, :]),
                            deps=[m_rope, pf["PT"]])
                    m_pt = b.mark("tensor", sT)
                    vop(lambda e, half=half, dstF=dstF: e.tensor_copy(
                        out=dstF[:, half * 1024:(half + 1) * 1024],
                        in_=PT[:, 0:1024]), deps=[m_pt])
                    pf["PT"] = b.mark("vector", sV)
            m_fm = b.mark("vector", sV)
            if debug and own and cglob == NPC:
                b.dma("sync", taps["sin_dbg"][:, :], SINB[:, :], sO, deps=[m_trig])
                b.dma("sync", taps["cos_dbg"][:, :], COSB[:, :], sO, deps=[m_trig])
                b.dma("sync", taps["stm_dbg"][:, :], STM[:, :], sO, deps=[m_tm])
                b.dma("sync", taps["spr_dbg"][:, :], SPARE[:, 0:128], sO, deps=[m_ev])
                b.dma("sync", taps["btm_dbg"][:, :], BT_TM[:, :], sO, deps=[m_rope])
                b.dma("sync", taps["ctm_dbg"][:, :], CB_TM[:, :], sO, deps=[m_rope])
                b.dma("sync", taps["bfm_dbg"][:, :], BT_FM[:, :], sO, deps=[m_fm])
                st["tap_fm"] = b.dma("sync", taps["cfm_dbg"][:, :], CB_FM[:, :], sO,
                                     deps=[m_fm])
                for _e in ("vector", "scalar", "tensor"):
                    b.wait(_e, (sO, sO.n))

            for qh in range(4):
                hset = list(range(qh * 8, qh * 8 + 8))
                if own:
                    d_rr = b.dma("scalar", ROWREP[:, 0:1024],
                                 scr_row[par:par + 1, qh * 1024:(qh + 1) * 1024]
                                 .broadcast_to([128, 1024]), sB,
                                 deps=[d_r2m, st["rowrep"], st["trig"]])
                    st["rowrep_dma"] = d_rr
                    for i, h in enumerate(hset):
                        vop(lambda e, i=i, h=h: e.scalar_tensor_tensor(
                            out=LBUF[:, i * 128:(i + 1) * 128],
                            in0=ROWREP[:, i * 128:(i + 1) * 128],
                            scalar=JLTM[:, h:h + 1], in1=CAUS[:, :],
                            op0=ALU.add, op1=ALU.min),
                            deps=[d_rr, m_tm, m_cons])
                    st["rowrep"] = b.mark("vector", sV)
                    sop(lambda e: e.activation(out=LBUF[:, 0:1024], in_=LBUF[:, 0:1024],
                                               func=AF.Exp, bias=0.0, scale=1.0),
                        deps=[st["rowrep"]])
                    m_msk = b.mark("scalar", sS)
                    if debug and cglob == NPC:
                        b.dma("sync", taps["l_dbg"][:, qh * 1024:(qh + 1) * 1024],
                              LBUF[:, 0:1024], sO, deps=[m_msk])
                    for i in (0, 2, 4, 6, None, 1, 3, 5, 7):
                        if CUT & 1:
                            break
                        if i is None:
                            top(lambda e: e.transpose(PT[:, 0:128], IDB[:, :],
                                                      IDB[:, :]), deps=[pf["PT"]])
                            continue
                        h = hset[i]
                        pg = PA if i < 4 else PB
                        ql = h % 2
                        top(lambda e, h=h, pg=pg, i=i, ql=ql: e.matmul(
                            pg[:, (i % 4) * 128:(i % 4) * 128 + 128],
                            BT_FM[ql * 64:ql * 64 + 64,
                                  (h // 2) * 128:(h // 2) * 128 + 128],
                            CB_FM[ql * 64:ql * 64 + 64,
                                  (h // 2) * 128:(h // 2) * 128 + 128],
                            start=True, stop=True),
                            deps=[m_fm, pf["PA" if i < 4 else "PB"]])
                    m_g = b.mark("tensor", sT)
                    vop(lambda e, qh=qh: e.tensor_mul(
                        out=CB_TM[:, qh * 512:(qh + 1) * 512]
                        .rearrange("p (h n) -> p h n", h=8),
                        in0=CB_TM[:, qh * 512:(qh + 1) * 512]
                        .rearrange("p (h n) -> p h n", h=8),
                        in1=CSCTM[:, qh * 8:(qh + 1) * 8]
                        .rearrange("p (h o) -> p h o", o=1)
                        .broadcast_to([128, 8, 64])),
                        deps=[m_g, m_tm])
                    m_cbs = b.mark("vector", sV)
                    for lb in range(4):
                        if CUT & 2:
                            break
                        bp = qh * 4 + lb
                        top(lambda e, bp=bp, lb=lb: e.transpose(
                            PT[:, lb * 128:(lb + 1) * 128],
                            CB_TM[:, bp * 128:bp * 128 + 128], IDB[:, :]),
                            deps=[m_cbs, pf["PT"]])
                    m_cbt = b.mark("tensor", sT)
                    vop(lambda e, qh=qh: e.tensor_copy(
                        out=CB_FM[:, qh * 512:(qh + 1) * 512], in_=PT[:, 0:512]),
                        deps=[m_cbt, st.get("tap_fm")])
                    pf["PT"] = b.mark("vector", sV)
                    m_cbf = pf["PT"]
                    for lb in range(4):
                        bp = qh * 4 + lb
                        vop(lambda e, lb=lb, bp=bp: e.tensor_copy(
                            out=SPARE[:, 128 + lb * 64:128 + lb * 64 + 64],
                            in_=HST[:, bp * 64:bp * 64 + 64]),
                            deps=[hst_ok[bp]])
                    m_hstb = b.mark("vector", sV)
                    vop(lambda e, qh=qh: e.tensor_mul(
                        out=XQA[:, 0:512].rearrange("p (h n) -> p h n", h=8),
                        in0=XSb[:, qh * 512:(qh + 1) * 512]
                        .rearrange("p (h n) -> p h n", h=8),
                        in1=DCOL[:, qh * 8:(qh + 1) * 8]
                        .rearrange("p (h o) -> p h o", o=1)
                        .broadcast_to([128, 8, 64])),
                        deps=[m_cons, m_ev, m_rope])
                    vop(lambda e, qh=qh: e.tensor_mul(
                        out=XQB[:, 0:512].rearrange("p (h n) -> p h n", h=8),
                        in0=XSb[:, qh * 512:(qh + 1) * 512]
                        .rearrange("p (h n) -> p h n", h=8),
                        in1=QTM[:, qh * 8:(qh + 1) * 8]
                        .rearrange("p (h o) -> p h o", o=1)
                        .broadcast_to([128, 8, 64])),
                        deps=[m_tm])
                    m_xq = b.mark("vector", sV)
                    m_ycons = st["ycons"]
                    for h4 in range(2):
                        pg = PA if h4 == 0 else PB
                        vop(lambda e, pg=pg, h4=h4: e.tensor_mul(
                            out=WTs[:, 0:512], in0=pg[:, :],
                            in1=LBUF[:, h4 * 512:h4 * 512 + 512]),
                            deps=[m_g, m_msk, m_ycons, st.get("tap_wt")])
                        m_w = b.mark("vector", sV)
                        if debug and cglob == NPC:
                            st["tap_wt"] = b.dma(
                                "sync", taps["wt_dbg"]
                                [:, qh * 1024 + h4 * 512:qh * 1024 + h4 * 512 + 512],
                                WTs[:, 0:512], sO, deps=[m_w])
                        pf["PA" if h4 == 0 else "PB"] = m_w
                        for i4 in range(4):
                            i = h4 * 4 + i4
                            h = qh * 8 + i
                            bp = h // 2
                            lb = i // 2
                            ql = h % 2
                            yc = lb * 128
                            if not CUT & 4:
                                top(lambda e, h=h, i4=i4, ql=ql, yc=yc: e.matmul(
                                    PY[ql * 64:ql * 64 + 64, yc:yc + 128],
                                    XSb[:, h * 64:(h + 1) * 64],
                                    ZIP[:, 0:128] if NO_INTRA else
                                    WTs[:, i4 * 128:(i4 + 1) * 128],
                                    start=True, stop=False),
                                    deps=[m_w, pf["PY"]])
                                top(lambda e, i=i, ql=ql, yc=yc: e.matmul(
                                    PY[ql * 64:ql * 64 + 64, yc:yc + 128],
                                    XQA[:, i * 64:(i + 1) * 64],
                                    ZIP[:, 0:128] if NO_SKIP else IDB[:, :],
                                    start=False, stop=False),
                                    deps=[m_xq])
                            lbq = (bp % 4)
                            if not CUT & 8:
                                top(lambda e, ql=ql, yc=yc, bp=bp, lbq=lbq: e.matmul(
                                    PY[ql * 64:ql * 64 + 64, yc:yc + 128],
                                    SPARE[ql * 64:ql * 64 + 64,
                                          128 + lbq * 64:128 + lbq * 64 + 64],
                                    ZIP[ql * 64:ql * 64 + 64, 0:128] if NO_INTER
                                    else CB_FM[ql * 64:ql * 64 + 64,
                                               bp * 128:bp * 128 + 128],
                                    start=False, stop=True),
                                    deps=[m_fm, m_hstb, m_cbf])
                        m_ycons = b.mark("tensor", sT)
                        st["ycons"] = m_ycons
                    m_y = m_ycons
                    for i, h in enumerate(hset):
                        if CUT & 16:
                            break
                        lb = i // 2
                        ql = h % 2
                        cc = (qh % 2) * 256 + lb * 64
                        top(lambda e, h=h, i=i, ql=ql, cc=cc: e.matmul(
                            PE2[ql * 64:ql * 64 + 64, cc:cc + 64],
                            BT_TM[:, h * 64:(h + 1) * 64],
                            XQB[:, i * 64:(i + 1) * 64],
                            start=True, stop=True),
                            deps=[m_xq, m_brope, pe2_free[qh % 2], st["laststt"]])
                    m_st = b.mark("tensor", sT)
                    p0 = qh * 4
                    vop(lambda e, p0=p0, gci=gci: e.tensor_mul(
                        out=YGb.rearrange("p (m t) -> p m t", m=16)
                        [:, p0:p0 + 4, gci * 128:gci * 128 + 128],
                        in0=PY[:, 0:512].rearrange("p (m t) -> p m t", m=4),
                        in1=SILUZ.rearrange("p (m t) -> p m t", m=16)
                        [:, p0:p0 + 4, gci * 128:gci * 128 + 128]),
                        deps=[m_y, st["siluz"], st["tap_yg"]])
                    pf["PY"] = b.mark("vector", sV)
                    for lb in range(4):
                        bp = qh * 4 + lb
                        cc = (qh % 2) * 256 + lb * 64
                        vop(lambda e, bp=bp, cc=cc, par=par: e.scalar_tensor_tensor(
                            out=HST[:, bp * 64:bp * 64 + 64],
                            in0=HST[:, bp * 64:bp * 64 + 64],
                            scalar=DECB[:, par * 16 + bp:par * 16 + bp + 1],
                            in1=PE2[:, cc:cc + 64],
                            op0=ALU.mult, op1=ALU.add),
                            deps=[m_st, m_decb])
                        hst_ok[bp] = b.mark("vector", sV)
                    pe2_free[qh % 2] = hst_ok[qh * 4 + 3]
                    st["laststt"] = hst_ok[qh * 4 + 3]
                else:
                    vop(lambda e, qh=qh: e.tensor_mul(
                        out=XQB[:, 0:512].rearrange("p (h n) -> p h n", h=8),
                        in0=XSb[:, qh * 512:(qh + 1) * 512]
                        .rearrange("p (h n) -> p h n", h=8),
                        in1=QTM[:, qh * 8:(qh + 1) * 8]
                        .rearrange("p (h o) -> p h o", o=1)
                        .broadcast_to([128, 8, 64])),
                        deps=[m_tm, m_ev, m_brope])
                    m_xq = b.mark("vector", sV)
                    for i, h in enumerate(hset):
                        lb = i // 2
                        ql = h % 2
                        cc = (qh % 2) * 256 + lb * 64
                        top(lambda e, h=h, i=i, ql=ql, cc=cc: e.matmul(
                            PE2[ql * 64:ql * 64 + 64, cc:cc + 64],
                            BT_TM[:, h * 64:(h + 1) * 64],
                            XQB[:, i * 64:(i + 1) * 64],
                            start=True, stop=True),
                            deps=[m_xq, m_brope, pe2_free[qh % 2], st["laststt"]])
                    m_st = b.mark("tensor", sT)
                    for lb in range(4):
                        bp = qh * 4 + lb
                        cc = (qh % 2) * 256 + lb * 64
                        vop(lambda e, bp=bp, cc=cc, par=par: e.scalar_tensor_tensor(
                            out=HST[:, bp * 64:bp * 64 + 64],
                            in0=HST[:, bp * 64:bp * 64 + 64],
                            scalar=DECB[:, par * 16 + bp:par * 16 + bp + 1],
                            in1=PE2[:, cc:cc + 64],
                            op0=ALU.mult, op1=ALU.add),
                            deps=[m_st, m_decb])
                        hst_ok[bp] = b.mark("vector", sV)
                    pe2_free[qh % 2] = hst_ok[qh * 4 + 3]
                    st["laststt"] = hst_ok[qh * 4 + 3]

            if own:
                vop(lambda e, gci=gci: e.tensor_copy(
                    out=SQW[:, 0:2048],
                    in_=YGb.rearrange("p (m t) -> p m t", m=16)
                    [:, :, gci * 128:gci * 128 + 128]), deps=[pf["PY"], st["wo"]])
                m_c = b.mark("vector", sV)
                sop(lambda e: e.activation(out=SQW[:, 0:2048], in_=SQW[:, 0:2048],
                                           func=AF.Square, bias=0.0, scale=1.0),
                    deps=[m_c])
                m_sq2 = b.mark("scalar", sS)
                for mc in range(16):
                    if CUT & 32:
                        break
                    top(lambda e, mc=mc: e.matmul(
                        PS[0:1, 0:128], ONEB[:, :],
                        SQW[:, mc * 128:(mc + 1) * 128],
                        start=(mc == 0), stop=(mc == 15)),
                        deps=([m_sq2, pf["PS"]] if mc == 0 else ()))
                m_rmsmm = b.mark("tensor", sT)
                vop(lambda e, gci=gci: e.tensor_copy(
                    out=RMSA[0:1, gci * 128:gci * 128 + 128], in_=PS[0:1, 0:128]),
                    deps=[m_rmsmm, st["r2w"]])
                pf["PS"] = b.mark("vector", sV)
                return pf["PS"]
            return b.mark("vector", sV)

        def cast_dma(dst, src, sem=None):
            sem = sem or sC
            sem.inc(16)
            b.ops["gpsimd"].append(lambda e, d=dst, sr=src, sm=sem: e.dma_start(
                out=d, in_=sr).then_inc(sm.h, 16))

        for wh in range(2):
            cast_dma(woutb_d[wh * 1024:(wh + 1) * 1024, :],
                     wout_d[wh * 1024:(wh + 1) * 1024, :], sCW)
        sc_wout = sCW.n
        cast_jobs = []
        for ei in range(NE):
            cast_jobs.append((w13b_d[ei], w13_d[ei]))
        for ei in range(NE):
            cast_jobs.append((w2b_d[ei], w2e_d[ei]))

        XSB2 = BFA[:, 2048:4096]
        BT2 = BFA[:, 4096:6144]
        sa_out = {}

        def pre_sa(c):
            par = c % 2
            XSbP = XSb if par == 0 else XSB2
            BTP = BT_TM if par == 0 else BT2
            QTMP = QTM if par == 0 else QTM2
            if cast_jobs:
                cast_dma(*cast_jobs.pop(0))
            d_xp = b.dma("gpsimd", XPB.rearrange("p (k t) -> p k t", k=8),
                         xpre[:, c * Q:(c + 1) * Q]
                         .rearrange("(k p) t -> p k t", p=128), sG,
                         deps=[st["hpre"], m_winz])
            m_h = emit_ln(lambda kc: XPB[:, kc * 128:(kc + 1) * 128], 128,
                          lambda kc: HG[:, kc * 128:(kc + 1) * 128],
                          ca1_col, sa_col, par, SQW,
                          deps=[d_xp, st["ipro"]],
                          x_wide=XPB.rearrange("p (k n) -> p k n", k=8),
                          c1w=ADL1[:, 0:8], s0w=ADLN[:, 0:8],
                          dst_wide=HG[:, 0:1024]
                          .rearrange("p (k n) -> p k n", k=8))
            st["hpre"] = m_h
            yield
            nix_t = [(PA, 0, 512), (PB, 512, 512), (PC, 1024, 512),
                     (PD, 1536, 512), (PY, 2048, 160)]
            ip_deps = [m_h, m_winx, pf["PA"], pf["PB"], pf["PC"], pf["PD"],
                       pf["PY"]]
            for kc in range(8):
                for ti, (pt, c0, w) in enumerate(nix_t):
                    top(lambda e, kc=kc, pt=pt, c0=c0, w=w: e.matmul(
                        pt[:, 0:w],
                        HG[:, kc * 128:(kc + 1) * 128],
                        WINX[:, kc * NIX + c0:kc * NIX + c0 + w],
                        start=(kc == 0), stop=(kc == 7)),
                        deps=(ip_deps if (kc == 0 and ti == 0) else ()))
            m_ip = b.mark("tensor", sT)
            st["ipro"] = m_ip
            for i, pt in enumerate((PA, PB, PC, PD)):
                vop(lambda e, i=i, pt=pt, XSbP=XSbP: e.tensor_copy(
                    out=XSbP[:, i * 512:(i + 1) * 512], in_=pt[:, :]), deps=[m_ip])
            vop(lambda e: e.tensor_copy(out=SPARE[:, 0:128], in_=PY[:, 0:128]))
            vop(lambda e: e.tensor_copy(out=ROWREP[:, 512:544], in_=PY[:, 128:160]))
            m_ev = b.mark("vector", sV)
            pf["PA"] = pf["PB"] = pf["PC"] = pf["PD"] = m_ev
            pf["PY"] = m_ev
            top(lambda e: e.transpose(PS[0:32, 0:128], ROWREP[:, 512:544], IDF[:, :]),
                deps=[m_ev, pf["PS"]])
            m_t1 = b.mark("tensor", sT)
            vop(lambda e: e.tensor_copy(out=t32(0), in_=PS[0:32, 0:128]), deps=[m_t1])
            m_v1 = b.mark("vector", sV)
            pf["PS"] = m_v1
            sop(lambda e: e.activation(out=t32(1), in_=t32(0), func=AF.Sigmoid,
                                       bias=DTBN[:, :], scale=-1.0), deps=[m_v1])
            sop(lambda e: e.activation(out=t32(1), in_=t32(1), func=AF.Ln,
                                       bias=0.0, scale=1.0))
            sop(lambda e: e.activation(out=t32(0), in_=t32(1), func=AF.Copy,
                                       bias=0.0, scale=-1.0))
            sop(lambda e: e.activation(out=t32(2), in_=t32(0), func=AF.Ln,
                                       bias=0.0, scale=1.0))
            m_lndt = b.mark("scalar", sS)
            vop(lambda e: e.tensor_tensor_scan(out=t32(1), data0=t32(0), data1=t32(0),
                                               initial=0.0, op0=ALU.add,
                                               op1=ALU.bypass),
                deps=[m_lndt])
            vop(lambda e: e.tensor_copy(out=TCOL[:, 0:1], in_=t32(1)[:, 127:128]))
            vop(lambda e: e.tensor_scalar(out=t32(3), in0=t32(1), scalar1=BASE[:, :],
                                          scalar2=None, op0=ALU.add))
            vop(lambda e: e.tensor_add(out=BASE[:, :], in0=BASE[:, :],
                                       in1=TCOL[:, 0:1]))
            vop(lambda e: e.tensor_scalar(out=t32(1), in0=t32(1), scalar1=ACOL[:, :],
                                          scalar2=None, op0=ALU.mult))
            vop(lambda e: e.tensor_scalar(out=TCOL[:, 1:2], in0=TCOL[:, 0:1],
                                          scalar1=ACOL[:, :], scalar2=None,
                                          op0=ALU.mult))
            vop(lambda e: e.scalar_tensor_tensor(out=t32(4), in0=t32(1),
                                                 scalar=TCOL[:, 1:2], in1=t32(2),
                                                 op0=ALU.subtract, op1=ALU.add),
                deps=[m_lndt])
            m_v3 = b.mark("vector", sV)
            sop(lambda e: e.activation(out=t32(4), in_=t32(4), func=AF.Exp,
                                       bias=0.0, scale=1.0), deps=[m_v3])
            sop(lambda e: e.activation(out=TDEC[:, :], in_=TCOL[:, 1:2], func=AF.Exp,
                                       bias=0.0, scale=-1.0), deps=[m_v3, st["decw"]])
            m_q = b.mark("scalar", sS)
            d_dc = b.dma("scalar", scr_dec[par:par + 1, :].rearrange("o n -> n o"),
                         TDEC[:, :], sB[1], deps=[m_q])
            st["decw"] = d_dc
            dview = scr_dec[par:par + 1, :].rearrange("o (bp two) -> (o two) bp",
                                                      two=2)
            b.dma("scalar", DECB[0:64, par * 16:(par + 1) * 16],
                  dview[0:1, :].broadcast_to([64, 16]),
                  sB[2], deps=[d_dc, st["sbv"][par]], slow=True)
            m_decb = b.dma("scalar", DECB[64:128, par * 16:(par + 1) * 16],
                           dview[1:2, :].broadcast_to([64, 16]), sB[2], slow=True)
            yield
            top(lambda e: e.transpose(PS[0:128, 0:32], t32(3), IDF[0:32, 0:32]),
                deps=[m_v3, pf["PS"]])
            top(lambda e: e.transpose(PS[0:128, 64:96], t32(4), IDF[0:32, 0:32]),
                deps=[m_q])
            m_t2 = b.mark("tensor", sT)
            vop(lambda e: e.tensor_copy(out=STM[:, :], in_=PS[:, 0:32]),
                deps=[m_t2])
            vop(lambda e, c=c, QTMP=QTMP: e.tensor_scalar(
                out=QTMP[:, :], in0=PS[:, 64:96], scalar1=VALID[:, c:c + 1],
                scalar2=None, op0=ALU.mult))
            m_tm = b.mark("vector", sV)
            pf["PS"] = m_tm
            ANG_A5 = ROWREP[:, 0:512]
            ANG_B5 = ROWREP[:, 512:1024]
            vop(lambda e: e.tensor_mul(
                out=ANG_A5.rearrange("p (h k) -> p h k", h=32),
                in0=STM[:, 0:32].rearrange("p (h o) -> p h o", o=1)
                .broadcast_to([128, 32, 16]),
                in1=IFRQ[:, 0:512].rearrange("p (h k) -> p h k", h=32)),
                deps=[m_tm, st.get("trig")])
            m_trig = None
            for trig, dstT in ((0, SINB), (1, COSB)):
                if trig == 1:
                    vop(lambda e: e.tensor_scalar(out=ANG_A5, in0=ANG_A5,
                                                  scalar1=0.25, scalar2=None,
                                                  op0=ALU.add), deps=[m_trig])
                vop(lambda e: e.tensor_copy(out=KI[:, 0:512], in_=ANG_A5))
                vop(lambda e: e.tensor_copy(out=ANG_B5, in_=KI[:, 0:512]))
                vop(lambda e: e.tensor_sub(out=ANG_B5, in0=ANG_A5, in1=ANG_B5))
                vop(lambda e: e.tensor_scalar(out=ANG_B5, in0=ANG_B5,
                                              scalar1=TWO_PI, scalar2=None,
                                              op0=ALU.mult))
                vop(lambda e: e.tensor_single_scalar(out=MRB[:, 0:512], in_=ANG_B5,
                                                     scalar=PI_CL, op=ALU.is_gt))
                vop(lambda e: e.tensor_single_scalar(out=MRB[:, 512:1024],
                                                     in_=ANG_B5,
                                                     scalar=-PI_CL, op=ALU.is_lt))
                vop(lambda e: e.tensor_sub(out=MRB[:, 0:512], in0=MRB[:, 0:512],
                                           in1=MRB[:, 512:1024]))
                vop(lambda e: e.scalar_tensor_tensor(out=ANG_B5, in0=MRB[:, 0:512],
                                                     scalar=-TWO_PI, in1=ANG_B5,
                                                     op0=ALU.mult, op1=ALU.add))
                vop(lambda e: e.tensor_scalar(out=ANG_B5, in0=ANG_B5,
                                              scalar1=PI_CL, scalar2=-PI_CL,
                                              op0=ALU.min, op1=ALU.max))
                m_red = b.mark("vector", sV)
                sop(lambda e, dstT=dstT, sc=(-1.0 if trig == 0 else 1.0):
                    e.activation(out=dstT[:, 0:512], in_=ANG_B5,
                                 func=AF.Sin, bias=0.0, scale=sc), deps=[m_red])
                m_trig = b.mark("scalar", sS)
            st["trig"] = m_trig
            yield
            Bt = SPARE[:, 0:64]
            bview = BTP.rearrange("p (h n) -> p h n", h=32)
            sview = SINB.rearrange("p (h k) -> p h k", h=32)
            cosv = COSB.rearrange("p (h k) -> p h k", h=32)
            wview = WTs.rearrange("p (h k) -> p h k", h=32)

            def b_bc(c0, c1):
                return (Bt[:, c0:c1].rearrange("p (o n) -> p o n", o=1)
                        .broadcast_to([128, 32, c1 - c0]))

            vop(lambda e: e.tensor_mul(out=wview, in0=cosv, in1=b_bc(0, 16)),
                deps=[m_trig, m_ev, st["sbm"][par]])
            vop(lambda e: e.tensor_mul(out=bview[:, :, 0:16], in0=sview,
                                       in1=b_bc(16, 32)))
            vop(lambda e: e.tensor_sub(out=bview[:, :, 0:16], in0=wview,
                                       in1=bview[:, :, 0:16]))
            vop(lambda e: e.tensor_mul(out=wview, in0=sview, in1=b_bc(0, 16)))
            vop(lambda e: e.tensor_mul(out=bview[:, :, 16:32], in0=cosv,
                                       in1=b_bc(16, 32)))
            vop(lambda e: e.tensor_add(out=bview[:, :, 16:32], in0=wview,
                                       in1=bview[:, :, 16:32]))
            vop(lambda e: e.tensor_copy(out=bview[:, :, 32:64], in_=b_bc(32, 64)))
            m_brope = b.mark("vector", sV)
            sa_out[c] = (m_brope, m_decb)
            yield

        def pre_sb(c, gen):
            par = c % 2
            XSbP = XSb if par == 0 else XSB2
            BTP = BT_TM if par == 0 else BT2
            QTMP = QTM if par == 0 else QTM2
            m_brope, m_decb = sa_out.pop(c)
            m_st = None
            for qh in range(4):
                vop(lambda e, qh=qh, XSbP=XSbP, QTMP=QTMP: e.tensor_mul(
                    out=XQB[:, 0:512].rearrange("p (h n) -> p h n", h=8),
                    in0=XSbP[:, qh * 512:(qh + 1) * 512]
                    .rearrange("p (h n) -> p h n", h=8),
                    in1=QTMP[:, qh * 8:(qh + 1) * 8]
                    .rearrange("p (h o) -> p h o", o=1)
                    .broadcast_to([128, 8, 64])),
                    deps=[m_brope])
                m_xq = b.mark("vector", sV)
                for i in range(8):
                    h = qh * 8 + i
                    lb = i // 2
                    ql = h % 2
                    cc = (qh % 2) * 256 + lb * 64
                    top(lambda e, h=h, i=i, ql=ql, cc=cc, BTP=BTP: e.matmul(
                        PE2[ql * 64:ql * 64 + 64, cc:cc + 64],
                        BTP[:, h * 64:(h + 1) * 64],
                        XQB[:, i * 64:(i + 1) * 64],
                        start=True, stop=True),
                        deps=[m_xq, m_brope, pe2_free[qh % 2]])
                m_st = b.mark("tensor", sT)
                if gen is not None:
                    next(gen, None)
                for lb in range(4):
                    bp = qh * 4 + lb
                    cc = (qh % 2) * 256 + lb * 64
                    vop(lambda e, bp=bp, cc=cc, par=par: e.scalar_tensor_tensor(
                        out=HST[:, bp * 64:bp * 64 + 64],
                        in0=HST[:, bp * 64:bp * 64 + 64],
                        scalar=DECB[:, par * 16 + bp:par * 16 + bp + 1],
                        in1=PE2[:, cc:cc + 64],
                        op0=ALU.mult, op1=ALU.add),
                        deps=[m_st, m_decb])
                    hst_ok[bp] = b.mark("vector", sV)
                pe2_free[qh % 2] = hst_ok[qh * 4 + 3]
                st["laststt"] = hst_ok[qh * 4 + 3]
            st["sbm"][par] = m_st
            st["sbv"][par] = hst_ok[15]

        if PH >= 1:
            gen = pre_sa(0)
            for _ in range(6):
                next(gen, None)
            for c in range(NPC):
                gen = pre_sa(c + 1) if c + 1 < NPC else None
                pre_sb(c, gen)
                if gen is not None:
                    for _ in range(6):
                        next(gen, None)

        for g in range(NGM if PH >= 3 else (1 if PH >= 2 else 0)):
            t0 = g * GM
            m_h = emit_ln(lambda kc, t0=t0: X[:, kc * LSEG + t0:kc * LSEG + t0 + GM], GM,
                          lambda kc: HG[:, kc * GM:(kc + 1) * GM],
                          ca1_col, sa_col, g % 2, SQW,
                          deps=[d_x, st["tap_h"], st["wo"], st["ipro"]],
                          x_wide=X.rearrange("p (k t) -> p k t", k=8)[:, :, t0:t0 + GM],
                          c1w=ADL1[:, 0:8], s0w=ADLN[:, 0:8],
                          dst_wide=HG[:, 0:2048]
                          .rearrange("p (k n) -> p k n", k=8))
            if debug:
                for kc in range(8):
                    st["tap_h"] = b.dma(
                        "sync", taps["h_dbg"][kc * 128:(kc + 1) * 128, t0:t0 + GM],
                        HG[:, kc * GM:(kc + 1) * GM], sO, deps=[m_h])
            for mz in range(16):
                pz = PE2 if mz % 2 == 0 else PY
                pzf = pe2_free[0] if mz % 2 == 0 else pf["PY"]
                for kc in range(8):
                    if CUT & 64:
                        break
                    top(lambda e, mz=mz, kc=kc, pz=pz: e.matmul(
                        pz[:, 0:GM],
                        WINZ[:, kc * DI + mz * 128:kc * DI + (mz + 1) * 128],
                        HG[:, kc * GM:(kc + 1) * GM],
                        start=(kc == 0), stop=(kc == 7)),
                        deps=[m_h, m_winz, pzf, pe2_free[1] if mz % 2 == 0 else None])
                m_z = b.mark("tensor", sT)
                sop(lambda e, mz=mz, pz=pz: e.activation(
                    out=SILUZ[:, mz * GM:(mz + 1) * GM], in_=pz[:, 0:GM],
                    func=AF.Silu, bias=0.0, scale=1.0), deps=[m_z])
                m_sz = b.mark("scalar", sS)
                if mz % 2 == 0:
                    pe2_free[0] = m_sz
                    pe2_free[1] = m_sz
                else:
                    pf["PY"] = m_sz
            st["siluz"] = b.mark("scalar", sS)
            m_rms = None
            for gci in range(2):
                cglob = NPC + g * 2 + gci
                m_rms = emit_chunk(
                    lambda kc, gci=gci: HG[:, kc * GM + gci * 128:kc * GM + gci * 128 + 128],
                    m_h, cglob, own=True, gci=gci)
            sop(lambda e: e.activation(out=R3[:, 0:GM], in_=RMSA[0:1, 0:GM],
                                       func=AF.Ln, bias=EPS[0:1, :], scale=1.0 / DI),
                deps=[m_rms, st["r2w"]])
            sop(lambda e: e.activation(out=R3[:, 0:GM], in_=R3[:, 0:GM],
                                       func=AF.Exp, bias=0.0, scale=-0.5))
            m_r2 = b.mark("scalar", sS)
            m_wr2 = b.dma("sync", scr_r2[g % 2:g % 2 + 1, 0:GM], R3[:, 0:GM], sQ,
                          deps=[m_r2])
            st["r2w"] = m_wr2
            m_r2b = b.dma("sync", R2B[:, 0:GM],
                          scr_r2[g % 2:g % 2 + 1, 0:GM].broadcast_to([128, GM]), sQ,
                          deps=[m_wr2])
            if debug:
                for mz in range(16):
                    st["tap_yg"] = b.dma(
                        "sync", taps["yg_dbg"][mz * 128:(mz + 1) * 128, t0:t0 + GM],
                        YGb.rearrange("p (m t) -> p m t", m=16)[:, mz, 0:GM],
                        sO, deps=[m_rms])
            wo_marks = []
            for kc in range(16):
                buf = SQW[:, (kc % 2) * 1024:(kc % 2) * 1024 + 1024]
                d_wo = b.dma("gpsimd", buf, woutb_d[kc * 128:(kc + 1) * 128, :], sG,
                             deps=[wo_marks[kc - 2] if kc >= 2 else m_rms,
                                   (sCW, sc_wout),
                                   pf["PA"], pf["PB"], pf["PC"], pf["PD"]])
                vop(lambda e, buf=buf, kc=kc: e.tensor_scalar(
                    out=buf, in0=buf, scalar1=NRMW[:, kc:kc + 1], scalar2=None,
                    op0=ALU.mult), deps=[d_wo, m_cons])
                m_nw = b.mark("vector", sV)
                for m8 in range(8):
                    if CUT & 128:
                        break
                    pt = (PA, PB, PC, PD)[m8 % 4]
                    cc = (m8 // 4) * 256
                    stt_ = (kc == 0 and m8 < 4)
                    stp_ = (kc == 15 and m8 >= 4)
                    top(lambda e, buf=buf, m8=m8, pt=pt, cc=cc, kc=kc, stt_=stt_,
                        stp_=stp_: e.matmul(
                        pt[:, cc:cc + GM],
                        buf[:, m8 * 128:(m8 + 1) * 128],
                        YGb[:, kc * GM:(kc + 1) * GM],
                        start=stt_, stop=stp_),
                        deps=[m_nw, pf["PA"], pf["PB"], pf["PC"], pf["PD"]])
                wo_marks.append(b.mark("tensor", sT))
            m_wo = wo_marks[-1]
            st["wo"] = m_wo
            for m8 in range(8):
                pt = (PA, PB, PC, PD)[m8 % 4]
                cc = (m8 // 4) * 256
                vop(lambda e, pt=pt, cc=cc, m8=m8: e.scalar_tensor_tensor(
                    out=FFT[:, 0:256], in0=pt[:, cc:cc + GM],
                    scalar=ga_col(m8), in1=R2B[:, 0:GM],
                    op0=ALU.mult, op1=ALU.mult), deps=[m_wo, m_r2b, m_adln])
                vop(lambda e, m8=m8, t0=t0: e.tensor_add(
                    out=X[:, m8 * LSEG + t0:m8 * LSEG + t0 + GM],
                    in0=X[:, m8 * LSEG + t0:m8 * LSEG + t0 + GM],
                    in1=FFT[:, 0:256]))
            m_xu = b.mark("vector", sV)
            pf["PA"] = pf["PB"] = pf["PC"] = pf["PD"] = m_xu

        m_mixer = barrier()
        if debug:
            for kc in range(8):
                b.dma("sync", taps["xmid_dbg"][kc * 128:(kc + 1) * 128, :],
                      X[:, kc * LSEG:(kc + 1) * LSEG], sO, deps=[m_mixer[1]])
            m_mixer = barrier()

        NGF_RUN = NGF if PH >= 5 else (1 if PH >= 4 else 0)
        d_wg = b.dma("gpsimd", WGB.rearrange("p (k g) -> p k g", k=8),
                     wg_d.rearrange("(k p) g -> p k g", p=128), sG,
                     deps=[m_mixer[0]])
        for g in range(NGF_RUN):
            t0 = g * GF
            m_h2 = emit_ln(lambda kc, t0=t0: X[:, kc * LSEG + t0:kc * LSEG + t0 + GF], GF,
                           lambda kc: H2G[:, kc * GF:(kc + 1) * GF],
                           cb1_col, sb_col, g % 2, XBF, deps=[m_mixer[1]],
                           x_wide=X.rearrange("p (k t) -> p k t", k=8)
                           [:, :, t0:t0 + GF])
            for kc in range(8):
                top(lambda e, kc=kc: e.matmul(
                    PS[0:16, 0:512], WGB[:, kc * 16:(kc + 1) * 16],
                    H2G[:, kc * GF:(kc + 1) * GF],
                    start=(kc == 0), stop=(kc == 7)),
                    deps=[d_wg, m_h2, pf["PS"]])
            m_gs = b.mark("tensor", sT)
            sop(lambda e: e.activation(out=GTF[0:16, 0:512], in_=PS[0:16, 0:512],
                                       func=AF.Sigmoid, bias=0.0, scale=1.0),
                deps=[m_gs, st["cmbw"]])
            m_sig = b.mark("scalar", sS)
            pf["PS"] = m_sig
            for sc in range(4):
                top(lambda e, sc=sc: e.transpose(PS[0:128, sc * 16:(sc + 1) * 16],
                                                 GTF[0:16, sc * 128:(sc + 1) * 128],
                                                 IDF[0:16, 0:16]),
                    deps=[m_sig, pf["PS"]])
            m_sct = b.mark("tensor", sT)
            vop(lambda e: e.tensor_copy(out=GTF[:, 512:576], in_=PS[:, 0:64]),
                deps=[m_sct])
            m_sctm = b.mark("vector", sV)
            pf["PS"] = m_sctm
            for sc in range(4):
                S0 = GTF[:, 512 + sc * 16:512 + (sc + 1) * 16]
                W1 = GTF[:, 640:656]
                W2 = GTF[:, 656:672]
                G2 = GTF[:, 672:676]
                M2c = GTF[:, 676:677]
                T1c = GTF[:, 677:678]
                CMB = GTF[:, 576 + sc * 16:576 + (sc + 1) * 16]
                AM = GTF[:, 678:682]
                pairs = [(i, j) for i in range(4) for j in range(i + 1, 4)]
                for pi, (i, j) in enumerate(pairs):
                    vop(lambda e, i=i, j=j, pi=pi: e.tensor_add(
                        out=GTF[:, 682 + pi * 4:686 + pi * 4],
                        in0=S0.rearrange("p (g j) -> p g j", g=4)[:, :, i],
                        in1=S0.rearrange("p (g j) -> p g j", g=4)[:, :, j]),
                        deps=[m_sctm])
                vop(lambda e: e.tensor_copy(out=G2, in_=GTF[:, 682:686]))
                for pi in range(1, 6):
                    vop(lambda e, pi=pi: e.tensor_tensor(
                        out=G2, in0=G2, in1=GTF[:, 682 + pi * 4:686 + pi * 4],
                        op=ALU.max))
                gp = [(i, j) for i in range(4) for j in range(i + 1, 4)]
                for pi, (i, j) in enumerate(gp):
                    vop(lambda e, i=i, j=j, pi=pi: e.tensor_tensor(
                        out=GTF[:, 682 + pi:683 + pi], in0=G2[:, i:i + 1],
                        in1=G2[:, j:j + 1], op=ALU.min))
                vop(lambda e: e.tensor_copy(out=M2c, in_=GTF[:, 682:683]))
                for pi in range(1, 6):
                    vop(lambda e, pi=pi: e.tensor_tensor(
                        out=M2c, in0=M2c, in1=GTF[:, 682 + pi:683 + pi], op=ALU.max))
                vop(lambda e: e.tensor_scalar(out=AM, in0=G2, scalar1=M2c,
                                              scalar2=None, op0=ALU.is_ge))
                amx = AM.rearrange("p (g o) -> p g o", o=1).broadcast_to([128, 4, 4])
                vop(lambda e: e.tensor_mul(out=W1.rearrange("p (g j) -> p g j", g=4),
                                           in0=S0.rearrange("p (g j) -> p g j", g=4),
                                           in1=amx))
                vop(lambda e: e.tensor_scalar(
                    out=W2.rearrange("p (g j) -> p g j", g=4), in0=amx,
                    scalar1=1e30, scalar2=-1e30, op0=ALU.mult, op1=ALU.add))
                vop(lambda e: e.tensor_add(out=W1, in0=W1, in1=W2))
                for it in range(3):
                    vop(lambda e: e.reduce_max(out=T1c, in_=W1, axis=mybir.AxisListType.X))
                    vop(lambda e: e.tensor_scalar(out=W2, in0=W1, scalar1=T1c,
                                                  scalar2=None, op0=ALU.is_ge))
                    vop(lambda e: e.tensor_scalar(out=W2, in0=W2, scalar1=-1e30,
                                                  scalar2=None, op0=ALU.mult))
                    vop(lambda e: e.tensor_add(out=W1, in0=W1, in1=W2))
                vop(lambda e: e.reduce_max(out=T1c, in_=W1, axis=mybir.AxisListType.X))
                vop(lambda e: e.tensor_mul(out=W1.rearrange("p (g j) -> p g j", g=4),
                                           in0=S0.rearrange("p (g j) -> p g j", g=4),
                                           in1=amx))
                vop(lambda e: e.tensor_scalar(
                    out=W2.rearrange("p (g j) -> p g j", g=4), in0=amx,
                    scalar1=1e30, scalar2=-1e30, op0=ALU.mult, op1=ALU.add))
                vop(lambda e: e.tensor_add(out=W1, in0=W1, in1=W2))
                vop(lambda e: e.tensor_scalar(out=W2, in0=W1, scalar1=T1c,
                                              scalar2=None, op0=ALU.is_ge))
                vop(lambda e: e.tensor_mul(out=W1, in0=S0, in1=W2))
                vop(lambda e: e.reduce_sum(out=T1c, in_=W1, axis=mybir.AxisListType.X))
                vop(lambda e: e.tensor_scalar(out=T1c, in0=T1c, scalar1=1e-6,
                                              scalar2=None, op0=ALU.add))
                vop(lambda e: e.reciprocal(out=T1c, in_=T1c))
                vop(lambda e: e.tensor_scalar(out=CMB, in0=W1, scalar1=T1c,
                                              scalar2=None, op0=ALU.mult))
            m_cmb = b.mark("vector", sV)
            for sc in range(4):
                top(lambda e, sc=sc: e.transpose(PS[0:16, sc * 128:(sc + 1) * 128],
                                                 GTF[:, 576 + sc * 16:576 + (sc + 1) * 16],
                                                 IDF[0:128, 0:128]),
                    deps=[m_cmb, pf["PS"]])
            m_ct = b.mark("tensor", sT)
            vop(lambda e: e.tensor_copy(out=GTF[0:16, 0:512], in_=PS[0:16, 0:512]),
                deps=[m_ct])
            m_cfm = b.mark("vector", sV)
            pf["PS"] = m_cfm
            if debug and g == 0:
                b.dma("sync", taps["cmb2_dbg"][:, :], GTF[0:16, 0:512], sO,
                      deps=[m_cfm])
                for kkc in range(8):
                    b.dma("sync", taps["h2_dbg"][:, kkc * 512:(kkc + 1) * 512],
                          H2G[:, kkc * 512:(kkc + 1) * 512], sO, deps=[m_h2])
                for _e in ("vector", "scalar", "tensor"):
                    b.wait(_e, (sO, sO.n))
            m_cw = b.dma("sync", scr_cmb[0:1, :].rearrange("o (e t) -> (o e) t", e=16),
                         GTF[0:16, 0:512], sQ, deps=[m_cfm])
            st["cmbw"] = m_cw
            exp_marks = []
            out67_mark = st.get("ffn_free")
            for ei in range(NE):
                buf = EWB[ei % 2]
                dd = [exp_marks[ei - 2] if ei >= 2 else
                      (m_mixer[0] if g == 0 else st["exp_prev"][ei % 2])]
                d13 = b.dma("gpsimd", buf[:, 0:8192].rearrange("p (k f) -> p k f", k=8),
                            w13b_d[ei].rearrange("(k p) f -> p k f", p=128), sG,
                            deps=dd + [(sC, sC.n)])
                d2 = b.dma("gpsimd",
                           buf[:, 8192:12288].rearrange("p (k m) -> p k m", k=4),
                           w2b_d[ei].rearrange("(k p) m -> p k m", p=128), sG)
                m_cbc = None
                if ei < 16:
                    d_cb = b.dma("sync", CBTF[:, 0:512],
                                 scr_cmb[0:1, ei * 512:(ei + 1) * 512]
                                 .broadcast_to([128, 512]), sQ,
                                 deps=[m_cw, st["cbc"]])
                    vop(lambda e, ei=ei: e.tensor_copy(
                        out=CBTB[:, (ei % 2) * 512:(ei % 2) * 512 + 512],
                        in_=CBTF[:, 0:512]), deps=[d_cb])
                    m_cbc = b.mark("vector", sV)
                    st["cbc"] = m_cbc
                m_vfc = [None] * 4
                mul_marks = [None] * 4
                silu_marks = [None] * 4
                w2b = ("PA", "PB", "PC", "PD", "PE2")

                def emit_p1(fc):
                    for kc in range(8):
                        top(lambda e, fc=fc, kc=kc, buf=buf: e.matmul(
                            PS[:, 0:512],
                            buf[:, kc * 1024 + fc * 128:
                                kc * 1024 + fc * 128 + 128],
                            H2G[:, kc * GF:(kc + 1) * GF],
                            start=(kc == 0), stop=(kc == 7)),
                            deps=([d13, m_h2, pf["PS"]] if fc == 0 else
                                  ([silu_marks[fc - 1]] if kc == 0 else ())))
                    m_p1u = b.mark("tensor", sT)
                    sop(lambda e, fc=fc: e.activation(
                        out=M1T[:, fc * 512:(fc + 1) * 512], in_=PS[:, 0:512],
                        func=AF.Silu, bias=0.0, scale=1.0), deps=[m_p1u])
                    silu_marks[fc] = b.mark("scalar", sS)

                def emit_p3(fc):
                    for kc in range(8):
                        top(lambda e, fc=fc, kc=kc, buf=buf: e.matmul(
                            PY[:, 0:512],
                            buf[:, kc * 1024 + 512 + fc * 128:
                                kc * 1024 + 512 + fc * 128 + 128],
                            H2G[:, kc * GF:(kc + 1) * GF],
                            start=(kc == 0), stop=(kc == 7)),
                            deps=(([silu_marks[fc], pf["PY"]] if fc == 0 else
                                   [silu_marks[fc], mul_marks[fc - 1]])
                                  if kc == 0 else ()))
                    m_p3u = b.mark("tensor", sT)
                    vop(lambda e, fc=fc: e.tensor_mul(
                        out=TVT[:, fc * 512:(fc + 1) * 512], in0=PY[:, 0:512],
                        in1=M1T[:, fc * 512:(fc + 1) * 512]),
                        deps=[m_p3u, silu_marks[fc]])
                    if ei < 16:
                        vop(lambda e, fc=fc, ei=ei: e.tensor_mul(
                            out=TVT[:, fc * 512:(fc + 1) * 512],
                            in0=TVT[:, fc * 512:(fc + 1) * 512],
                            in1=CBTB[:, (ei % 2) * 512:(ei % 2) * 512 + 512]),
                            deps=[m_cbc])
                    mul_marks[fc] = b.mark("vector", sV)
                    m_vfc[fc] = mul_marks[fc]

                def emit_w2(fc):
                    for m8 in range(5):
                        tgt = (PA, PB, PC, PD, PE2)[m8]
                        stt_ = (ei == 0 and fc == 0)
                        stp_ = (ei == NE - 1 and fc == 3)
                        top(lambda e, m8=m8, fc=fc, buf=buf, tgt=tgt, stt_=stt_,
                            stp_=stp_: e.matmul(
                            tgt[:, 0:512],
                            buf[:, 8192 + fc * 1024 + m8 * 128:
                                8192 + fc * 1024 + m8 * 128 + 128],
                            TVT[:, fc * 512:(fc + 1) * 512],
                            start=stt_, stop=stp_),
                            deps=[d2, m_vfc[fc],
                                  (pf.get(w2b[m8])
                                   if (ei == 0 and fc == 0 and m8 != 4) else None),
                                  (pe2_free[0] if (m8 == 4 and ei == 0 and fc == 0)
                                   else None),
                                  (pe2_free[1] if (m8 == 4 and ei == 0 and fc == 0)
                                   else None)])

                emit_p1(0)
                emit_p3(0)
                emit_p1(1)
                emit_w2(0)
                emit_p3(1)
                emit_p1(2)
                emit_w2(1)
                emit_p3(2)
                emit_p1(3)
                emit_w2(2)
                emit_p3(3)
                emit_w2(3)
                if debug and g == 0 and ei == 0:
                    b.dma("sync", taps["v_dbg"][:, :], TVT[:, :], sO, deps=[m_vfc[3]])
                    b.dma("sync", taps["m1_dbg"][:, :], M1T[:, :], sO, deps=[m_vfc[3]])
                    for _e in ("vector", "scalar", "tensor"):
                        b.wait(_e, (sO, sO.n))
                evac5 = None
                for (m8, tgt, acc, a0) in ((5, PS, OUT3, 0), (6, PY, OUT67, 0),
                                           (7, PS, OUT67, 512)):
                    pdep = (silu_marks[3] if m8 == 5 else
                            (mul_marks[3] if m8 == 6 else evac5))
                    for fc in range(4):
                        top(lambda e, m8=m8, fc=fc, buf=buf, tgt=tgt: e.matmul(
                            tgt[:, 0:512],
                            buf[:, 8192 + fc * 1024 + m8 * 128:
                                8192 + fc * 1024 + m8 * 128 + 128],
                            TVT[:, fc * 512:(fc + 1) * 512],
                            start=(fc == 0), stop=(fc == 3)),
                            deps=([d2, m_vfc[fc]] + ([pdep] if fc == 0 else [])))
                    m_w2 = b.mark("tensor", sT)
                    if ei == 0:
                        vop(lambda e, acc=acc, a0=a0, tgt=tgt: e.tensor_copy(
                            out=acc[:, a0:a0 + 512], in_=tgt[:, 0:512]),
                            deps=[m_w2, out67_mark])
                    else:
                        vop(lambda e, acc=acc, a0=a0, tgt=tgt: e.tensor_add(
                            out=acc[:, a0:a0 + 512], in0=acc[:, a0:a0 + 512],
                            in1=tgt[:, 0:512]), deps=[m_w2])
                    mm = b.mark("vector", sV)
                    if m8 == 5:
                        evac5 = mm
                    elif m8 == 6:
                        pf["PY"] = mm
                    else:
                        pf["PS"] = mm
                exp_marks.append(b.mark("tensor", sT))
            st["exp_prev"] = [exp_marks[16], exp_marks[17]]
            m_exp = exp_marks[-1]
            m_o67 = b.mark("vector", sV)
            for m8 in range(8):
                if m8 < 5:
                    src = (PA, PB, PC, PD, PE2)[m8]
                    vop(lambda e, src=src, m8=m8: e.tensor_scalar(
                        out=FFT[:, 0:512], in0=src[:, 0:512], scalar1=gb_col(m8),
                        scalar2=None, op0=ALU.mult), deps=[m_exp, m_adln])
                else:
                    srcb = OUT3 if m8 == 5 else OUT67
                    a0 = 0 if m8 <= 6 else 512
                    vop(lambda e, srcb=srcb, a0=a0, m8=m8: e.tensor_scalar(
                        out=FFT[:, 0:512],
                        in0=srcb[:, a0:a0 + 512],
                        scalar1=gb_col(m8), scalar2=None, op0=ALU.mult),
                        deps=[m_o67, m_adln])
                vop(lambda e, m8=m8, t0=t0: e.tensor_add(
                    out=X[:, m8 * LSEG + t0:m8 * LSEG + t0 + GF],
                    in0=X[:, m8 * LSEG + t0:m8 * LSEG + t0 + GF],
                    in1=FFT[:, 0:512]))
            ffn_out_free = b.mark("vector", sV)
            st["ffn_free"] = ffn_out_free
            pf["PA"] = pf["PB"] = pf["PC"] = pf["PD"] = pf["PY"] = ffn_out_free
            pe2_free[0] = pe2_free[1] = ffn_out_free

        m_done = barrier()
        for kc in range(8):
            b.dma("sync", out_T[kc * 128:(kc + 1) * 128, :],
                  X[:, kc * LSEG:(kc + 1) * LSEG], sO, deps=[m_done[1]])
        b.emit("sync", lambda e, v=sO.n: e.wait_ge(sO.h, v))

        with nc.Block() as blk:
            blk.sync(lambda e: [f(e) for f in b.ops["sync"]])
            blk.tensor(lambda e: [f(e) for f in b.ops["tensor"]])
            blk.vector(lambda e: [f(e) for f in b.ops["vector"]])
            blk.scalar(lambda e: [f(e) for f in b.ops["scalar"]])
            blk.gpsimd(lambda e: [f(e) for f in b.ops["gpsimd"]])
    return nc


_PROG_CACHE = {}


def _get_prog(debug=False):
    if debug not in _PROG_CACHE:
        _PROG_CACHE[debug] = build_core_program(debug)
    return _PROG_CACHE[debug]


def _stage_inputs(inp):
    f = np.float32
    x = np.asarray(inp["L0_attn_x"], f)
    win = np.asarray(inp["L0_attn_W_in"], f)
    wout = np.ascontiguousarray(np.asarray(inp["L0_attn_W_out"], f))
    dtbias = np.asarray(inp["L0_attn_dt_bias"], f)
    alogv = np.asarray(inp["L0_attn_A_log"], f)
    dskip = np.asarray(inp["L0_attn_D_skip"], f)
    nrm = np.asarray(inp["L0_attn_norm_weight"], f)
    wgate = np.asarray(inp["L0_ffn_w_gate"], f)
    w1e = np.asarray(inp["L0_ffn_w1_e"], f)
    w3e = np.asarray(inp["L0_ffn_w3_e"], f)
    w2e = np.asarray(inp["L0_ffn_w2_e"], f)
    w1s = np.asarray(inp["L0_ffn_w1_s"], f)
    w3s = np.asarray(inp["L0_ffn_w3_s"], f)
    w2s = np.asarray(inp["L0_ffn_w2_s"], f)
    cv = np.asarray(inp["L0_c"], f)
    wmod = np.ascontiguousarray(np.asarray(inp["L0_wmod"], f))

    perm = np.concatenate([np.arange(0, 32, 2), np.arange(1, 32, 2),
                           np.arange(32, 64)])
    win_x = np.ascontiguousarray(np.concatenate(
        [win[:, 2048:4096], win[:, 4096:4160][:, perm],
         win[:, 4160:4224][:, perm], win[:, 4224:4256]], axis=1))
    win_z = np.ascontiguousarray(win[:, 0:2048])

    w13 = np.empty((NE, DM, 1024), f)
    w2a = np.empty((NE, 512, DM), f)
    for e in range(16):
        w13[e, :, 0:512] = w1e[e].T
        w13[e, :, 512:1024] = w3e[e].T
        w2a[e] = w2e[e].T
    for half in range(2):
        sl = slice(half * 512, (half + 1) * 512)
        w13[16 + half, :, 0:512] = w1s[sl, :].T
        w13[16 + half, :, 512:1024] = w3s[sl, :].T
        w2a[16 + half] = w2s[:, sl].T

    invfreq = ((1.0 / 10000.0 ** (np.arange(16, dtype=np.float64) / 16.0))
               / TWO_PI).astype(f)
    ifr = np.broadcast_to(np.tile(invfreq, 32)[None, :], (128, 512)).astype(f).copy()
    caus = np.where(np.arange(128)[None, :] >= np.arange(128)[:, None],
                    f(4.0), f(-1e30)).astype(f)
    common = dict(
        win_x=win_x, win_z=win_z, wout=wout,
        dtb=np.ascontiguousarray(dtbias[:, None]),
        alog=np.ascontiguousarray(alogv[:, None]),
        dcols=np.broadcast_to(dskip[None, :], (128, 32)).copy(),
        normw=np.ascontiguousarray(nrm.reshape(16, 128).T),
        wg=np.ascontiguousarray(wgate.T), w13=w13, w2e=w2a,
        wmod=wmod,
        identb=np.eye(128, dtype=BF), identf=np.eye(128, dtype=f),
        onesb=np.ones((128, 1), BF), ifreqt=ifr, causn=caus,
    )
    in_maps = []
    for core in range(8):
        bi = core // 4
        seg = core % 4
        s0 = seg * LSEG
        xs_T = np.ascontiguousarray(x[bi, s0:s0 + LSEG, :].T)
        npad = LPRE - s0
        xpre_T = np.zeros((DM, LPRE), f)
        if s0 > 0:
            xpre_T[:, npad:] = x[bi, 0:s0, :].T
        val = np.zeros((128, NPC), f)
        val[:, npad // Q:] = 1.0
        m = dict(common)
        m.update(xseg=xs_T, xpre=xpre_T, validd=val,
                 cvec=np.ascontiguousarray(cv[bi].reshape(8, 128).T))
        in_maps.append(m)
    return in_maps


def kernel(**inputs):
    import os
    debug = bool(os.environ.get("MK_DBG"))
    nc = _get_prog(debug)
    in_maps = _stage_inputs(inputs)
    res = run_bass_kernel_spmd(nc, in_maps, core_ids=list(range(8)))
    out = np.empty((2, 8192, DM), np.float32)
    for core in range(8):
        bi, seg = core // 4, core % 4
        out[bi, seg * LSEG:(seg + 1) * LSEG, :] = res.results[core]["out_T"].T
    if debug:
        kernel._dbg = res.results
    return out
